# version 01
# baseline (speedup 1.0000x reference)
import numpy as np
from contextlib import ExitStack
import concourse.bass as bass
import concourse.mybir as mybir
from concourse.bass_utils import run_bass_kernel_spmd

F32 = mybir.dt.float32
I32 = mybir.dt.int32
BF16 = mybir.dt.bfloat16
Alu = mybir.AluOpType
Act = mybir.ActivationFunctionType
AX = mybir.AxisListType

NCORES = 8
B = 2
T = 2048
NT = B * T
D = 1024
H = 16
HD = 64
DCH = D // 128
I0 = 3072
I1 = 2816
QMAX = 32767.0
EPS = 1e-5

DBG_SHAPE = [128, 8 * 4096]


class K:
    def __init__(self, nc, es):
        self.nc = nc
        self.es = es
        self.sems = {}
        self.cnt = {}
        for n in ["dS", "dA", "dG", "vS", "tS", "sS", "gS", "cS"]:
            self.sems[n] = es.enter_context(nc.semaphore(n))
            self.cnt[n] = 0

    def inc(self, inst, name, n=1):
        inst.then_inc(self.sems[name], n)
        self.cnt[name] += n
        return self.cnt[name]

    def wait(self, eng, name, val=None):
        eng.wait_ge(self.sems[name], self.cnt[name] if val is None else val)

    def tok(self, eng, name, n=1):
        self.cnt[name] += n
        eng.drain().then_inc(self.sems[name], n)
        return self.cnt[name]

    def barrier(self, engines):
        for e in engines:
            for n in ["dS", "dA", "dG", "vS", "tS", "sS", "gS", "cS"]:
                if self.cnt[n]:
                    e.wait_ge(self.sems[n], self.cnt[n])


def doubling(k, q, dst, width, dsem, base=0, nrows=128):
    p = 1
    while p < nrows:
        k.inc(q.dma_start(out=dst[base + p:base + 2 * p, 0:width],
                          in_=dst[base:base + p, 0:width]), dsem, 16)
        k.wait(q, dsem)
        p *= 2


def build_program(dbg=None, stop_after=None):
    nc = bass.Bass("TRN2", target_bir_lowering=False, debug=False)
    es = ExitStack()
    k = K(nc, es)
    sync, vec, sca, ten, gp = nc.sync, nc.vector, nc.scalar, nc.tensor, nc.gpsimd
    ALL_ENG = [sync, vec, sca, ten, gp]

    x_in = nc.dram_tensor("x", [NT, D], F32, kind="ExternalInput")
    y_out = nc.dram_tensor("y", [NT, D], F32, kind="ExternalOutput")
    w = {}
    for name, shape in [
        ("l0_wq", [D, D]), ("l0_wk", [D, D]), ("l0_wv", [D, D]), ("l0_wo", [D, D]),
        ("l0_wup", [I0, D]), ("l0_wgate", [I0, D]), ("l0_wdown", [D, I0]),
        ("l1_wqkv", [D, 3 * D]), ("l1_bqkv", [1, 3 * D]),
        ("l1_wout", [D, D]), ("l1_bout", [1, D]),
        ("l1_w1r", [1, 32]), ("l1_b1r", [1, 32]), ("l1_w2r", [32, 16]), ("l1_b2r", [1, 16]),
        ("l1_wgate", [I1, D]), ("l1_wup", [I1, D]), ("l1_wdown", [D, I1]),
        ("rope_c2", [T, HD]), ("rope_s2", [T, HD]),
        ("relrep", [32, 4224]),
    ]:
        w[name] = nc.dram_tensor(name, shape, F32, kind="ExternalInput")

    dr = {}
    for name, shape, dt in [
        ("spill_up", [8 * 24 * 128, 512], BF16),
        ("spill_gate", [8 * 24 * 128, 512], BF16),
        ("spill_gate2", [8 * 24 * 128, 512], BF16),
        ("spill_o", [8 * 24 * 128, 512], BF16),
        ("attn_sp", [DCH * 128, NT], F32),
        ("x1_park", [DCH * 128, NT], F32),
        ("xnt_park", [NT, D], BF16),
        ("qkt_park", [2 * T, D], BF16),
        ("wb_scr", [4 * D, D], BF16),
        ("wup_scr", [I0, D], BF16),
        ("wgate_scr", [I0, D], BF16),
        ("wdown_scr", [D, I0], BF16),
        ("x1tok_park", [NT, D], F32),
        ("xfm_park", [DCH * 128, NT], F32),
        ("exptab", [16, 4224], BF16),
        ("xn1f_park", [NT, D], F32),
        ("qk1_park", [2 * DCH * 128, T], BF16),
        ("wqkv_scr", [D, 3 * D], BF16),
        ("wout_scr", [D, D], BF16),
        ("xq1t_park", [NT, D], BF16),
        ("bounce16", [1, 16], F32),
        ("hq1t_park", [NT, I1], BF16),
    ]:
        dr[name] = nc.dram_tensor(name, shape, dt)
    for j in range(4):
        dr[f"cc_in{j}"] = nc.dram_tensor(f"cc_in{j}", [1, 4], F32)
        dr[f"cc_out{j}"] = nc.dram_tensor(f"cc_out{j}", [1, 4], F32)

    dbg_t = nc.dram_tensor("dbg", DBG_SHAPE, F32, kind="ExternalOutput") if dbg else None

    def S(name, shape, dt):
        return es.enter_context(nc.sbuf_tensor(name, shape, dt))

    xnfm = S("xnfm", [128, DCH * NT], BF16)
    ident = S("ident", [128, 128], BF16)
    identF = S("identF", [128, 128], F32)
    iia = S("iia", [128, 128], I32)
    scal = S("scal", [128, 32], F32)
    gmax = S("gmax", [128, 16], F32)
    row = S("rowA", [1, 32], F32)
    lnr = S("lnr", [128, 16], F32)

    C_CX, C_SX = 0, 1

    blk = es.enter_context(nc.Block())

    k.inc(gp.iota(iia[:, :], [[1, 128]], channel_multiplier=-1,
                  allow_small_or_imprecise_dtypes=True), "gS")
    for j, wn in enumerate(["l0_wq", "l0_wk", "l0_wv", "l0_wo"]):
        k.inc(gp.dma_start(out=dr["wb_scr"][j * D:(j + 1) * D, :], in_=w[wn][:, :]), "dG", 16)
    k.wait(vec, "gS")
    k.inc(vec.tensor_scalar(ident[:, :], iia[:, :], 0.0, 1.0, Alu.is_equal, Alu.mult), "vS")
    k.inc(vec.tensor_scalar(identF[:, :], iia[:, :], 0.0, 1.0, Alu.is_equal, Alu.mult), "vS")
    k.inc(vec.memset(gmax[:, :], 0.0), "vS")
    k.inc(vec.memset(scal[:, :], 0.0), "vS")
    k.inc(vec.memset(row[:, :], 0.0), "vS")

    def ln_pass(tag, src_dram, gmax_col, dst_fm, park_xfm=False, park_f32=None):
        with (
            nc.sbuf_tensor(f"xst{tag}", [128, 2 * D], F32) as xstage,
            nc.sbuf_tensor(f"xnt{tag}", [128, 2 * D], BF16) as xnt,
            nc.sbuf_tensor(f"xfs{tag}", [128, 2 * D], F32) as xfs,
            nc.psum_tensor(f"psL{tag}", [128, 2048], F32) as psL,
        ):
            park_tok = []
            stats_tok = []
            xf_tok = []
            xft_tok = []
            for it in range(NT // 128):
                buf = it % 2
                if it >= 2:
                    k.wait(sync, "vS", stats_tok[it - 2])
                    if park_xfm:
                        k.wait(sync, "tS", xft_tok[it - 2])
                k.inc(sync.dma_start(out=xstage[:, buf * D:(buf + 1) * D],
                                     in_=src_dram[it * 128:(it + 1) * 128, :]), "dS", 16)
                xt = xstage[:, buf * D:(buf + 1) * D]
                k.wait(vec, "dS")
                k.inc(vec.bn_stats(lnr[:, 0:6], xt[:, 0:512]), "vS")
                k.inc(vec.bn_stats(lnr[:, 6:12], xt[:, 512:1024]), "vS")
                k.wait(vec, "vS")
                k.inc(vec.bn_aggr(lnr[:, 12:14], lnr[:, 0:12]), "vS")
                k.wait(vec, "vS")
                k.inc(vec.tensor_scalar(lnr[:, 3:4], lnr[:, 12:13], -1.0, None, Alu.mult), "vS")
                k.inc(vec.tensor_scalar(lnr[:, 6:7], lnr[:, 13:14], EPS, None, Alu.add), "vS")
                k.wait(vec, "vS")
                k.inc(vec.reciprocal(lnr[:, 7:8], lnr[:, 6:7]), "vS")
                k.wait(sca, "vS")
                k.inc(sca.activation(lnr[:, 8:9], lnr[:, 7:8], Act.Sqrt), "sS")
                k.wait(vec, "sS")
                if it >= 2:
                    k.wait(vec, "dA", park_tok[it - 2])
                cxn = k.inc(vec.tensor_scalar(xnt[:, buf * D:(buf + 1) * D], xt,
                                              lnr[:, 3:4], lnr[:, 8:9], Alu.add, Alu.mult), "vS")
                if park_f32 is not None:
                    k.inc(vec.tensor_scalar(xfs[:, buf * D:(buf + 1) * D], xt,
                                            lnr[:, 3:4], lnr[:, 8:9], Alu.add, Alu.mult), "vS")
                    cxn = k.tok(vec, "vS")
                    k.wait(sca, "vS", cxn)
                    k.wait(sca, "dA")
                    k.inc(sca.dma_start(out=park_f32[it * 128:(it + 1) * 128, :],
                                        in_=xfs[:, buf * D:(buf + 1) * D]), "dA", 16)
                stats_tok.append(cxn)
                k.wait(vec, "vS", cxn)
                k.inc(vec.tensor_reduce(lnr[:, 9:10], xnt[:, buf * D:(buf + 1) * D], axis=AX.X,
                                        op=Alu.max, apply_absolute_value=True), "vS")
                k.wait(vec, "vS")
                k.inc(vec.tensor_tensor(gmax[:, gmax_col:gmax_col + 1],
                                        gmax[:, gmax_col:gmax_col + 1], lnr[:, 9:10], Alu.max), "vS")
                k.wait(sca, "vS", cxn)
                park_tok.append(k.inc(sca.dma_start(out=dr["xnt_park"][it * 128:(it + 1) * 128, :],
                                                    in_=xnt[:, buf * D:(buf + 1) * D]), "dA", 16))
                if park_xfm:
                    k.wait(ten, "vS", cxn)
                    if it >= 2:
                        k.wait(ten, "sS", xf_tok[it - 2])
                    ti = None
                    for c in range(DCH):
                        ti = ten.transpose(psL[:, buf * 1024 + c * 128: buf * 1024 + (c + 1) * 128],
                                           xstage[:, buf * D + c * 128: buf * D + (c + 1) * 128],
                                           identF[:, :])
                    tt_ = k.tok(ten, "tS")
                    xft_tok.append(tt_)
                    k.wait(sca, "tS", tt_)
                    k.wait(sca, "dA")
                    ei = sca.copy(xfs[:, buf * D:(buf + 1) * D], psL[:, buf * 1024:(buf + 1) * 1024])
                    xf_tok.append(k.inc(ei, "sS"))
                    k.wait(sca, "sS", xf_tok[-1])
                    for c in range(DCH):
                        k.inc(sca.dma_start(
                            out=dr["xfm_park"][c * 128:(c + 1) * 128, it * 128:(it + 1) * 128],
                            in_=xfs[:, buf * D + c * 128: buf * D + (c + 1) * 128]), "dA", 16)
            k.wait(sync, "dA")
            for c in range(DCH):
                k.inc(sync.dma_start(out=dst_fm[:, c * NT:(c + 1) * NT],
                                     in_=dr["xnt_park"][:, c * 128:(c + 1) * 128],
                                     transpose=True), "dS", 16)

    ln_pass("0", x_in, 0, xnfm, park_xfm=True)

    k.wait(gp, "vS")
    k.inc(gp.tensor_reduce(row[0:1, 0:1], gmax[:, 0:1], axis=AX.C, op=Alu.max), "gS")
    k.wait(vec, "gS")
    k.inc(vec.tensor_scalar(row[0:1, 1:2], row[0:1, 0:1], 1.0 / QMAX, 1e-12, Alu.mult, Alu.max), "vS")
    k.wait(gp, "vS")
    k.inc(gp.dma_start(out=dr["cc_in0"][0:1, 0:4], in_=row[0:1, 0:4]), "dG", 16)
    k.wait(gp, "dG")
    k.inc(gp.collective_compute("AllReduce", Alu.max, replica_groups=[list(range(NCORES))],
                                ins=[dr["cc_in0"].ap()], outs=[dr["cc_out0"].ap()]), "cS")
    k.wait(gp, "cS")
    k.inc(gp.dma_start(out=row[0:1, 4:8], in_=dr["cc_out0"][0:1, 0:4]), "dG", 16)
    k.wait(vec, "dG")
    k.inc(vec.tensor_copy(scal[0:1, C_SX:C_SX + 1], row[0:1, 5:6]), "vS")
    k.inc(vec.reciprocal(scal[0:1, C_CX:C_CX + 1], row[0:1, 5:6]), "vS")

    def l0_attn_seq(s):
        k.barrier(ALL_ENG)
        with (
            nc.sbuf_tensor(f"yfm{s}", [128, DCH * T], BF16) as yfm,
            nc.sbuf_tensor(f"qn2{s}", [128, 16], F32) as qn2,
            nc.sbuf_tensor(f"kn2{s}", [128, 16], F32) as kn2,
            nc.sbuf_tensor(f"mcol{s}", [128, 16], F32) as mcol,
        ):
            k.wait(vec, "vS")
            k.inc(vec.memset(qn2[:, :], 0.0), "vS")
            k.inc(vec.memset(kn2[:, :], 0.0), "vS")
            with (
                nc.sbuf_tensor(f"kfm{s}", [128, DCH * T], BF16) as kfm,
                nc.sbuf_tensor(f"v65_{s}", [128, 16 * 16 * 65], BF16) as v65,
            ):
                k.wait(vec, "vS")
                k.inc(vec.memset(v65[:, :], 1.0), "vS")
                with (
                    nc.sbuf_tensor(f"wkm{s}", [128, 2 * DCH * 1024], BF16) as wkm,
                    nc.sbuf_tensor(f"c2a{s}", [128, 16 * 64], BF16) as c2a,
                    nc.sbuf_tensor(f"s2a{s}", [128, 16 * 64], BF16) as s2a,
                    nc.sbuf_tensor(f"qkrot{s}", [128, 2 * 1024], BF16) as qkrot,
                    nc.sbuf_tensor(f"sqst{s}", [128, 1024], BF16) as sqst,
                    nc.psum_tensor(f"psP{s}", [128, 3072], F32) as psP,
                ):
                    k.wait(sync, "dG")
                    for j in range(2):
                        for ic in range(DCH):
                            k.inc(sync.dma_start(
                                out=wkm[:, (j * DCH + ic) * 1024:(j * DCH + ic + 1) * 1024],
                                in_=dr["wb_scr"][j * D:(j + 1) * D, ic * 128:(ic + 1) * 128],
                                transpose=True), "dS", 16)
                    k.inc(gp.dma_start(out=c2a[:, :], in_=bass.AP(w["rope_c2"], 0,
                        [[64, 128], [8192, 16], [1, 64]])), "dG", 16)
                    k.inc(gp.dma_start(out=s2a[:, :], in_=bass.AP(w["rope_s2"], 0,
                        [[64, 128], [8192, 16], [1, 64]])), "dG", 16)
                    wload_tok = k.cnt["dS"]
                    k.wait(vec, "dG")

                    def qk_mms(tc, wslice, psslice, tok0):
                        mi = None
                        for c in range(DCH):
                            stat = xnfm[:, c * NT + tok0: c * NT + tok0 + 128]
                            for j, nh in [(jj, nn) for jj in wslice for nn in range(2)]:
                                mi = ten.matmul(
                                    psP[:, psslice[j] * 1024 + nh * 512: psslice[j] * 1024 + (nh + 1) * 512],
                                    stat,
                                    wkm[:, (j * DCH + c) * 1024 + nh * 512:
                                        (j * DCH + c) * 1024 + (nh + 1) * 512],
                                    start=(c == 0), stop=(c == DCH - 1))
                        return mi

                    rope_tok = []
                    park_tok = []
                    for tc in range(16):
                        tok0 = s * T + tc * 128
                        k.wait(ten, "dS", wload_tok)
                        if tc >= 1:
                            k.wait(ten, "vS", rope_tok[tc - 1])
                        mt = k.inc(qk_mms(tc, [0, 1], {0: 0, 1: 1}, tok0), "tS")
                        k.wait(vec, "tS", mt)
                        if tc >= 2:
                            k.wait(vec, "dA", park_tok[2 * (tc - 2) + 1])
                        c2s = c2a[:, tc * 64:(tc + 1) * 64].unsqueeze(1).broadcast_to((128, 16, 64))
                        s2s = s2a[:, tc * 64:(tc + 1) * 64].unsqueeze(1).broadcast_to((128, 16, 64))
                        for j in range(2):
                            src = psP[:, j * 1024:(j + 1) * 1024]
                            t1 = psP[:, 2048:3072]
                            k.wait(vec, "vS")
                            k.inc(vec.tensor_tensor(t1.rearrange("p (h d) -> p h d", d=64),
                                                    src.rearrange("p (h d) -> p h d", d=64), c2s,
                                                    Alu.mult), "vS")
                            srcp = src.rearrange("p (h i two) -> p h i two", h=16, two=2)
                            s2p = s2s.rearrange("p h (i two) -> p h i two", two=2)
                            dst = qkrot[:, j * 1024:(j + 1) * 1024].rearrange("p (h i two) -> p h i two", h=16, two=2)
                            k.inc(vec.tensor_tensor(dst[:, :, :, 0], srcp[:, :, :, 1], s2p[:, :, :, 0],
                                                    Alu.mult), "vS")
                            k.inc(vec.tensor_tensor(dst[:, :, :, 1], srcp[:, :, :, 0], s2p[:, :, :, 1],
                                                    Alu.mult), "vS")
                            k.wait(vec, "vS")
                            k.inc(vec.tensor_tensor(qkrot[:, j * 1024:(j + 1) * 1024],
                                                    qkrot[:, j * 1024:(j + 1) * 1024], t1, Alu.add), "vS")
                        rope_tok.append(k.cnt["vS"])
                        for j, acc in [(0, qn2), (1, kn2)]:
                            k.wait(sca, "vS")
                            k.inc(sca.activation(sqst[:, :], qkrot[:, j * 1024:(j + 1) * 1024],
                                                 Act.Square), "sS")
                            k.wait(vec, "sS")
                            k.inc(vec.tensor_reduce(lnr[:, 0:16].unsqueeze(2),
                                                    sqst[:, :].rearrange("p (h d) -> p h d", d=64),
                                                    axis=AX.X, op=Alu.add), "vS")
                            k.wait(vec, "vS")
                            k.inc(vec.tensor_tensor(acc[:, :], acc[:, :], lnr[:, 0:16], Alu.max), "vS")
                        k.wait(sca, "vS")
                        park_tok.append(k.inc(sca.dma_start(out=dr["qkt_park"][tc * 128:(tc + 1) * 128, :],
                                                            in_=qkrot[:, 0:1024]), "dA", 16))
                        park_tok.append(k.inc(sca.dma_start(
                            out=dr["qkt_park"][T + tc * 128:T + (tc + 1) * 128, :],
                            in_=qkrot[:, 1024:2048]), "dA", 16))
                    k.wait(sync, "tS")
                    for ic in range(DCH):
                        k.inc(sync.dma_start(out=wkm[:, ic * 1024:(ic + 1) * 1024],
                                             in_=dr["wb_scr"][2 * D:3 * D, ic * 128:(ic + 1) * 128],
                                             transpose=True), "dS", 16)
                    vload_tok = k.cnt["dS"]
                    vev_tok = []
                    for tc in range(16):
                        tok0 = s * T + tc * 128
                        k.wait(ten, "dS", vload_tok)
                        if tc >= 2:
                            k.wait(ten, "vS", vev_tok[tc - 2])
                        mi = None
                        for c in range(DCH):
                            stat = xnfm[:, c * NT + tok0: c * NT + tok0 + 128]
                            for nh in range(2):
                                mi = ten.matmul(psP[:, (tc % 2) * 1024 + nh * 512:
                                                     (tc % 2) * 1024 + (nh + 1) * 512],
                                                stat,
                                                wkm[:, c * 1024 + nh * 512: c * 1024 + (nh + 1) * 512],
                                                start=(c == 0), stop=(c == DCH - 1))
                        mt = k.inc(mi, "tS")
                        k.wait(vec, "tS", mt)
                        vdst = v65[:, tc * 1040: (tc + 1) * 1040].rearrange("p (h e) -> p h e", e=65)
                        vev_tok.append(k.inc(vec.tensor_copy(
                            vdst[:, :, 0:64],
                            psP[:, (tc % 2) * 1024:(tc % 2 + 1) * 1024].rearrange("p (h d) -> p h d", d=64)),
                            "vS"))
                    k.wait(sync, "dA")
                    for c in range(DCH):
                        k.inc(sync.dma_start(out=kfm[:, c * T:(c + 1) * T],
                                             in_=dr["qkt_park"][T:2 * T, c * 128:(c + 1) * 128],
                                             transpose=True), "dS", 16)
                    k.wait(gp, "vS")
                    k.inc(gp.tensor_reduce(row[0:1, 0:16], qn2[:, :], axis=AX.C, op=Alu.max), "gS")
                    k.wait(gp, "gS")
                    k.inc(gp.tensor_reduce(row[0:1, 16:32], kn2[:, :], axis=AX.C, op=Alu.max), "gS")
                    k.wait(vec, "gS")
                    k.inc(vec.tensor_tensor(row[0:1, 0:16], row[0:1, 0:16], row[0:1, 16:32],
                                            Alu.mult), "vS")
                    k.wait(sca, "vS")
                    k.inc(sca.activation(row[0:1, 0:16], row[0:1, 0:16], Act.Sqrt), "sS")
                    k.wait(vec, "sS")
                    k.inc(vec.tensor_scalar(mcol[0:1, 0:16], row[0:1, 0:16], -0.126, None, Alu.mult), "vS")
                    k.wait(sync, "vS")
                    doubling(k, sync, mcol, 16, "dS")

                k.barrier(ALL_ENG)
                with (
                    nc.sbuf_tensor(f"qbuf{s}", [128, 2 * 512], BF16) as qbuf,
                    nc.sbuf_tensor(f"pbuf{s}", [128, 2 * 3072], BF16) as pbuf,
                    nc.sbuf_tensor(f"yst{s}", [128, T], F32) as ystage,
                    nc.sbuf_tensor(f"ycop{s}", [128, 1024], F32) as ycop,
                    nc.sbuf_tensor(f"zrep{s}", [128, T], F32) as zrep,
                    nc.psum_tensor(f"psS{s}", [128, 3072], F32) as psS,
                    nc.psum_tensor(f"psY{s}", [128, 1024], F32) as psY,
                ):
                    gsz = [3, 3, 3, 3, 3, 1]
                    gof = [0, 3, 6, 9, 12, 15]
                    NG = 6
                    yev_tok = None
                    ysc_tok = None
                    ycop_tok = None
                    for hp in range(8):
                        hA = 2 * hp
                        qld_tok = [None] * 4
                        qsc_tok = [None] * 4
                        for qt in range(4):
                            qb = qt % 2
                            if qt >= 2:
                                k.wait(sync, "tS", qsc_tok[qt - 2])
                            qld_tok[qt] = k.inc(sync.dma_start(
                                out=qbuf[:, qb * 512:(qb + 1) * 512],
                                in_=dr["qkt_park"][qt * 512:(qt + 1) * 512, hp * 128:(hp + 1) * 128],
                                transpose=True), "dS", 16)
                            exp_tok = []
                            for g in range(NG):
                                k.wait(ten, "dS", qld_tok[qt])
                                if g >= 1:
                                    k.wait(ten, "sS", exp_tok[g - 1])
                                elif yev_tok is not None:
                                    k.wait(ten, "vS", yev_tok)
                                mi = None
                                for jj in range(gsz[g]):
                                    kc = gof[g] + jj
                                    for hh in range(2):
                                        mi = ten.matmul(
                                            psS[:, jj * 1024 + hh * 512: jj * 1024 + (hh + 1) * 512],
                                            kfm[hh * 64:(hh + 1) * 64,
                                                hp * T + kc * 128: hp * T + (kc + 1) * 128],
                                            qbuf[hh * 64:(hh + 1) * 64, qb * 512:(qb + 1) * 512],
                                            start=True, stop=True)
                                sc_tok = k.tok(ten, "tS")
                                qsc_tok[qt] = sc_tok
                                k.wait(sca, "tS", sc_tok)
                                pp = g % 2
                                ei = None
                                for hh in range(2):
                                    for jj in range(gsz[g]):
                                        ei = sca.activation(
                                            pbuf[:, pp * 3072 + jj * 1024 + hh * 512:
                                                 pp * 3072 + jj * 1024 + (hh + 1) * 512],
                                            psS[:, jj * 1024 + hh * 512: jj * 1024 + (hh + 1) * 512],
                                            Act.Exp, bias=mcol[:, (hA + hh):(hA + hh) + 1], scale=0.125)
                                exp_tok.append(k.tok(sca, "sS"))
                                if dbg == "P0" and s == 0 and hp == 0 and qt == 0 and g == 0:
                                    k.wait(vec, "sS")
                                    with nc.sbuf_tensor("scdump", [128, 3072], F32) as scd:
                                        k.inc(vec.tensor_copy(scd[:, :], psS[:, :]), "vS")
                                        k.wait(gp, "vS")
                                        k.inc(gp.dma_start(out=dbg_t[:, 0:3072], in_=scd[:, :]), "dG", 16)
                                    k.wait(gp, "dG")
                                    k.inc(gp.dma_start(out=dbg_t[:, 4096:4096 + 3072],
                                                       in_=pbuf[:, 0:3072]), "dG", 16)
                                    k.inc(gp.dma_start(out=dbg_t[:, 8192:8192 + 16],
                                                       in_=mcol[:, 0:16]), "dG", 16)
                                    k.wait(gp, "dG")
                                    k.wait(sca, "dG")
                                k.wait(ten, "sS", exp_tok[g])
                                mi = None
                                for jj in range(gsz[g]):
                                    kc = gof[g] + jj
                                    for hh in range(2):
                                        mi = ten.matmul(
                                            psY[0:65, hh * 512:(hh + 1) * 512],
                                            v65[:, kc * 1040 + (hA + hh) * 65:
                                                kc * 1040 + (hA + hh) * 65 + 65],
                                            pbuf[:, pp * 3072 + jj * 1024 + hh * 512:
                                                 pp * 3072 + jj * 1024 + (hh + 1) * 512],
                                            start=(g == 0 and jj == 0),
                                            stop=(g == NG - 1 and jj == gsz[g] - 1))
                                qsc_tok[qt] = k.inc(mi, "tS")
                            k.wait(vec, "tS")
                            if ysc_tok is not None:
                                k.wait(vec, "vS", ysc_tok)
                            if ycop_tok is not None:
                                k.wait(vec, "dS", ycop_tok)
                            k.inc(vec.tensor_copy(ystage[0:64, qt * 512:(qt + 1) * 512],
                                                  psY[0:64, 0:512]), "vS")
                            k.inc(vec.tensor_copy(ycop[0:64, 0:512], psY[0:64, 512:1024]), "vS")
                            yev_tok = k.inc(vec.tensor_copy(ycop[64:65, 0:1024], psY[64:65, 0:1024]), "vS")
                            k.wait(sync, "vS")
                            k.inc(sync.dma_start(out=ystage[64:128, qt * 512:(qt + 1) * 512],
                                                 in_=ycop[0:64, 0:512]), "dS", 16)
                            k.inc(sync.dma_start(out=zrep[0:1, qt * 512:(qt + 1) * 512],
                                                 in_=ycop[64:65, 0:512]), "dS", 16)
                            ycop_tok = k.inc(sync.dma_start(out=zrep[64:65, qt * 512:(qt + 1) * 512],
                                                            in_=ycop[64:65, 512:1024]), "dS", 16)
                        k.wait(vec, "dS", ycop_tok)
                        k.inc(vec.reciprocal(zrep[0:1, :], zrep[0:1, :]), "vS")
                        k.inc(vec.reciprocal(zrep[64:65, :], zrep[64:65, :]), "vS")
                        k.wait(sync, "vS")
                        p2 = 1
                        while p2 < 64:
                            k.inc(sync.dma_start(out=zrep[p2:2 * p2, :], in_=zrep[0:p2, :]), "dS", 16)
                            k.inc(sync.dma_start(out=zrep[64 + p2:64 + 2 * p2, :],
                                                 in_=zrep[64:64 + p2, :]), "dS", 16)
                            k.wait(sync, "dS")
                            p2 *= 2
                        k.wait(vec, "dS")
                        ysc_tok = k.inc(vec.tensor_tensor(yfm[:, hp * T:(hp + 1) * T], ystage[:, :],
                                                          zrep[:, :], Alu.mult), "vS")
                    if dbg == "yz0" and s == 0:
                        k.barrier([gp])
                        k.inc(gp.dma_start(out=dbg_t[:, 0:2048], in_=ystage[:, :]), "dG", 16)
                        k.inc(gp.dma_start(out=dbg_t[:, 2048:4096], in_=zrep[:, :]), "dG", 16)
                        k.inc(gp.dma_start(out=dbg_t[:, 8192:8192 + 16], in_=mcol[:, 0:16]), "dG", 16)
                        k.wait(gp, "dG")
            if dbg == "yfm0" and s == 0:
                k.barrier([gp])
                for c in range(8):
                    k.inc(gp.dma_start(out=dbg_t[:, c * 2048:(c + 1) * 2048],
                                       in_=yfm[:, c * T:(c + 1) * T]), "dG", 16)
                k.wait(gp, "dG")
            k.barrier(ALL_ENG)
            with (
                nc.sbuf_tensor(f"wo{s}", [128, DCH * 1024], BF16) as wo_km,
                nc.sbuf_tensor(f"ost{s}", [128, 2 * 2048], F32) as ostage,
                nc.psum_tensor(f"psO{s}", [128, 2048], F32) as psO,
            ):
                for ic in range(DCH):
                    k.inc(sync.dma_start(out=wo_km[:, ic * 1024:(ic + 1) * 1024],
                                         in_=dr["wb_scr"][3 * D:4 * D, ic * 128:(ic + 1) * 128],
                                         transpose=True), "dS", 16)
                osp = [None, None]
                est = [None, None]
                for oc in range(DCH):
                    k.wait(ten, "dS")
                    if oc >= 1:
                        k.wait(ten, "sS", est[(oc - 1) % 2])
                    mi = None
                    for yc in range(DCH):
                        for tb in range(4):
                            mi = ten.matmul(psO[:, tb * 512:(tb + 1) * 512],
                                            wo_km[:, yc * 1024 + oc * 128: yc * 1024 + oc * 128 + 128],
                                            yfm[:, yc * T + tb * 512: yc * T + (tb + 1) * 512],
                                            start=(yc == 0), stop=(yc == DCH - 1))
                    mt = k.inc(mi, "tS")
                    ob = oc % 2
                    k.wait(sca, "tS", mt)
                    if oc >= 2:
                        k.wait(sca, "dA", osp[ob])
                    est[ob] = k.inc(sca.copy(ostage[:, ob * 2048:(ob + 1) * 2048], psO[:, :]), "sS")
                    k.wait(sca, "sS", est[ob])
                    osp[ob] = k.inc(sca.dma_start(out=dr["attn_sp"][oc * 128:(oc + 1) * 128,
                                                                    s * T:(s + 1) * T],
                                                  in_=ostage[:, ob * 2048:(ob + 1) * 2048]), "dA", 16)

    l0_attn_seq(0)
    if dbg not in ("qpark0", "yfm0", "yz0"):
        l0_attn_seq(1)
    if dbg == "qpark0":
        k.barrier([gp])
        for i in range(32):
            k.inc(gp.dma_start(out=dbg_t[:, i * 1024:(i + 1) * 1024],
                               in_=dr["qkt_park"][i * 128:(i + 1) * 128, :]), "dG", 16)
        k.wait(gp, "dG")

    k.barrier(ALL_ENG)
    xq_cm = nc.sbuf_tensor("xqfm", [128, DCH * NT], BF16)
    xqfm = xq_cm.__enter__()
    k.wait(sync, "vS")
    doubling(k, sync, scal, 32, "dS")
    with nc.sbuf_tensor("xqi", [128, 2048], I32) as xqi:
        k.wait(vec, "dS")
        for st in range(DCH * NT // 2048):
            cols = slice(st * 2048, (st + 1) * 2048)
            k.wait(vec, "vS")
            ci = k.inc(vec.tensor_scalar(xqi[:, :], xnfm[:, cols], scal[:, C_CX:C_CX + 1], None,
                                         Alu.mult), "vS")
            k.wait(vec, "vS", ci)
            k.inc(vec.tensor_copy(xqfm[:, cols], xqi[:, :]), "vS")

    C_C2U, C_C2G, C_SG, C_C2G2, C_C2O, C_SO, C_SU, C_SG2 = 2, 3, 4, 5, 6, 7, 8, 9

    def wqdq_lpbq(wsrc, scr, nchunks, width, blk_sz, qmax_w, tag):
        nb = width // blk_sz
        with (
            nc.sbuf_tensor(f"wn{tag}", [128, 2 * width], F32) as wn,
            nc.sbuf_tensor(f"wb{tag}", [128, 2 * nb], F32) as wbm,
            nc.sbuf_tensor(f"wi{tag}", [128, width], I32) as wi,
            nc.sbuf_tensor(f"wd{tag}", [128, 2 * width], BF16) as wd,
        ):
            out_tok = []
            for oc in range(nchunks):
                bu = oc % 2
                if oc >= 2:
                    k.wait(sync, "vS", out_tok[oc - 2])
                k.inc(sync.dma_start(out=wn[:, bu * width:(bu + 1) * width],
                                     in_=wsrc[oc * 128:(oc + 1) * 128, :]), "dS", 16)
                wt = wn[:, bu * width:(bu + 1) * width]
                k.wait(vec, "dS")
                k.wait(vec, "vS")
                k.inc(vec.tensor_reduce(wbm[:, bu * nb:(bu + 1) * nb].unsqueeze(2),
                                        wt.rearrange("p (b e) -> p b e", e=blk_sz),
                                        axis=AX.X, op=Alu.max, apply_absolute_value=True), "vS")
                k.wait(vec, "vS")
                k.inc(vec.tensor_scalar(wbm[:, bu * nb:(bu + 1) * nb], wbm[:, bu * nb:(bu + 1) * nb],
                                        1.0 / qmax_w, 1e-12, Alu.mult, Alu.max), "vS")
                k.wait(vec, "vS")
                k.inc(vec.reciprocal(wbm[:, bu * nb:(bu + 1) * nb], wbm[:, bu * nb:(bu + 1) * nb]), "vS")
                k.wait(vec, "vS")
                rsb = wbm[:, bu * nb:(bu + 1) * nb].unsqueeze(2).broadcast_to((128, nb, blk_sz))
                k.inc(vec.tensor_tensor(wt.rearrange("p (b e) -> p b e", e=blk_sz),
                                        wt.rearrange("p (b e) -> p b e", e=blk_sz), rsb, Alu.mult), "vS")
                k.wait(vec, "vS")
                k.inc(vec.tensor_scalar(wi[:, :], wt, 1.0, None, Alu.mult), "vS")
                k.wait(vec, "vS")
                k.inc(vec.reciprocal(wbm[:, bu * nb:(bu + 1) * nb], wbm[:, bu * nb:(bu + 1) * nb]), "vS")
                k.wait(vec, "vS")
                sb = wbm[:, bu * nb:(bu + 1) * nb].unsqueeze(2).broadcast_to((128, nb, blk_sz))
                k.inc(vec.tensor_tensor(wd[:, bu * width:(bu + 1) * width].rearrange(
                    "p (b e) -> p b e", e=blk_sz), wi[:, :].rearrange("p (b e) -> p b e", e=blk_sz),
                    sb, Alu.mult), "vS")
                ot = k.tok(vec, "vS")
                out_tok.append(ot)
                k.wait(sca, "vS", ot)
                k.inc(sca.dma_start(out=scr[oc * 128:(oc + 1) * 128, :],
                                    in_=wd[:, bu * width:(bu + 1) * width]), "dA", 16)
        k.barrier(ALL_ENG)

    def ffn_mm_spill(wkm_t, spill, gcol, kchunks, ochunks):
        with (
            nc.sbuf_tensor(f"fst{gcol}", [128, 2 * 4096], BF16) as fst,
            nc.psum_tensor(f"psF{gcol}", [128, 4096], F32) as psF,
        ):
            ev_tok = [None, None]
            sp_tok = [None, None]
            for oc in range(ochunks):
                bu = oc % 2
                k.wait(ten, "dS")
                if oc >= 1:
                    k.wait(ten, "sS", ev_tok[(oc - 1) % 2])
                mi = None
                for kc in range(kchunks):
                    for tb in range(8):
                        mi = ten.matmul(psF[:, tb * 512:(tb + 1) * 512],
                                        wkm_t[:, kc * (128 * ochunks) + oc * 128:
                                              kc * (128 * ochunks) + (oc + 1) * 128],
                                        xqfm[:, kc * NT + tb * 512: kc * NT + (tb + 1) * 512],
                                        start=(kc == 0), stop=(kc == kchunks - 1))
                mt = k.tok(ten, "tS")
                k.wait(sca, "tS", mt)
                if oc >= 2:
                    k.wait(sca, "dA", sp_tok[bu])
                ev_tok[bu] = k.tok(sca, "sS") if False else None
                ei = sca.copy(fst[:, bu * 4096:(bu + 1) * 4096], psF[:, :])
                ev_tok[bu] = k.inc(ei, "sS")
                k.wait(vec, "sS", ev_tok[bu])
                k.inc(vec.tensor_reduce(lnr[:, 0:1], fst[:, bu * 4096:(bu + 1) * 4096],
                                        axis=AX.X, op=Alu.max, apply_absolute_value=True), "vS")
                k.wait(vec, "vS")
                k.inc(vec.tensor_tensor(gmax[:, gcol:gcol + 1], gmax[:, gcol:gcol + 1],
                                        lnr[:, 0:1], Alu.max), "vS")
                k.wait(sca, "sS", ev_tok[bu])
                sp_tok[bu] = k.inc(sca.dma_start(
                    out=bass.AP(spill, oc * 128 * 512,
                                [[512, 128], [24 * 128 * 512, 8], [1, 512]]),
                    in_=fst[:, bu * 4096:(bu + 1) * 4096].rearrange("p (t e) -> p t e", e=512)),
                    "dA", 16)

    def allreduce_max(idx, srcs, dsts):
        k.wait(gp, "vS")
        for i, c in enumerate(srcs):
            k.inc(gp.tensor_reduce(row[0:1, i:i + 1], gmax[:, c:c + 1], axis=AX.C, op=Alu.max), "gS")
            k.wait(gp, "gS")
        k.inc(gp.dma_start(out=dr[f"cc_in{idx}"][0:1, 0:4], in_=row[0:1, 0:4]), "dG", 16)
        k.wait(gp, "dG")
        k.inc(gp.collective_compute("AllReduce", Alu.max, replica_groups=[list(range(NCORES))],
                                    ins=[dr[f"cc_in{idx}"].ap()], outs=[dr[f"cc_out{idx}"].ap()]), "cS")
        k.wait(gp, "cS")
        k.inc(gp.dma_start(out=row[0:1, 4:8], in_=dr[f"cc_out{idx}"][0:1, 0:4]), "dG", 16)

    k.barrier(ALL_ENG)
    wqdq_lpbq(w["l0_wup"], dr["wup_scr"], 24, 1024, 32, 7.0, "u0")
    wqdq_lpbq(w["l0_wgate"], dr["wgate_scr"], 24, 1024, 32, 7.0, "g0")

    with nc.sbuf_tensor("wffkm", [128, DCH * I0], BF16) as wffkm:
        k.barrier(ALL_ENG)
        for ic in range(DCH):
            k.inc(sync.dma_start(out=wffkm[:, ic * I0:(ic + 1) * I0],
                                 in_=dr["wup_scr"][:, ic * 128:(ic + 1) * 128],
                                 transpose=True), "dS", 16)
        ffn_mm_spill(wffkm, dr["spill_up"], 1, DCH, 24)
        k.barrier([sync])
        k.wait(sync, "tS")
        for ic in range(DCH):
            k.inc(sync.dma_start(out=wffkm[:, ic * I0:(ic + 1) * I0],
                                 in_=dr["wgate_scr"][:, ic * 128:(ic + 1) * 128],
                                 transpose=True), "dS", 16)
        ffn_mm_spill(wffkm, dr["spill_gate"], 2, DCH, 24)

    xq_cm.__exit__(None, None, None)

    allreduce_max(1, [1, 2], None)
    k.wait(vec, "dG")
    for i, (c2c, ssc) in enumerate([(C_C2U, C_SU), (C_C2G, C_SG)]):
        k.inc(vec.tensor_tensor(row[0:1, 8 + i:9 + i], row[0:1, 4 + i:5 + i],
                                scal[0:1, C_SX:C_SX + 1], Alu.mult), "vS")
        k.wait(vec, "vS")
        k.inc(vec.tensor_scalar(row[0:1, 8 + i:9 + i], row[0:1, 8 + i:9 + i],
                                1.0 / QMAX, 1e-12, Alu.mult, Alu.max), "vS")
        k.wait(vec, "vS")
        k.inc(vec.tensor_copy(scal[0:1, ssc:ssc + 1], row[0:1, 8 + i:9 + i]), "vS")
        k.inc(vec.reciprocal(row[0:1, 12 + i:13 + i], row[0:1, 8 + i:9 + i]), "vS")
        k.wait(vec, "vS")
        k.inc(vec.tensor_tensor(scal[0:1, c2c:c2c + 1], row[0:1, 12 + i:13 + i],
                                scal[0:1, C_SX:C_SX + 1], Alu.mult), "vS")
    k.wait(sync, "vS")
    doubling(k, sync, scal, 32, "dS")

    HW_ = 6 * 512
    with (
        nc.sbuf_tensor("cin", [128, 2 * 24 * 512], BF16) as cin,
        nc.sbuf_tensor("ci32", [128, HW_], I32) as ci32,
        nc.sbuf_tensor("csig", [128, HW_], I32) as csig,
        nc.sbuf_tensor("cprod", [128, HW_], F32) as cprod,
        nc.sbuf_tensor("cout", [128, 24 * 512], BF16) as cout,
        nc.sbuf_tensor("cpad", [128, 8], F32) as cpad,
    ):
        ld_tok = [None, None]
        st_tok = []
        codma = None
        for tb in range(8):
            bu = tb % 2
            if tb >= 2:
                k.wait(sync, "vS", st_tok[tb - 2])
            ld_tok[bu] = k.inc(sync.dma_start(
                out=cin[:, bu * 24 * 512:(bu + 1) * 24 * 512].rearrange("p (o e) -> p o e", e=512),
                in_=bass.AP(dr["spill_gate"], tb * 24 * 128 * 512, [[512, 128], [128 * 512, 24], [1, 512]])),
                "dS", 16)
            for hf in range(4):
                src = cin[:, bu * 24 * 512 + hf * HW_: bu * 24 * 512 + (hf + 1) * HW_]
                k.wait(vec, "dS", ld_tok[bu])
                k.wait(vec, "vS")
                k.inc(vec.tensor_scalar(ci32[:, :], src, scal[:, C_C2G:C_C2G + 1], None,
                                        Alu.mult), "vS")
                ct = k.tok(vec, "vS")
                k.wait(sca, "vS", ct)
                k.inc(sca.activation(cprod[:, :], ci32[:, :], Act.Sigmoid,
                                     bias=0.0, scale=scal[:, C_SG:C_SG + 1]), "sS")
                st2 = k.tok(sca, "sS")
                k.wait(sca, "sS", st2)
                k.inc(sca.activation(csig[:, :], cprod[:, :], Act.Copy,
                                     bias=0.0, scale=65536.0), "sS")
                st3 = k.tok(sca, "sS")
                k.wait(vec, "sS", st3)
                k.inc(vec.tensor_tensor(cprod[:, :], ci32[:, :], csig[:, :], Alu.mult), "vS")
                k.wait(vec, "vS")
                k.inc(vec.tensor_reduce(lnr[:, 0:1], cprod[:, :], axis=AX.X, op=Alu.max,
                                        apply_absolute_value=True), "vS")
                k.wait(vec, "vS")
                k.inc(vec.tensor_tensor(gmax[:, 3:4], gmax[:, 3:4], lnr[:, 0:1], Alu.max), "vS")
                if codma is not None:
                    k.wait(vec, "dA", codma)
                k.inc(vec.tensor_copy(cout[:, hf * HW_:(hf + 1) * HW_], cprod[:, :]), "vS")
            ot = k.tok(vec, "vS")
            st_tok.append(ot)
            k.wait(sca, "vS", ot)
            codma = k.inc(sca.dma_start(
                out=bass.AP(dr["spill_gate2"], tb * 24 * 128 * 512, [[512, 128], [128 * 512, 24], [1, 512]]),
                in_=cout[:, :].rearrange("p (o e) -> p o e", e=512)), "dA", 16)
        k.wait(sync, "dA")

    allreduce_max(2, [3], None)
    k.wait(vec, "dG")
    k.inc(vec.tensor_tensor(row[0:1, 8:9], row[0:1, 4:5], scal[0:1, C_SG:C_SG + 1], Alu.mult), "vS")
    k.wait(vec, "vS")
    k.inc(vec.tensor_scalar(row[0:1, 8:9], row[0:1, 8:9], 1.0 / (65536.0 * QMAX), 1e-12,
                            Alu.mult, Alu.max), "vS")
    k.wait(vec, "vS")
    k.inc(vec.tensor_copy(scal[0:1, C_SG2:C_SG2 + 1], row[0:1, 8:9]), "vS")
    k.inc(vec.reciprocal(row[0:1, 9:10], row[0:1, 8:9]), "vS")
    k.wait(vec, "vS")
    k.inc(vec.tensor_tensor(row[0:1, 9:10], row[0:1, 9:10], scal[0:1, C_SG:C_SG + 1], Alu.mult), "vS")
    k.wait(vec, "vS")
    k.inc(vec.tensor_scalar(scal[0:1, C_C2G2:C_C2G2 + 1], row[0:1, 9:10], 1.0 / 65536.0, None,
                            Alu.mult), "vS")
    k.wait(sync, "vS")
    doubling(k, sync, scal, 32, "dS")

    with (
        nc.sbuf_tensor("din", [128, 24 * 512], BF16) as din,
        nc.sbuf_tensor("din2", [128, 24 * 512], BF16) as din2,
        nc.sbuf_tensor("di32", [128, HW_], I32) as di32,
        nc.sbuf_tensor("dj32", [128, HW_], I32) as dj32,
        nc.sbuf_tensor("dprod", [128, HW_], F32) as dprod,
        nc.sbuf_tensor("dout", [128, 24 * 512], BF16) as dout,
    ):
        st_tok = []
        dodma = None
        for tb in range(8):
            if tb >= 1:
                k.wait(sync, "vS", st_tok[tb - 1])
            lt = k.inc(sync.dma_start(
                out=din[:, :].rearrange("p (o e) -> p o e", e=512),
                in_=bass.AP(dr["spill_up"], tb * 24 * 128 * 512, [[512, 128], [128 * 512, 24], [1, 512]])),
                "dS", 16)
            lt2 = k.inc(sync.dma_start(
                out=din2[:, :].rearrange("p (o e) -> p o e", e=512),
                in_=bass.AP(dr["spill_gate2"], tb * 24 * 128 * 512, [[512, 128], [128 * 512, 24], [1, 512]])),
                "dS", 16)
            for hf in range(4):
                k.wait(vec, "dS", lt2)
                k.wait(vec, "vS")
                k.inc(vec.tensor_scalar(di32[:, :], din[:, hf * HW_:(hf + 1) * HW_],
                                        scal[:, C_C2U:C_C2U + 1], None, Alu.mult), "vS")
                k.inc(vec.tensor_scalar(dj32[:, :], din2[:, hf * HW_:(hf + 1) * HW_],
                                        scal[:, C_C2G2:C_C2G2 + 1], None, Alu.mult), "vS")
                k.wait(vec, "vS")
                k.inc(vec.tensor_tensor(dprod[:, :], di32[:, :], dj32[:, :], Alu.mult), "vS")
                k.wait(vec, "vS")
                k.inc(vec.tensor_reduce(lnr[:, 0:1], dprod[:, :], axis=AX.X, op=Alu.max,
                                        apply_absolute_value=True), "vS")
                k.wait(vec, "vS")
                k.inc(vec.tensor_tensor(gmax[:, 4:5], gmax[:, 4:5], lnr[:, 0:1], Alu.max), "vS")
                if dodma is not None:
                    k.wait(vec, "dA", dodma)
                k.inc(vec.tensor_copy(dout[:, hf * HW_:(hf + 1) * HW_], dprod[:, :]), "vS")
            ot = k.tok(vec, "vS")
            st_tok.append(ot)
            k.wait(sca, "vS", ot)
            dodma = k.inc(sca.dma_start(
                out=bass.AP(dr["spill_o"], tb * 24 * 128 * 512, [[512, 128], [128 * 512, 24], [1, 512]]),
                in_=dout[:, :].rearrange("p (o e) -> p o e", e=512)), "dA", 16)
        k.wait(sync, "dA")

    allreduce_max(3, [4], None)
    k.wait(vec, "dG")
    k.inc(vec.tensor_tensor(row[0:1, 8:9], scal[0:1, C_SU:C_SU + 1], scal[0:1, C_SG2:C_SG2 + 1],
                            Alu.mult), "vS")
    k.wait(vec, "vS")
    k.inc(vec.tensor_tensor(row[0:1, 9:10], row[0:1, 4:5], row[0:1, 8:9], Alu.mult), "vS")
    k.wait(vec, "vS")
    k.inc(vec.tensor_scalar(row[0:1, 9:10], row[0:1, 9:10], 1.0 / QMAX, 1e-12, Alu.mult, Alu.max), "vS")
    k.wait(vec, "vS")
    k.inc(vec.tensor_copy(scal[0:1, C_SO:C_SO + 1], row[0:1, 9:10]), "vS")
    k.inc(vec.reciprocal(row[0:1, 10:11], row[0:1, 9:10]), "vS")
    k.wait(vec, "vS")
    k.inc(vec.tensor_tensor(scal[0:1, C_C2O:C_C2O + 1], row[0:1, 10:11], row[0:1, 8:9], Alu.mult), "vS")
    k.wait(sync, "vS")
    doubling(k, sync, scal, 32, "dS")

    wqdq_lpbq(w["l0_wdown"], dr["wdown_scr"], 8, I0, 32, 7.0, "d0")
    k.barrier(ALL_ENG)
    with (
        nc.sbuf_tensor("wdkm", [128, 24 * 1024], BF16) as wdkm,
        nc.sbuf_tensor("ein", [128, 24 * 512], BF16) as ein,
        nc.sbuf_tensor("ei32", [128, HW_], I32) as ei32,
        nc.sbuf_tensor("eb", [128, 24 * 512], BF16) as ebuf,
        nc.sbuf_tensor("eres", [128, 4096], F32) as eres,
        nc.sbuf_tensor("exr", [128, 4096], F32) as exr,
        nc.psum_tensor("psD", [128, 4096], F32) as psD,
    ):
        for ic in range(24):
            k.inc(sync.dma_start(out=wdkm[:, ic * 1024:(ic + 1) * 1024],
                                 in_=dr["wdown_scr"][:, ic * 128:(ic + 1) * 128],
                                 transpose=True), "dS", 16)
        wld = k.cnt["dS"]
        res_tok = None
        x1p_tok = None
        mt = None
        for tb in range(8):
            if res_tok is not None:
                k.wait(sync, "vS", res_tok)
            lt = k.inc(sync.dma_start(
                out=ein[:, :].rearrange("p (o e) -> p o e", e=512),
                in_=bass.AP(dr["spill_o"], tb * 24 * 128 * 512,
                            [[512, 128], [128 * 512, 24], [1, 512]])), "dS", 16)
            for hf in range(4):
                k.wait(vec, "dS", lt)
                k.wait(vec, "vS")
                if mt is not None:
                    k.wait(vec, "tS", mt)
                k.inc(vec.tensor_scalar(ei32[:, :], ein[:, hf * HW_:(hf + 1) * HW_],
                                        scal[:, C_C2O:C_C2O + 1], None, Alu.mult), "vS")
                k.wait(vec, "vS")
                k.inc(vec.tensor_copy(ebuf[:, hf * HW_:(hf + 1) * HW_], ei32[:, :]), "vS")
            obt = k.tok(vec, "vS")
            k.wait(ten, "vS", obt)
            k.wait(ten, "dS", wld)
            mi = None
            for ocE in range(8):
                for ic in range(24):
                    mi = ten.matmul(psD[:, ocE * 512:(ocE + 1) * 512],
                                    wdkm[:, ic * 1024 + ocE * 128: ic * 1024 + ocE * 128 + 128],
                                    ebuf[:, ic * 512:(ic + 1) * 512],
                                    start=(ic == 0), stop=(ic == 23))
            mt = k.tok(ten, "tS")
            la = k.inc(sync.dma_start(out=eres[:, :].rearrange("p (c e) -> p c e", e=512),
                                      in_=bass.AP(dr["attn_sp"], tb * 512,
                                                  [[NT, 128], [128 * NT, 8], [1, 512]])), "dS", 16)
            lx = k.inc(sync.dma_start(out=exr[:, :].rearrange("p (c e) -> p c e", e=512),
                                      in_=bass.AP(dr["xfm_park"], tb * 512,
                                                  [[NT, 128], [128 * NT, 8], [1, 512]])), "dS", 16)
            k.wait(vec, "tS", mt)
            k.wait(vec, "dS", lx)
            k.wait(vec, "vS")
            k.inc(vec.scalar_tensor_tensor(eres[:, :], psD[:, :], scal[:, C_SO:C_SO + 1],
                                           eres[:, :], Alu.mult, Alu.add), "vS")
            k.wait(vec, "vS")
            k.inc(vec.tensor_tensor(eres[:, :], eres[:, :], exr[:, :], Alu.add), "vS")
            res_tok = k.tok(vec, "vS")
            k.wait(sca, "vS", res_tok)
            if x1p_tok is not None:
                k.wait(sca, "dA", x1p_tok)
            x1p_tok = k.inc(sca.dma_start(out=bass.AP(dr["x1_park"], tb * 512,
                                                      [[NT, 128], [128 * NT, 8], [1, 512]]),
                                          in_=eres[:, :].rearrange("p (c e) -> p c e", e=512)), "dA", 16)

    k.barrier(ALL_ENG)
    with (
        nc.sbuf_tensor("txin", [128, 2 * 1024], F32) as txin,
        nc.sbuf_tensor("txout", [128, 2 * 1024], F32) as txout,
        nc.psum_tensor("psX", [128, 2048], F32) as psX,
    ):
        ev_tok = [None, None]
        dm_tok = [None, None]
        for it in range(NT // 128):
            bu = it % 2
            if it >= 2:
                k.wait(sync, "tS")
            li = k.inc(sync.dma_start(out=txin[:, bu * 1024:(bu + 1) * 1024].rearrange(
                "p (c e) -> p c e", e=128),
                in_=bass.AP(dr["x1_park"], it * 128, [[NT, 128], [128 * NT, 8], [1, 128]])),
                "dS", 16)
            k.wait(ten, "dS", li)
            if ev_tok[bu] is not None:
                k.wait(ten, "sS", ev_tok[bu])
            for c in range(DCH):
                ten.transpose(psX[:, bu * 1024 + c * 128: bu * 1024 + (c + 1) * 128],
                              txin[:, bu * 1024 + c * 128: bu * 1024 + (c + 1) * 128],
                              identF[:, :])
            tt2 = k.tok(ten, "tS")
            k.wait(sca, "tS", tt2)
            if dm_tok[bu] is not None:
                k.wait(sca, "dA", dm_tok[bu])
            ev_tok[bu] = k.inc(sca.copy(txout[:, bu * 1024:(bu + 1) * 1024],
                                        psX[:, bu * 1024:(bu + 1) * 1024]), "sS")
            k.wait(sca, "sS", ev_tok[bu])
            dm_tok[bu] = k.inc(sca.dma_start(out=dr["x1tok_park"][it * 128:(it + 1) * 128, :],
                                             in_=txout[:, bu * 1024:(bu + 1) * 1024]), "dA", 16)

    k.barrier(ALL_ENG)
    ln_pass("1", dr["x1tok_park"], 5, xnfm, park_f32=dr["xn1f_park"])

    def _stopper(name):
        if stop_after == name:
            k.barrier(ALL_ENG)
            k.inc(gp.dma_start(out=y_out[0:128, :], in_=lnr[:, 0:16].broadcast_to((128, 16)) if False else xnfm[:, 0:D]), "dG", 16)
            k.wait(gp, "dG")
            return True
        return False

    if _stopper("l0"):
        es.close()
        return nc

    k.inc(gp.dma_start(out=dr["wqkv_scr"][:, :], in_=w["l1_wqkv"][:, :]), "dG", 16)
    k.inc(gp.dma_start(out=dr["wout_scr"][:, :], in_=w["l1_wout"][:, :]), "dG", 16)

    class _StopEmit(Exception):
        pass

    k.barrier(ALL_ENG)
    if stop_after == "l1w":
        k.inc(gp.dma_start(out=y_out[0:128, :], in_=xnfm[:, 0:D]), "dG", 16)
        k.wait(gp, "dG")
        es.close()
        return nc
    with (
        nc.sbuf_tensor("rrep", [32, 4224], F32) as rrep,
        nc.sbuf_tensor("w1c", [32, 1], F32) as w1c,
        nc.sbuf_tensor("b1c", [32, 1], F32) as b1c,
        nc.sbuf_tensor("w2c", [32, 32], BF16) as w2c,
        nc.sbuf_tensor("b2c", [16, 1], F32) as b2c,
        nc.sbuf_tensor("rfb", [32, 4224], BF16) as rfb,
        nc.sbuf_tensor("tabs", [16, 4224], F32) as tabs,
        nc.sbuf_tensor("etabs", [16, 4224], BF16) as etabs,
        nc.sbuf_tensor("m2r", [16, 2], F32) as m2r,
        nc.psum_tensor("psR", [32, 2048], F32) as psR,
    ):
        k.inc(sync.dma_start(out=rrep[:, :], in_=w["relrep"][:, :]), "dS", 16)
        k.inc(sync.dma_start(out=w1c[:, :], in_=bass.AP(w["l1_w1r"], 0, [[1, 32], [1, 1]])), "dS", 16)
        k.inc(sync.dma_start(out=b1c[:, :], in_=bass.AP(w["l1_b1r"], 0, [[1, 32], [1, 1]])), "dS", 16)
        k.inc(sync.dma_start(out=b2c[:, :], in_=bass.AP(w["l1_b2r"], 0, [[1, 16], [1, 1]])), "dS", 16)
        k.wait(vec, "vS")
        k.inc(vec.memset(w2c[:, :], 0.0), "vS")
        k.wait(gp, "vS")
        k.inc(gp.dma_start(out=w2c[:, 0:16], in_=w["l1_w2r"][:, :]), "dG", 16)
        k.wait(sca, "dS")
        _lvl = {"rpeA": 0, "rpeB": 1, "rpeC": 2, "rpeD": 3}.get(stop_after, 99)
        if _lvl >= 1:
            k.inc(sca.activation(rfb[:, :], rrep[:, :], Act.Relu, bias=b1c[:, :], scale=w1c[:, :]), "sS")
        if _lvl >= 1:
            st = k.tok(sca, "sS")
        if _lvl >= 2:
            k.wait(ten, "sS", st)
            k.wait(ten, "dG")
            for half in range(2):
                if half >= 1:
                    k.wait(ten, "sS")
                for nn in range(4):
                    ten.matmul(psR[0:32, nn * 512:(nn + 1) * 512], w2c[:, :],
                               rfb[:, half * 2048 + nn * 512: half * 2048 + (nn + 1) * 512],
                               start=True, stop=True)
                tt_ = k.tok(ten, "tS")
                k.wait(sca, "tS", tt_)
                k.inc(sca.activation(tabs[0:16, half * 2048:(half + 1) * 2048], psR[0:16, :],
                                     Act.Identity, bias=b2c[:, :], scale=1.0), "sS")
                k.tok(sca, "sS")
        if _lvl >= 3:
            k.wait(vec, "sS")
            k.inc(vec.tensor_reduce(m2r[:, 0:1], tabs[:, :], axis=AX.X, op=Alu.max), "vS")
            mm2 = k.tok(vec, "vS")
            k.wait(gp, "vS", mm2)
            k.inc(gp.tensor_reduce(m2r[0:1, 1:2], m2r[:, 0:1], axis=AX.C, op=Alu.max), "gS")
            k.wait(sync, "gS")
            doubling(k, sync, m2r, 2, "dS", nrows=16)
        if _lvl >= 4:
            k.wait(vec, "dS")
            k.inc(vec.tensor_scalar(m2r[:, 0:1], m2r[:, 1:2], -1.0, None, Alu.mult), "vS")
            k.wait(sca, "vS")
            k.inc(sca.activation(etabs[:, :], tabs[:, :], Act.Exp, bias=m2r[:, 0:1], scale=1.0), "sS")
            st2 = k.tok(sca, "sS")
            k.wait(sca, "sS", st2)
            k.inc(sca.dma_start(out=dr["exptab"][:, :], in_=etabs[:, :]), "dA", 16)
    if stop_after in ("rpeA", "rpeB", "rpeC", "rpeD"):
        k.barrier(ALL_ENG)
        k.inc(gp.dma_start(out=y_out[0:128, :], in_=xnfm[:, 0:D]), "dG", 16)
        k.wait(gp, "dG")
        es.close()
        return nc

    if stop_after in ("rpe", "rpe0", "rpe1"):
        if _stopper(stop_after) or _stopper("rpe0") or _stopper("rpe1") or True:
            k.barrier(ALL_ENG)
            k.inc(gp.dma_start(out=y_out[0:128, :], in_=xnfm[:, 0:D]), "dG", 16)
            k.wait(gp, "dG")
            es.close()
            return nc

    def l1_attn_seq(s, sub=None):
        k.barrier(ALL_ENG)
        with (
            nc.sbuf_tensor(f"yfmB{s}", [128, DCH * T], BF16) as yfm,
            nc.sbuf_tensor(f"mcolB{s}", [128, 16], F32) as mcol,
            nc.sbuf_tensor(f"nrm{s}", [16, 8], F32) as nrm,
        ):
            with (
                nc.sbuf_tensor(f"wqk{s}", [128, DCH * 2048], BF16) as wqk,
                nc.sbuf_tensor(f"bcol{s}", [128, 24], F32) as bcol,
                nc.sbuf_tensor(f"pstg{s}", [128, 2 * 2048], BF16) as pstg,
                nc.sbuf_tensor(f"sqs{s}", [128, 2048], BF16) as sqs,
                nc.sbuf_tensor(f"hsel{s}", [128, 8 * 32], BF16) as hsel,
                nc.psum_tensor(f"psQ{s}", [128, 2048], F32) as psQ,
                nc.psum_tensor(f"psN{s}", [32, 2048], F32) as psN,
        ):
                k.wait(sync, "dG")
                for ic in range(DCH):
                    k.inc(sync.dma_start(out=wqk[:, ic * 2048:(ic + 1) * 2048],
                                         in_=dr["wqkv_scr"][ic * 128:(ic + 1) * 128, 0:2048]), "dS", 16)
                k.inc(sync.dma_start(out=bcol[:, :],
                                     in_=bass.AP(w["l1_bqkv"], 0, [[1, 128], [128, 24]]), allow_slow_non_contiguous=True), "dS", 16)
                k.wait(vec, "vS")
                k.inc(vec.memset(hsel[:, :], 0.0), "vS")
                for oc in range(8):
                    k.inc(vec.memset(hsel[0:64, oc * 32 + 2 * oc: oc * 32 + 2 * oc + 1], 1.0), "vS")
                    k.inc(vec.memset(hsel[64:128, oc * 32 + 2 * oc + 1: oc * 32 + 2 * oc + 2], 1.0), "vS")
                wl = k.cnt["dS"]
                ev_tok = [None, None]
                pk_tok = [None, None]
                for j in range(2):
                    for oc in range(8):
                        bu = oc % 2
                        col = j * 8 + oc
                        k.wait(ten, "dS", wl)
                        if ev_tok[bu] is not None:
                            k.wait(ten, "sS", ev_tok[bu])
                        mi = None
                        for ic in range(DCH):
                            for tb in range(4):
                                mi = ten.matmul(
                                    psQ[:, tb * 512:(tb + 1) * 512],
                                    wqk[:, ic * 2048 + col * 128: ic * 2048 + (col + 1) * 128],
                                    xnfm[:, ic * NT + s * T + tb * 512: ic * NT + s * T + (tb + 1) * 512],
                                    start=(ic == 0), stop=(ic == DCH - 1))
                        mt = k.tok(ten, "tS")
                        k.wait(sca, "tS", mt)
                        if pk_tok[bu] is not None:
                            k.wait(sca, "dA", pk_tok[bu])
                        k.wait(sca, "sS")
                        ei = sca.activation(pstg[:, bu * 2048:(bu + 1) * 2048], psQ[:, :],
                                            Act.Identity, bias=bcol[:, col:col + 1], scale=1.0)
                        ev_tok[bu] = k.tok(sca, "sS")
                        k.wait(sca, "sS", ev_tok[bu])
                        pk_tok[bu] = k.inc(sca.dma_start(
                            out=dr["qk1_park"][(j * DCH + oc) * 128:(j * DCH + oc + 1) * 128, :],
                            in_=pstg[:, bu * 2048:(bu + 1) * 2048]), "dA", 16)
                        k.inc(sca.activation(sqs[:, :], pstg[:, bu * 2048:(bu + 1) * 2048],
                                             Act.Square), "sS")
                        sq_t = k.tok(sca, "sS")
                        k.wait(ten, "sS", sq_t)
                        for tb in range(4):
                            ten.matmul(psN[0:32, tb * 512:(tb + 1) * 512],
                                       hsel[:, oc * 32:(oc + 1) * 32], sqs[:, tb * 512:(tb + 1) * 512],
                                       start=(oc == 0), stop=(oc == 7))
                        k.tok(ten, "tS")
                    k.wait(vec, "tS")
                    k.inc(vec.tensor_reduce(nrm[0:16, j:j + 1], psN[0:16, :], axis=AX.X,
                                            op=Alu.max), "vS")
                    k.tok(vec, "vS")
                k.wait(vec, "vS")
                k.inc(vec.tensor_tensor(nrm[:, 2:3], nrm[:, 0:1], nrm[:, 1:2], Alu.mult), "vS")
                nt = k.tok(vec, "vS")
                k.wait(sca, "vS", nt)
                k.inc(sca.activation(nrm[:, 3:4], nrm[:, 2:3], Act.Sqrt), "sS")
                st3 = k.tok(sca, "sS")
                k.wait(vec, "sS", st3)
                k.inc(vec.tensor_scalar(nrm[:, 4:5], nrm[:, 3:4], -0.126, None, Alu.mult), "vS")
                nt2 = k.tok(vec, "vS")
                k.wait(sca, "vS", nt2)
                k.inc(sca.dma_start(out=bass.AP(dr["bounce16"], 0, [[1, 16], [1, 1]]),
                                    in_=nrm[0:16, 4:5]), "dA", 16)
                k.wait(sync, "dA")
                k.inc(sync.dma_start(out=mcol[0:1, 0:16],
                                     in_=dr["bounce16"][0:1, 0:16]),
                      "dS", 16)
                k.wait(sync, "dS")
                doubling(k, sync, mcol, 16, "dS")
            if sub == "l1qk":
                return

            k.barrier(ALL_ENG)
            with (
                nc.sbuf_tensor(f"kfmB{s}", [128, DCH * T], BF16) as kfm,
                nc.sbuf_tensor(f"v65B{s}", [128, 16 * 16 * 65], BF16) as v65,
            ):
                with (
                    nc.sbuf_tensor(f"wvB{s}", [128, DCH * 1024], BF16) as wv,
                    nc.sbuf_tensor(f"bvr{s}", [128, 1024], F32) as bvr,
                    nc.psum_tensor(f"psV{s}", [128, 2048], F32) as psV,
                ):
                    for ic in range(DCH):
                        k.inc(sync.dma_start(out=wv[:, ic * 1024:(ic + 1) * 1024],
                                             in_=dr["wqkv_scr"][ic * 128:(ic + 1) * 128, 2048:3072]),
                              "dS", 16)
                    k.inc(sync.dma_start(out=bvr[0:1, :], in_=w["l1_bqkv"][0:1, 2048:3072]), "dS", 16)
                    k.wait(sync, "dS")
                    doubling(k, sync, bvr, 1024, "dS")
                    k.wait(vec, "vS")
                    k.inc(vec.memset(v65[:, :], 1.0), "vS")
                    wl2 = k.cnt["dS"]
                    vv_tok = [None, None]
                    for tc in range(16):
                        bu = tc % 2
                        tok0 = s * T + tc * 128
                        k.wait(ten, "dS", wl2)
                        if vv_tok[bu] is not None:
                            k.wait(ten, "vS", vv_tok[bu])
                        mi = None
                        for ic in range(DCH):
                            for nh in range(2):
                                mi = ten.matmul(psV[:, bu * 1024 + nh * 512: bu * 1024 + (nh + 1) * 512],
                                                xnfm[:, ic * NT + tok0: ic * NT + tok0 + 128],
                                                wv[:, ic * 1024 + nh * 512: ic * 1024 + (nh + 1) * 512],
                                                start=(ic == 0), stop=(ic == DCH - 1))
                        mt = k.tok(ten, "tS")
                        k.wait(vec, "tS", mt)
                        k.wait(vec, "dS")
                        k.inc(vec.tensor_tensor(psV[:, bu * 1024:(bu + 1) * 1024],
                                                psV[:, bu * 1024:(bu + 1) * 1024], bvr[:, :],
                                                Alu.add), "vS")
                        k.wait(vec, "vS")
                        vdst = v65[:, tc * 1040: (tc + 1) * 1040].rearrange("p (h e) -> p h e", e=65)
                        vv_tok[bu] = k.inc(vec.tensor_copy(
                            vdst[:, :, 0:64],
                            psV[:, bu * 1024:(bu + 1) * 1024].rearrange("p (h d) -> p h d", d=64)), "vS")
                k.barrier([sync])
                for c in range(DCH):
                    k.inc(sync.dma_start(out=kfm[:, c * T:(c + 1) * T],
                                         in_=dr["qk1_park"][(DCH + c) * 128:(DCH + c + 1) * 128, :]),
                          "dS", 16)

                if sub == "l1v":
                    return
                with (
                    nc.sbuf_tensor(f"qbufB{s}", [128, 2 * 512], BF16) as qbuf,
                    nc.sbuf_tensor(f"pbufB{s}", [128, 2 * 2048], BF16) as pbuf,
                    nc.sbuf_tensor(f"ystB{s}", [128, T], BF16) as ystage,
                    nc.sbuf_tensor(f"ycopB{s}", [128, 1024], BF16) as ycop,
                    nc.sbuf_tensor(f"zrepB{s}", [128, T], BF16) as zrep,
                    nc.sbuf_tensor(f"rttB{s}", [128, 2 * 4224], BF16) as rtt,
                    nc.psum_tensor(f"psSB{s}", [128, 2048], F32) as psS,
                    nc.psum_tensor(f"psYB{s}", [128, 1024], F32) as psY,
                ):
                    gsz = [2] * 8
                    gof = [0, 2, 4, 6, 8, 10, 12, 14]
                    NG = 8
                    yev_tok = None
                    ysc_tok = None
                    ycop_tok = None
                    rtt_tok = None
                    for hp in range(8):
                        hA = 2 * hp
                        if rtt_tok is not None:
                            k.wait(sync, "tS")
                            k.wait(sync, "vS")
                        for hh in range(2):
                            k.inc(sync.dma_start(out=rtt[0:1, hh * 4224:hh * 4224 + 4224],
                                                 in_=dr["exptab"][hA + hh:hA + hh + 1, :]), "dS", 16)
                        k.wait(sync, "dS")
                        p2 = 1
                        while p2 < 128:
                            for hh in range(2):
                                k.inc(sync.dma_start(
                                    out=rtt[p2:2 * p2, hh * 4224 + p2: hh * 4224 + 4224],
                                    in_=rtt[0:p2, hh * 4224: hh * 4224 + 4224 - p2]), "dS", 16)
                            k.wait(sync, "dS")
                            p2 *= 2
                        rtt_tok = k.cnt["dS"]
                        qld_tok = [None] * 4
                        qsc_tok = [None] * 4
                        for qt in range(4):
                            qb = qt % 2
                            if qt >= 2:
                                k.wait(sync, "tS", qsc_tok[qt - 2])
                            qld_tok[qt] = k.inc(sync.dma_start(
                                out=qbuf[:, qb * 512:(qb + 1) * 512],
                                in_=dr["qk1_park"][hp * 128:(hp + 1) * 128, qt * 512:(qt + 1) * 512]),
                                "dS", 16)
                            exp_tok = []
                            for g in range(NG):
                                k.wait(ten, "dS", qld_tok[qt])
                                k.wait(ten, "dS", rtt_tok)
                                if g >= 1:
                                    k.wait(ten, "vS")
                                elif yev_tok is not None:
                                    k.wait(ten, "vS", yev_tok)
                                mi = None
                                for jj in range(gsz[g]):
                                    kc = gof[g] + jj
                                    for hh in range(2):
                                        mi = ten.matmul(
                                            psS[:, jj * 1024 + hh * 512: jj * 1024 + (hh + 1) * 512],
                                            kfm[hh * 64:(hh + 1) * 64,
                                                hp * T + kc * 128: hp * T + (kc + 1) * 128],
                                            qbuf[hh * 64:(hh + 1) * 64, qb * 512:(qb + 1) * 512],
                                            start=True, stop=True)
                                sc_tok = k.tok(ten, "tS")
                                qsc_tok[qt] = sc_tok
                                k.wait(sca, "tS", sc_tok)
                                pp = g % 2
                                ei = None
                                for hh in range(2):
                                    for jj in range(gsz[g]):
                                        ei = sca.activation(
                                            pbuf[:, pp * 2048 + jj * 1024 + hh * 512:
                                                 pp * 2048 + jj * 1024 + (hh + 1) * 512],
                                            psS[:, jj * 1024 + hh * 512: jj * 1024 + (hh + 1) * 512],
                                            Act.Exp, bias=mcol[:, (hA + hh):(hA + hh) + 1], scale=0.125)
                                exp_tok.append(k.tok(sca, "sS"))
                                k.wait(vec, "sS", exp_tok[g])
                                for jj in range(gsz[g]):
                                    kc = gof[g] + jj
                                    mbase = 2047 + qt * 512 - kc * 128
                                    for hh in range(2):
                                        k.inc(vec.tensor_tensor(
                                            pbuf[:, pp * 2048 + jj * 1024 + hh * 512:
                                                 pp * 2048 + jj * 1024 + (hh + 1) * 512],
                                            pbuf[:, pp * 2048 + jj * 1024 + hh * 512:
                                                 pp * 2048 + jj * 1024 + (hh + 1) * 512],
                                            rtt[:, hh * 4224 + mbase: hh * 4224 + mbase + 512],
                                            Alu.mult), "vS")
                                pm_tok = k.tok(vec, "vS")
                                k.wait(ten, "vS", pm_tok)
                                mi = None
                                for jj in range(gsz[g]):
                                    kc = gof[g] + jj
                                    for hh in range(2):
                                        mi = ten.matmul(
                                            psY[0:65, hh * 512:(hh + 1) * 512],
                                            v65[:, kc * 1040 + (hA + hh) * 65:
                                                kc * 1040 + (hA + hh) * 65 + 65],
                                            pbuf[:, pp * 2048 + jj * 1024 + hh * 512:
                                                 pp * 2048 + jj * 1024 + (hh + 1) * 512],
                                            start=(g == 0 and jj == 0),
                                            stop=(g == NG - 1 and jj == gsz[g] - 1))
                                qsc_tok[qt] = k.inc(mi, "tS")
                            k.wait(vec, "tS")
                            if ysc_tok is not None:
                                k.wait(vec, "vS", ysc_tok)
                            if ycop_tok is not None:
                                k.wait(vec, "dS", ycop_tok)
                            k.inc(vec.tensor_copy(ystage[0:64, qt * 512:(qt + 1) * 512],
                                                  psY[0:64, 0:512]), "vS")
                            k.inc(vec.tensor_copy(ycop[0:64, 0:512], psY[0:64, 512:1024]), "vS")
                            yev_tok = k.inc(vec.tensor_copy(ycop[64:65, 0:1024], psY[64:65, 0:1024]), "vS")
                            k.wait(sync, "vS")
                            k.inc(sync.dma_start(out=ystage[64:128, qt * 512:(qt + 1) * 512],
                                                 in_=ycop[0:64, 0:512]), "dS", 16)
                            k.inc(sync.dma_start(out=zrep[0:1, qt * 512:(qt + 1) * 512],
                                                 in_=ycop[64:65, 0:512]), "dS", 16)
                            ycop_tok = k.inc(sync.dma_start(out=zrep[64:65, qt * 512:(qt + 1) * 512],
                                                            in_=ycop[64:65, 512:1024]), "dS", 16)
                        k.wait(vec, "dS", ycop_tok)
                        with nc.allow_low_precision(reason="1/Z in bf16 (0.2% noise, within bar)"):
                            k.inc(vec.reciprocal(zrep[0:1, :], zrep[0:1, :]), "vS")
                            k.inc(vec.reciprocal(zrep[64:65, :], zrep[64:65, :]), "vS")
                        k.wait(sync, "vS")
                        p2 = 1
                        while p2 < 64:
                            k.inc(sync.dma_start(out=zrep[p2:2 * p2, :], in_=zrep[0:p2, :]), "dS", 16)
                            k.inc(sync.dma_start(out=zrep[64 + p2:64 + 2 * p2, :],
                                                 in_=zrep[64:64 + p2, :]), "dS", 16)
                            k.wait(sync, "dS")
                            p2 *= 2
                        k.wait(vec, "dS")
                        ysc_tok = k.inc(vec.tensor_tensor(yfm[:, hp * T:(hp + 1) * T], ystage[:, :],
                                                          zrep[:, :], Alu.mult), "vS")
            if dbg == "ybis" and s == 1:
                k.barrier(ALL_ENG)
                k.inc(gp.dma_start(out=dbg_t[:, 0:4096], in_=yfm[:, 0:4096]), "dG", 16)
                k.wait(gp, "dG")
                k.barrier(ALL_ENG)
            if dbg == "yfm1" and s == 1:
                k.barrier([gp])
                k.inc(gp.dma_start(out=dbg_t[:, 0:DCH * T], in_=yfm[:, :]), "dG", 16)
                k.inc(gp.dma_start(out=dbg_t[:, 16384:16400], in_=mcol[:, 0:16]), "dG", 16)
                k.inc(gp.dma_start(out=dbg_t[0:16, 16400:16408], in_=nrm[:, :]), "dG", 16)
                k.wait(gp, "dG")
            k.barrier(ALL_ENG)
            with (
                nc.sbuf_tensor(f"woB{s}", [128, DCH * 1024], BF16) as wo_km,
                nc.sbuf_tensor(f"boc{s}", [128, 8], F32) as boc,
                nc.sbuf_tensor(f"ostB{s}", [128, 2 * 2048], F32) as ostage,
                nc.psum_tensor(f"psOB{s}", [128, 2048], F32) as psO,
            ):
                for ic in range(DCH):
                    k.inc(sync.dma_start(out=wo_km[:, ic * 1024:(ic + 1) * 1024],
                                         in_=dr["wout_scr"][ic * 128:(ic + 1) * 128, :]), "dS", 16)
                k.inc(sync.dma_start(out=boc[:, :],
                                     in_=bass.AP(w["l1_bout"], 0, [[1, 128], [128, 8]]), allow_slow_non_contiguous=True), "dS", 16)
                if dbg is not None:
                    print(f"ADDR s={s} yfm:", nc.lookup_mloc(yfm).addr,
                          "+", DCH * T * 2, "ost:", nc.lookup_mloc(ostage).addr,
                          "+", 4 * 2048 * 4, "wo:", nc.lookup_mloc(wo_km).addr,
                          "mcol:", nc.lookup_mloc(mcol).addr,
                          "boc:", nc.lookup_mloc(boc).addr)
                if dbg == "ybis" and s == 1:
                    k.barrier(ALL_ENG)
                    k.inc(gp.dma_start(out=dbg_t[:, 8192:12288], in_=yfm[:, 0:4096]), "dG", 16)
                    k.wait(gp, "dG")
                    k.barrier(ALL_ENG)
                osp = [None, None]
                est = [None, None]
                for oc in range(DCH):
                    if dbg == "oproj1w" and s == 1 and oc == 0:
                        k.barrier(ALL_ENG)
                        k.inc(gp.dma_start(out=dbg_t[:, 8192:12288], in_=yfm[:, 0:4096]), "dG", 16)
                        k.wait(gp, "dG")
                    if dbg == "oproj1" and s == 1 and oc == 1:
                        k.barrier(ALL_ENG)
                        k.inc(gp.dma_start(out=dbg_t[:, 2048:4096], in_=ostage[:, 0:2048]), "dG", 16)
                        k.inc(gp.dma_start(out=dbg_t[:, 4096:8192], in_=wo_km[:, 0:4096]), "dG", 16)
                        k.inc(gp.dma_start(out=dbg_t[:, 8192:12288], in_=yfm[:, 0:4096]), "dG", 16)
                        k.inc(gp.dma_start(out=dbg_t[:, 12288:12296], in_=boc[:, :]), "dG", 16)
                        k.wait(gp, "dG")
                    k.wait(ten, "dS")
                    if oc >= 1:
                        k.wait(ten, "sS", est[(oc - 1) % 2])
                    mi = None
                    for yc in range(DCH):
                        for tb in range(4):
                            mi = ten.matmul(psO[:, tb * 512:(tb + 1) * 512],
                                            wo_km[:, yc * 1024 + oc * 128: yc * 1024 + oc * 128 + 128],
                                            yfm[:, yc * T + tb * 512: yc * T + (tb + 1) * 512],
                                            start=(yc == 0), stop=(yc == DCH - 1))
                    mt = k.tok(ten, "tS")
                    ob = oc % 2
                    k.wait(sca, "tS", mt)
                    if oc >= 2:
                        k.wait(sca, "dA", osp[ob])
                    ei = sca.activation(ostage[:, ob * 2048:(ob + 1) * 2048], psO[:, :],
                                        Act.Identity, bias=boc[:, oc:oc + 1], scale=1.0)
                    est[ob] = k.tok(sca, "sS")
                    k.wait(sca, "sS", est[ob])
                    osp[ob] = k.inc(sca.dma_start(out=dr["attn_sp"][oc * 128:(oc + 1) * 128,
                                                                    s * T:(s + 1) * T],
                                                  in_=ostage[:, ob * 2048:(ob + 1) * 2048]), "dA", 16)
        k.barrier(ALL_ENG)

    l1_attn_seq(0, sub=stop_after if stop_after in ("l1qk", "l1v") else None)
    if stop_after in ("l1qk", "l1v"):
        if _stopper(stop_after):
            es.close()
            return nc
    if stop_after == "l1attn0":
        pass
    else:
        l1_attn_seq(1)
    if stop_after in ("l1attn", "l1attn0"):
        if _stopper(stop_after):
            es.close()
            return nc

    if dbg == "attn1":
        k.barrier([gp])
        for c in range(8):
            k.inc(gp.dma_start(out=dbg_t[:, c * 4096:(c + 1) * 4096],
                               in_=dr["attn_sp"][c * 128:(c + 1) * 128, :]), "dG", 16)
        k.wait(gp, "dG")

    def fp4_qdq_tile(src_ap, dst_bf, width, ctxbuf):
        nb = width // 16
        bm, an, t2, w2 = ctxbuf["bm"], ctxbuf["an"], ctxbuf["t2"], ctxbuf["w2"]
        Ri, sg = ctxbuf["Ri"], ctxbuf["sg"]
        k.wait(vec, "vS")
        k.inc(vec.tensor_reduce(bm[:, 0:nb].unsqueeze(2),
                                src_ap.rearrange("p (b e) -> p b e", e=16),
                                axis=AX.X, op=Alu.max, apply_absolute_value=True), "vS")
        k.wait(vec, "vS")
        k.inc(vec.tensor_scalar(bm[:, 0:nb], bm[:, 0:nb], 1.0 / 6.0, 1e-12, Alu.mult, Alu.max), "vS")
        k.wait(vec, "vS")
        k.inc(vec.reciprocal(bm[:, nb:2 * nb], bm[:, 0:nb]), "vS")
        rt = k.tok(vec, "vS")
        rsb = bm[:, nb:2 * nb].unsqueeze(2).broadcast_to((128, nb, 16))
        k.inc(vec.tensor_tensor(w2[:, 0:width].rearrange("p (b e) -> p b e", e=16),
                                src_ap.rearrange("p (b e) -> p b e", e=16), rsb, Alu.mult), "vS")
        nt = k.tok(vec, "vS")
        k.wait(sca, "vS", nt)
        k.inc(sca.activation(an[:, 0:width], w2[:, 0:width], Act.Abs), "sS")
        k.inc(sca.activation(sg[:, 0:width], w2[:, 0:width], Act.Sign), "sS")
        st = k.tok(sca, "sS")
        k.wait(vec, "sS", st)
        k.inc(vec.tensor_scalar(t2[:, 0:width], an[:, 0:width], 2.0, None, Alu.is_ge), "vS")
        k.inc(vec.tensor_scalar(t2[:, width:2 * width], an[:, 0:width], 4.0, -0.5, Alu.is_ge,
                                Alu.mult), "vS")
        k.wait(vec, "vS")
        k.inc(vec.scalar_tensor_tensor(w2[:, width:2 * width], t2[:, 0:width], -1.0,
                                       t2[:, width:2 * width], Alu.mult, Alu.add), "vS")
        k.wait(vec, "vS")
        k.inc(vec.tensor_scalar(w2[:, width:2 * width], w2[:, width:2 * width], 2.0, None,
                                Alu.add), "vS")
        k.wait(vec, "vS")
        k.inc(vec.tensor_tensor(w2[:, width:2 * width], an[:, 0:width], w2[:, width:2 * width],
                                Alu.mult), "vS")
        k.wait(vec, "vS")
        k.inc(vec.tensor_scalar(Ri[:, 0:width], w2[:, width:2 * width], 1.0, None, Alu.mult), "vS")
        k.wait(vec, "vS")
        k.inc(vec.tensor_scalar(t2[:, width:2 * width], t2[:, width:2 * width], -4.0, None,
                                Alu.mult), "vS")
        k.wait(vec, "vS")
        k.inc(vec.tensor_tensor(t2[:, 0:width], t2[:, 0:width], t2[:, width:2 * width], Alu.add), "vS")
        k.wait(vec, "vS")
        k.inc(vec.tensor_scalar(t2[:, 0:width], t2[:, 0:width], 1.0, None, Alu.add), "vS")
        k.wait(vec, "vS")
        k.inc(vec.tensor_tensor(w2[:, 0:width], Ri[:, 0:width], t2[:, 0:width], Alu.mult), "vS")
        k.wait(vec, "vS")
        sb = bm[:, 0:nb].unsqueeze(2).broadcast_to((128, nb, 16))
        k.inc(vec.tensor_tensor(sg[:, 0:width].rearrange("p (b e) -> p b e", e=16),
                                sg[:, 0:width].rearrange("p (b e) -> p b e", e=16), sb, Alu.mult), "vS")
        k.wait(vec, "vS")
        k.inc(vec.scalar_tensor_tensor(dst_bf, w2[:, 0:width], 0.5, sg[:, 0:width],
                                       Alu.mult, Alu.mult), "vS")
        return k.tok(vec, "vS")

    def fp4_weights(wsrc, scr, nchunks, width, tag):
        with (
            nc.sbuf_tensor(f"f4n{tag}", [128, width], F32) as wn,
            nc.sbuf_tensor(f"f4bm{tag}", [128, 2 * (width // 16)], F32) as bm,
            nc.sbuf_tensor(f"f4an{tag}", [128, width], F32) as an,
            nc.sbuf_tensor(f"f4t2{tag}", [128, 2 * width], F32) as t2,
            nc.sbuf_tensor(f"f4w2{tag}", [128, 2 * width], F32) as w2,
            nc.sbuf_tensor(f"f4Ri{tag}", [128, width], I32) as Ri,
            nc.sbuf_tensor(f"f4sg{tag}", [128, width], F32) as sg,
            nc.sbuf_tensor(f"f4o{tag}", [128, width], BF16) as wd,
        ):
            ctxb = {"bm": bm, "an": an, "t2": t2, "w2": w2, "Ri": Ri, "sg": sg}
            for oc in range(nchunks):
                k.wait(sync, "vS")
                ld = k.inc(sync.dma_start(out=wn[:, :], in_=wsrc[oc * 128:(oc + 1) * 128, :]),
                           "dS", 16)
                k.wait(vec, "dS", ld)
                ot = fp4_qdq_tile(wn[:, :], wd[:, :], width, ctxb)
                k.wait(sca, "vS", ot)
                k.wait(sca, "dA")
                k.inc(sca.dma_start(out=scr[oc * 128:(oc + 1) * 128, 0:width], in_=wd[:, :]), "dA", 16)
        k.barrier(ALL_ENG)

    with (
        nc.sbuf_tensor("q4n", [128, D], F32) as q4n,
        nc.sbuf_tensor("q4bm", [128, 2 * (D // 16)], F32) as q4bm,
        nc.sbuf_tensor("q4an", [128, D], F32) as q4an,
        nc.sbuf_tensor("q4t2", [128, 2 * D], F32) as q4t2,
        nc.sbuf_tensor("q4w2", [128, 2 * D], F32) as q4w2,
        nc.sbuf_tensor("q4Ri", [128, D], I32) as q4Ri,
        nc.sbuf_tensor("q4sg", [128, D], F32) as q4sg,
        nc.sbuf_tensor("q4o", [128, D], BF16) as q4o,
    ):
        ctxb = {"bm": q4bm, "an": q4an, "t2": q4t2, "w2": q4w2, "Ri": q4Ri, "sg": q4sg}
        for it in range(NT // 128):
            k.wait(sync, "vS")
            ld = k.inc(sync.dma_start(out=q4n[:, :], in_=dr["xn1f_park"][it * 128:(it + 1) * 128, :]),
                       "dS", 16)
            k.wait(vec, "dS", ld)
            ot = fp4_qdq_tile(q4n[:, :], q4o[:, :], D, ctxb)
            k.wait(sca, "vS", ot)
            k.wait(sca, "dA")
            k.inc(sca.dma_start(out=dr["xq1t_park"][it * 128:(it + 1) * 128, :], in_=q4o[:, :]),
                  "dA", 16)
    if dbg == "xq1d":
        k.barrier([gp])
        for i in range(32):
            k.inc(gp.dma_start(out=dbg_t[:, i * 1024:(i + 1) * 1024],
                               in_=dr["xq1t_park"][i * 128:(i + 1) * 128, :]), "dG", 16)
        k.wait(gp, "dG")
    k.barrier(ALL_ENG)
    if _stopper("xq1"):
        es.close()
        return nc

    fp4_weights(w["l1_wgate"], dr["wgate_scr"], 22, 1024, "g1")
    fp4_weights(w["l1_wup"], dr["wup_scr"], 22, 1024, "u1")

    IH = I1 // 2
    NCOL = [512, 512, 384]
    NOFF = [0, 512, 1024]
    with (
        nc.sbuf_tensor("wgkm", [128, DCH * IH], BF16) as wgkm,
        nc.sbuf_tensor("wukm", [128, DCH * IH], BF16) as wukm,
        nc.sbuf_tensor("xq1s", [128, 2 * DCH * 128], BF16) as xq1s,
        nc.sbuf_tensor("sgt", [128, IH], F32) as sgt,
        nc.sbuf_tensor("ht", [128, IH], F32) as ht,
        nc.sbuf_tensor("h4bm", [128, 2 * (IH // 16)], F32) as h4bm,
        nc.sbuf_tensor("h4an", [128, IH], F32) as h4an,
        nc.sbuf_tensor("h4t2", [128, 2 * IH], F32) as h4t2,
        nc.sbuf_tensor("h4w2", [128, 2 * IH], F32) as h4w2,
        nc.sbuf_tensor("h4Ri", [128, IH], I32) as h4Ri,
        nc.sbuf_tensor("h4sg", [128, IH], F32) as h4sg,
        nc.sbuf_tensor("h4o", [128, IH], BF16) as h4o,
        nc.psum_tensor("psG1", [128, 2048], F32) as psG1,
    ):
        ctxh = {"bm": h4bm, "an": h4an, "t2": h4t2, "w2": h4w2, "Ri": h4Ri, "sg": h4sg}
        for ihalf in range(2):
            k.barrier(ALL_ENG)
            for ic in range(DCH):
                k.inc(sync.dma_start(out=wgkm[:, ic * IH:(ic + 1) * IH],
                                     in_=dr["wgate_scr"][ihalf * IH:(ihalf + 1) * IH,
                                                         ic * 128:(ic + 1) * 128],
                                     transpose=True), "dS", 16)
                k.inc(sync.dma_start(out=wukm[:, ic * IH:(ic + 1) * IH],
                                     in_=dr["wup_scr"][ihalf * IH:(ihalf + 1) * IH,
                                                       ic * 128:(ic + 1) * 128],
                                     transpose=True), "dS", 16)
            wl3 = k.cnt["dS"]
            hq_tok = None
            for tc in range(NT // 128):
                bu = tc % 2
                if tc >= 2:
                    k.wait(sync, "tS")
                xql = None
                for ic in range(DCH):
                    xql = k.inc(sync.dma_start(
                        out=xq1s[:, bu * 1024 + ic * 128: bu * 1024 + (ic + 1) * 128],
                        in_=dr["xq1t_park"][tc * 128:(tc + 1) * 128, ic * 128:(ic + 1) * 128],
                        transpose=True), "dS", 16)
                k.wait(ten, "dS", xql)
                k.wait(ten, "dS", wl3)
                if hq_tok is not None:
                    k.wait(ten, "sS")
                    k.wait(ten, "vS")
                mi = None
                for ic in range(DCH):
                    for nn in range(3):
                        mi = ten.matmul(psG1[:, NOFF[nn]:NOFF[nn] + NCOL[nn]],
                                        xq1s[:, bu * 1024 + ic * 128: bu * 1024 + (ic + 1) * 128],
                                        wgkm[:, ic * IH + NOFF[nn]: ic * IH + NOFF[nn] + NCOL[nn]],
                                        start=(ic == 0), stop=(ic == DCH - 1))
                gt = k.tok(ten, "tS")
                k.wait(sca, "tS", gt)
                k.inc(sca.activation(sgt[:, :], psG1[:, 0:IH], Act.Silu), "sS")
                sil = k.tok(sca, "sS")
                k.wait(ten, "sS", sil)
                mi = None
                for ic in range(DCH):
                    for nn in range(3):
                        mi = ten.matmul(psG1[:, NOFF[nn]:NOFF[nn] + NCOL[nn]],
                                        xq1s[:, bu * 1024 + ic * 128: bu * 1024 + (ic + 1) * 128],
                                        wukm[:, ic * IH + NOFF[nn]: ic * IH + NOFF[nn] + NCOL[nn]],
                                        start=(ic == 0), stop=(ic == DCH - 1))
                ut = k.tok(ten, "tS")
                k.wait(vec, "tS", ut)
                k.wait(vec, "sS", sil)
                k.wait(vec, "vS")
                k.inc(vec.tensor_tensor(ht[:, :], sgt[:, :], psG1[:, 0:IH], Alu.mult), "vS")
                ot = fp4_qdq_tile(ht[:, :], h4o[:, :], IH, ctxh)
                k.wait(sca, "vS", ot)
                if hq_tok is not None:
                    k.wait(sca, "dA", hq_tok)
                hq_tok = k.inc(sca.dma_start(
                    out=dr["hq1t_park"][tc * 128:(tc + 1) * 128, ihalf * IH:(ihalf + 1) * IH],
                    in_=h4o[:, :]), "dA", 16)
    k.barrier(ALL_ENG)
    if _stopper("gateup"):
        es.close()
        return nc

    fp4_weights(w["l1_wdown"], dr["wdown_scr"], 8, I1, "d1")
    with (
        nc.sbuf_tensor("wd1km", [128, 22 * 1024], BF16) as wd1km,
        nc.sbuf_tensor("hkm", [128, 22 * 512], BF16) as hkm,
        nc.sbuf_tensor("eres2", [128, 4096], F32) as eres,
        nc.sbuf_tensor("exr2", [128, 4096], F32) as exr,
        nc.sbuf_tensor("eyt", [128, 2 * 1024], F32) as eyt,
        nc.psum_tensor("psD2", [128, 4096], F32) as psD,
    ):
        psE = psD
        for ic in range(22):
            k.inc(sync.dma_start(out=wd1km[:, ic * 1024:(ic + 1) * 1024],
                                 in_=dr["wdown_scr"][0:D, ic * 128:(ic + 1) * 128],
                                 transpose=True), "dS", 16)
        wld = k.cnt["dS"]
        res_tok = None
        ydma = None
        mt = None
        for tb in range(8):
            hl = None
            if mt is not None:
                k.wait(sync, "tS", mt)
            for ic in range(22):
                hl = k.inc(sync.dma_start(
                    out=hkm[:, ic * 512:(ic + 1) * 512],
                    in_=dr["hq1t_park"][tb * 512:(tb + 1) * 512, ic * 128:(ic + 1) * 128],
                    transpose=True), "dS", 16)
            k.wait(ten, "dS", hl)
            k.wait(ten, "dS", wld)
            if res_tok is not None:
                k.wait(ten, "vS", res_tok)
                k.wait(ten, "sS")
            mi = None
            for ocE in range(8):
                for ic in range(22):
                    mi = ten.matmul(psD[:, ocE * 512:(ocE + 1) * 512],
                                    wd1km[:, ic * 1024 + ocE * 128: ic * 1024 + ocE * 128 + 128],
                                    hkm[:, ic * 512:(ic + 1) * 512],
                                    start=(ic == 0), stop=(ic == 21))
            mt = k.tok(ten, "tS")
            la = k.inc(sync.dma_start(out=eres[:, :].rearrange("p (c e) -> p c e", e=512),
                                      in_=bass.AP(dr["attn_sp"], tb * 512,
                                                  [[NT, 128], [128 * NT, 8], [1, 512]])), "dS", 16)
            lx = k.inc(sync.dma_start(out=exr[:, :].rearrange("p (c e) -> p c e", e=512),
                                      in_=bass.AP(dr["x1_park"], tb * 512,
                                                  [[NT, 128], [128 * NT, 8], [1, 512]])), "dS", 16)
            k.wait(vec, "tS", mt)
            k.wait(vec, "dS", lx)
            k.wait(vec, "vS")
            k.inc(vec.tensor_tensor(eres[:, :], eres[:, :], psD[:, :], Alu.add), "vS")
            k.wait(vec, "vS")
            k.inc(vec.tensor_tensor(eres[:, :], eres[:, :], exr[:, :], Alu.add), "vS")
            res_tok = k.tok(vec, "vS")
            for st_ in range(4):
                hb = st_ % 2
                k.wait(ten, "vS", res_tok)
                k.wait(ten, "sS")
                ti = None
                for c in range(DCH):
                    ti = ten.transpose(psE[:, hb * 1024 + c * 128: hb * 1024 + (c + 1) * 128],
                                       eres[:, c * 512 + st_ * 128: c * 512 + st_ * 128 + 128],
                                       identF[:, :])
                tt2 = k.tok(ten, "tS")
                k.wait(sca, "tS", tt2)
                k.wait(sca, "dA")
                ei = sca.copy(eyt[:, hb * 1024:(hb + 1) * 1024], psE[:, hb * 1024:(hb + 1) * 1024])
                et = k.inc(ei, "sS")
                k.wait(sca, "sS", et)
                ydma = k.inc(sca.dma_start(
                    out=y_out[tb * 512 + st_ * 128:tb * 512 + st_ * 128 + 128, :],
                    in_=eyt[:, hb * 1024:(hb + 1) * 1024]), "dA", 16)
        k.wait(sca, "dA")
        k.wait(sync, "dA")

    if dbg == "wdq":
        k.barrier([gp])
        for i in range(24):
            k.inc(gp.dma_start(out=dbg_t[:, i * 1024:(i + 1) * 1024],
                               in_=dr["wup_scr"][i * 128:(i + 1) * 128, :]), "dG", 16)
        k.wait(gp, "dG")
    if dbg in ("up0", "gate0", "g20", "o0f"):
        spname = {"up0": "spill_up", "gate0": "spill_gate", "g20": "spill_gate2", "o0f": "spill_o"}[dbg]
        k.barrier([gp])
        for tb_ in range(2):
            k.inc(gp.dma_start(
                out=dbg_t[:, tb_ * 16384: tb_ * 16384 + 12288].rearrange("p (o e) -> p o e", e=512),
                in_=bass.AP(dr[spname], tb_ * 24 * 128 * 512,
                            [[512, 128], [128 * 512, 24], [1, 512]])), "dG", 16)
        k.inc(gp.dma_start(out=dbg_t[:, 16352:16384], in_=scal[:, 0:32]), "dG", 16)
        k.wait(gp, "dG")
    if dbg in ("x1", "xn1"):
        k.barrier([gp])
        if dbg == "x1":
            for i in range(32):
                k.inc(gp.dma_start(out=dbg_t[:, i * 1024:(i + 1) * 1024],
                                   in_=dr["x1tok_park"][i * 128:(i + 1) * 128, :]), "dG", 16)
        else:
            for half in range(8):
                k.inc(gp.dma_start(out=dbg_t[:, half * 4096:(half + 1) * 4096],
                                   in_=xnfm[:, half * 4096:(half + 1) * 4096]), "dG", 16)
        k.wait(gp, "dG")
    if dbg == "xn0":
        k.wait(gp, "sS")
        k.wait(gp, "dS")
        for half in range(8):
            k.inc(gp.dma_start(out=dbg_t[:, half * 4096:(half + 1) * 4096],
                               in_=xnfm[:, half * 4096:(half + 1) * 4096]), "dG", 16)
        k.wait(gp, "dG")
    elif dbg == "xq0":
        k.wait(gp, "vS")
        for half in range(8):
            k.inc(gp.dma_start(out=dbg_t[:, half * 4096:(half + 1) * 4096],
                               in_=xqfm[:, half * 4096:(half + 1) * 4096]), "dG", 16)
        k.wait(gp, "dG")
    elif dbg == "attn0":
        k.wait(gp, "dA")
        for c in range(8):
            k.inc(gp.dma_start(out=dbg_t[:, c * 4096:(c + 1) * 4096],
                               in_=dr["attn_sp"][c * 128:(c + 1) * 128, :]), "dG", 16)
        k.wait(gp, "dG")


    es.close()
    return nc


def host_consts():
    i = np.arange(HD // 2, dtype=np.float32)
    angles = np.power(np.float32(10000.0), 2.0 * i / HD).astype(np.float32)
    theta = np.arange(T, dtype=np.float32)[:, None] * angles[None, :]
    cos = np.cos(theta).astype(np.float32)
    sin = np.sin(theta).astype(np.float32)
    c2 = np.zeros((T, HD), np.float32)
    s2 = np.zeros((T, HD), np.float32)
    c2[:, 0::2] = cos
    c2[:, 1::2] = cos
    s2[:, 0::2] = -sin
    s2[:, 1::2] = sin
    dd = np.arange(-2047, 2048, dtype=np.float32)
    reld = np.sign(dd) * np.log1p(np.abs(dd))
    relrow = np.zeros((4224,), np.float32)
    relrow[:4095] = reld
    relrep = np.broadcast_to(relrow, (32, 4224)).copy()
    return c2, s2, relrep


def make_inmaps(inputs):
    c2, s2, relrep = host_consts()
    xs = np.ascontiguousarray(inputs["L0_attn_x"].astype(np.float32)).reshape(NCORES, NT, D)
    shared = {
        "l0_wq": inputs["L0_attn_w_q"], "l0_wk": inputs["L0_attn_w_k"],
        "l0_wv": inputs["L0_attn_w_v"], "l0_wo": inputs["L0_attn_w_o"],
        "l0_wup": inputs["L0_ffn_w_up"], "l0_wgate": inputs["L0_ffn_w_gate"],
        "l0_wdown": inputs["L0_ffn_w_down"],
        "l1_wqkv": inputs["L1_attn_w_qkv"], "l1_bqkv": inputs["L1_attn_b_qkv"].reshape(1, -1),
        "l1_wout": inputs["L1_attn_w_out"], "l1_bout": inputs["L1_attn_b_out"].reshape(1, -1),
        "l1_w1r": inputs["L1_attn_w1_rpe"], "l1_b1r": inputs["L1_attn_b1_rpe"].reshape(1, -1),
        "l1_w2r": inputs["L1_attn_w2_rpe"], "l1_b2r": inputs["L1_attn_b2_rpe"].reshape(1, -1),
        "l1_wgate": inputs["L1_ffn_w_gate"], "l1_wup": inputs["L1_ffn_w_up"],
        "l1_wdown": inputs["L1_ffn_w_down"],
        "rope_c2": c2, "rope_s2": s2, "relrep": relrep,
    }
    shared = {kk: np.ascontiguousarray(np.asarray(v).astype(np.float32)) for kk, v in shared.items()}
    return [dict(shared, x=np.ascontiguousarray(xs[c])) for c in range(NCORES)]


def kernel(**inputs):
    nc = build_program()
    in_maps = make_inmaps(inputs)
    res = run_bass_kernel_spmd(nc, in_maps, core_ids=list(range(NCORES)))
    out = np.stack([res.results[c]["y"] for c in range(NCORES)])
    return out.reshape(NCORES * B, T, D).astype(np.float32)


if __name__ == "__main__":
    build_program()
    print("build ok")


# version 02
# speedup vs baseline: 1.0709x; 1.0709x over previous
import numpy as np
from contextlib import ExitStack
import concourse.bass as bass
import concourse.mybir as mybir
from concourse.bass_utils import run_bass_kernel_spmd

F32 = mybir.dt.float32
I32 = mybir.dt.int32
BF16 = mybir.dt.bfloat16
Alu = mybir.AluOpType
Act = mybir.ActivationFunctionType
AX = mybir.AxisListType

NCORES = 8
B = 2
T = 2048
NT = B * T
D = 1024
H = 16
HD = 64
DCH = D // 128
I0 = 3072
I1 = 2816
QMAX = 32767.0
EPS = 1e-5

DBG_SHAPE = [128, 8 * 4096]


class K:
    def __init__(self, nc, es):
        self.nc = nc
        self.es = es
        self.sems = {}
        self.cnt = {}
        for n in ["dS", "dA", "dG", "vS", "tS", "sS", "gS", "cS"]:
            self.sems[n] = es.enter_context(nc.semaphore(n))
            self.cnt[n] = 0

    def inc(self, inst, name, n=1):
        inst.then_inc(self.sems[name], n)
        self.cnt[name] += n
        return self.cnt[name]

    def wait(self, eng, name, val=None):
        eng.wait_ge(self.sems[name], self.cnt[name] if val is None else val)

    def tok(self, eng, name, n=1):
        self.cnt[name] += n
        eng.drain().then_inc(self.sems[name], n)
        return self.cnt[name]

    def barrier(self, engines):
        for e in engines:
            for n in ["dS", "dA", "dG", "vS", "tS", "sS", "gS", "cS"]:
                if self.cnt[n]:
                    e.wait_ge(self.sems[n], self.cnt[n])


def doubling(k, q, dst, width, dsem, base=0, nrows=128):
    p = 1
    while p < nrows:
        k.inc(q.dma_start(out=dst[base + p:base + 2 * p, 0:width],
                          in_=dst[base:base + p, 0:width]), dsem, 16)
        k.wait(q, dsem)
        p *= 2


def build_program(dbg=None, stop_after=None):
    nc = bass.Bass("TRN2", target_bir_lowering=False, debug=False)
    es = ExitStack()
    k = K(nc, es)
    sync, vec, sca, ten, gp = nc.sync, nc.vector, nc.scalar, nc.tensor, nc.gpsimd
    ALL_ENG = [sync, vec, sca, ten, gp]

    x_in = nc.dram_tensor("x", [NT, D], F32, kind="ExternalInput")
    y_out = nc.dram_tensor("y", [NT, D], F32, kind="ExternalOutput")
    w = {}
    for name, shape in [
        ("l0_wq", [D, D]), ("l0_wk", [D, D]), ("l0_wv", [D, D]), ("l0_wo", [D, D]),
        ("l0_wup", [I0, D]), ("l0_wgate", [I0, D]), ("l0_wdown", [D, I0]),
        ("l1_wqkv", [D, 3 * D]), ("l1_bqkv", [1, 3 * D]),
        ("l1_wout", [D, D]), ("l1_bout", [1, D]),
        ("l1_w1r", [1, 32]), ("l1_b1r", [1, 32]), ("l1_w2r", [32, 16]), ("l1_b2r", [1, 16]),
        ("l1_wgate", [I1, D]), ("l1_wup", [I1, D]), ("l1_wdown", [D, I1]),
        ("rope_c2", [T, HD]), ("rope_s2", [T, HD]),
        ("relrep", [32, 4224]),
    ]:
        w[name] = nc.dram_tensor(name, shape, F32, kind="ExternalInput")

    dr = {}
    for name, shape, dt in [
        ("spill_up", [8 * 24 * 128, 512], BF16),
        ("spill_gate", [8 * 24 * 128, 512], BF16),
        ("spill_gate2", [8 * 24 * 128, 512], BF16),
        ("spill_o", [8 * 24 * 128, 512], BF16),
        ("attn_sp", [DCH * 128, NT], F32),
        ("x1_park", [DCH * 128, NT], F32),
        ("xnt_park", [NT, D], BF16),
        ("qkt_park", [2 * T, D], BF16),
        ("wb_scr", [4 * D, D], BF16),
        ("wup_scr", [I0, D], BF16),
        ("wgate_scr", [I0, D], BF16),
        ("wdown_scr", [D, I0], BF16),
        ("x1tok_park", [NT, D], F32),
        ("xfm_park", [DCH * 128, NT], F32),
        ("exptab", [16, 4224], BF16),
        ("xn1f_park", [NT, D], F32),
        ("qk1_park", [2 * DCH * 128, T], BF16),
        ("wqkv_scr", [D, 3 * D], BF16),
        ("wout_scr", [D, D], BF16),
        ("xq1t_park", [NT, D], BF16),
        ("bounce16", [1, 16], F32),
        ("hq1t_park", [NT, I1], BF16),
    ]:
        dr[name] = nc.dram_tensor(name, shape, dt)
    for j in range(4):
        dr[f"cc_in{j}"] = nc.dram_tensor(f"cc_in{j}", [1, 4], F32)
        dr[f"cc_out{j}"] = nc.dram_tensor(f"cc_out{j}", [1, 4], F32)

    dbg_t = nc.dram_tensor("dbg", DBG_SHAPE, F32, kind="ExternalOutput") if dbg else None

    def S(name, shape, dt):
        return es.enter_context(nc.sbuf_tensor(name, shape, dt))

    xnfm = S("xnfm", [128, DCH * NT], BF16)
    ident = S("ident", [128, 128], BF16)
    identF = S("identF", [128, 128], F32)
    iia = S("iia", [128, 128], I32)
    scal = S("scal", [128, 32], F32)
    gmax = S("gmax", [128, 16], F32)
    row = S("rowA", [1, 32], F32)
    lnr = S("lnr", [128, 16], F32)

    C_CX, C_SX = 0, 1

    blk = es.enter_context(nc.Block())

    k.inc(gp.iota(iia[:, :], [[1, 128]], channel_multiplier=-1,
                  allow_small_or_imprecise_dtypes=True), "gS")
    for j, wn in enumerate(["l0_wq", "l0_wk", "l0_wv", "l0_wo"]):
        k.inc(gp.dma_start(out=dr["wb_scr"][j * D:(j + 1) * D, :], in_=w[wn][:, :]), "dG", 16)
    k.wait(vec, "gS")
    k.inc(vec.tensor_scalar(ident[:, :], iia[:, :], 0.0, 1.0, Alu.is_equal, Alu.mult), "vS")
    k.inc(vec.tensor_scalar(identF[:, :], iia[:, :], 0.0, 1.0, Alu.is_equal, Alu.mult), "vS")
    k.inc(vec.memset(gmax[:, :], 0.0), "vS")
    k.inc(vec.memset(scal[:, :], 0.0), "vS")
    k.inc(vec.memset(row[:, :], 0.0), "vS")

    def ln_pass(tag, src_dram, gmax_col, dst_fm, park_xfm=False, park_f32=None):
        with (
            nc.sbuf_tensor(f"xst{tag}", [128, 2 * D], F32) as xstage,
            nc.sbuf_tensor(f"xnt{tag}", [128, 2 * D], BF16) as xnt,
            nc.sbuf_tensor(f"xfs{tag}", [128, 2 * D], F32) as xfs,
            nc.psum_tensor(f"psL{tag}", [128, 2048], F32) as psL,
        ):
            park_tok = []
            stats_tok = []
            xf_tok = []
            xft_tok = []
            for it in range(NT // 128):
                buf = it % 2
                if it >= 2:
                    k.wait(sync, "vS", stats_tok[it - 2])
                    if park_xfm:
                        k.wait(sync, "tS", xft_tok[it - 2])
                k.inc(sync.dma_start(out=xstage[:, buf * D:(buf + 1) * D],
                                     in_=src_dram[it * 128:(it + 1) * 128, :]), "dS", 16)
                xt = xstage[:, buf * D:(buf + 1) * D]
                k.wait(vec, "dS")
                k.inc(vec.bn_stats(lnr[:, 0:6], xt[:, 0:512]), "vS")
                k.inc(vec.bn_stats(lnr[:, 6:12], xt[:, 512:1024]), "vS")
                k.wait(vec, "vS")
                k.inc(vec.bn_aggr(lnr[:, 12:14], lnr[:, 0:12]), "vS")
                k.wait(vec, "vS")
                k.inc(vec.tensor_scalar(lnr[:, 3:4], lnr[:, 12:13], -1.0, None, Alu.mult), "vS")
                k.inc(vec.tensor_scalar(lnr[:, 6:7], lnr[:, 13:14], EPS, None, Alu.add), "vS")
                k.wait(vec, "vS")
                k.inc(vec.reciprocal(lnr[:, 7:8], lnr[:, 6:7]), "vS")
                k.wait(sca, "vS")
                k.inc(sca.activation(lnr[:, 8:9], lnr[:, 7:8], Act.Sqrt), "sS")
                k.wait(vec, "sS")
                if it >= 2:
                    k.wait(vec, "dA", park_tok[it - 2])
                cxn = k.inc(vec.tensor_scalar(xnt[:, buf * D:(buf + 1) * D], xt,
                                              lnr[:, 3:4], lnr[:, 8:9], Alu.add, Alu.mult), "vS")
                if park_f32 is not None:
                    k.inc(vec.tensor_scalar(xfs[:, buf * D:(buf + 1) * D], xt,
                                            lnr[:, 3:4], lnr[:, 8:9], Alu.add, Alu.mult), "vS")
                    cxn = k.tok(vec, "vS")
                    k.wait(sca, "vS", cxn)
                    k.wait(sca, "dA")
                    k.inc(sca.dma_start(out=park_f32[it * 128:(it + 1) * 128, :],
                                        in_=xfs[:, buf * D:(buf + 1) * D]), "dA", 16)
                stats_tok.append(cxn)
                k.wait(vec, "vS", cxn)
                k.inc(vec.tensor_reduce(lnr[:, 9:10], xnt[:, buf * D:(buf + 1) * D], axis=AX.X,
                                        op=Alu.max, apply_absolute_value=True), "vS")
                k.wait(vec, "vS")
                k.inc(vec.tensor_tensor(gmax[:, gmax_col:gmax_col + 1],
                                        gmax[:, gmax_col:gmax_col + 1], lnr[:, 9:10], Alu.max), "vS")
                k.wait(sca, "vS", cxn)
                park_tok.append(k.inc(sca.dma_start(out=dr["xnt_park"][it * 128:(it + 1) * 128, :],
                                                    in_=xnt[:, buf * D:(buf + 1) * D]), "dA", 16))
                if park_xfm:
                    k.wait(ten, "vS", cxn)
                    if it >= 2:
                        k.wait(ten, "sS", xf_tok[it - 2])
                    ti = None
                    for c in range(DCH):
                        ti = ten.transpose(psL[:, buf * 1024 + c * 128: buf * 1024 + (c + 1) * 128],
                                           xstage[:, buf * D + c * 128: buf * D + (c + 1) * 128],
                                           identF[:, :])
                    tt_ = k.tok(ten, "tS")
                    xft_tok.append(tt_)
                    k.wait(sca, "tS", tt_)
                    k.wait(sca, "dA")
                    ei = sca.copy(xfs[:, buf * D:(buf + 1) * D], psL[:, buf * 1024:(buf + 1) * 1024])
                    xf_tok.append(k.inc(ei, "sS"))
                    k.wait(sca, "sS", xf_tok[-1])
                    for c in range(DCH):
                        k.inc(sca.dma_start(
                            out=dr["xfm_park"][c * 128:(c + 1) * 128, it * 128:(it + 1) * 128],
                            in_=xfs[:, buf * D + c * 128: buf * D + (c + 1) * 128]), "dA", 16)
            k.wait(sync, "dA")
            for c in range(DCH):
                k.inc(sync.dma_start(out=dst_fm[:, c * NT:(c + 1) * NT],
                                     in_=dr["xnt_park"][:, c * 128:(c + 1) * 128],
                                     transpose=True), "dS", 16)

    def _stopper(name):
        if stop_after == name:
            k.barrier(ALL_ENG)
            k.inc(gp.dma_start(out=y_out[0:128, :], in_=xnfm[:, 0:D]), "dG", 16)
            k.wait(gp, "dG")
            return True
        return False

    ln_pass("0", x_in, 0, xnfm, park_xfm=True)

    k.wait(gp, "vS")
    k.inc(gp.tensor_reduce(row[0:1, 0:1], gmax[:, 0:1], axis=AX.C, op=Alu.max), "gS")
    k.wait(vec, "gS")
    k.inc(vec.tensor_scalar(row[0:1, 1:2], row[0:1, 0:1], 1.0 / QMAX, 1e-12, Alu.mult, Alu.max), "vS")
    k.wait(gp, "vS")
    k.inc(gp.dma_start(out=dr["cc_in0"][0:1, 0:4], in_=row[0:1, 0:4]), "dG", 16)
    k.wait(gp, "dG")
    k.inc(gp.collective_compute("AllReduce", Alu.max, replica_groups=[list(range(NCORES))],
                                ins=[dr["cc_in0"].ap()], outs=[dr["cc_out0"].ap()]), "cS")
    k.wait(gp, "cS")
    k.inc(gp.dma_start(out=row[0:1, 4:8], in_=dr["cc_out0"][0:1, 0:4]), "dG", 16)
    k.wait(vec, "dG")
    k.inc(vec.tensor_copy(scal[0:1, C_SX:C_SX + 1], row[0:1, 5:6]), "vS")
    k.inc(vec.reciprocal(scal[0:1, C_CX:C_CX + 1], row[0:1, 5:6]), "vS")

    if _stopper("ln0"):
        es.close()
        return nc
    def l0_attn_seq(s):
        k.barrier(ALL_ENG)
        with (
            nc.sbuf_tensor(f"yfm{s}", [128, DCH * T], BF16) as yfm,
            nc.sbuf_tensor(f"qn2{s}", [128, 16], F32) as qn2,
            nc.sbuf_tensor(f"kn2{s}", [128, 16], F32) as kn2,
            nc.sbuf_tensor(f"mcol{s}", [128, 16], F32) as mcol,
        ):
            k.wait(vec, "vS")
            k.inc(vec.memset(qn2[:, :], 0.0), "vS")
            k.inc(vec.memset(kn2[:, :], 0.0), "vS")
            with (
                nc.sbuf_tensor(f"kfm{s}", [128, DCH * T], BF16) as kfm,
                nc.sbuf_tensor(f"v65_{s}", [128, 16 * 16 * 65], BF16) as v65,
            ):
                k.wait(vec, "vS")
                k.inc(vec.memset(v65[:, :], 1.0), "vS")
                with (
                    nc.sbuf_tensor(f"wkm{s}", [128, 2 * DCH * 1024], BF16) as wkm,
                    nc.sbuf_tensor(f"c2a{s}", [128, 16 * 64], BF16) as c2a,
                    nc.sbuf_tensor(f"s2a{s}", [128, 16 * 64], BF16) as s2a,
                    nc.sbuf_tensor(f"qkrot{s}", [128, 2 * 1024], BF16) as qkrot,
                    nc.sbuf_tensor(f"sqst{s}", [128, 1024], BF16) as sqst,
                    nc.psum_tensor(f"psP{s}", [128, 3072], F32) as psP,
                ):
                    k.wait(sync, "dG")
                    for j in range(2):
                        for ic in range(DCH):
                            k.inc(sync.dma_start(
                                out=wkm[:, (j * DCH + ic) * 1024:(j * DCH + ic + 1) * 1024],
                                in_=dr["wb_scr"][j * D:(j + 1) * D, ic * 128:(ic + 1) * 128],
                                transpose=True), "dS", 16)
                    k.inc(gp.dma_start(out=c2a[:, :], in_=bass.AP(w["rope_c2"], 0,
                        [[64, 128], [8192, 16], [1, 64]])), "dG", 16)
                    k.inc(gp.dma_start(out=s2a[:, :], in_=bass.AP(w["rope_s2"], 0,
                        [[64, 128], [8192, 16], [1, 64]])), "dG", 16)
                    wload_tok = k.cnt["dS"]
                    k.wait(vec, "dG")

                    def qk_mms(tc, wslice, psslice, tok0):
                        mi = None
                        for c in range(DCH):
                            stat = xnfm[:, c * NT + tok0: c * NT + tok0 + 128]
                            for j, nh in [(jj, nn) for jj in wslice for nn in range(2)]:
                                mi = ten.matmul(
                                    psP[:, psslice[j] * 1024 + nh * 512: psslice[j] * 1024 + (nh + 1) * 512],
                                    stat,
                                    wkm[:, (j * DCH + c) * 1024 + nh * 512:
                                        (j * DCH + c) * 1024 + (nh + 1) * 512],
                                    start=(c == 0), stop=(c == DCH - 1))
                        return mi

                    rope_tok = []
                    park_tok = []
                    for tc in range(16):
                        tok0 = s * T + tc * 128
                        k.wait(ten, "dS", wload_tok)
                        if tc >= 1:
                            k.wait(ten, "vS", rope_tok[tc - 1])
                        mt = k.inc(qk_mms(tc, [0, 1], {0: 0, 1: 1}, tok0), "tS")
                        k.wait(vec, "tS", mt)
                        if tc >= 2:
                            k.wait(vec, "dA", park_tok[2 * (tc - 2) + 1])
                        c2s = c2a[:, tc * 64:(tc + 1) * 64].unsqueeze(1).broadcast_to((128, 16, 64))
                        s2s = s2a[:, tc * 64:(tc + 1) * 64].unsqueeze(1).broadcast_to((128, 16, 64))
                        for j in range(2):
                            src = psP[:, j * 1024:(j + 1) * 1024]
                            t1 = psP[:, 2048:3072]
                            k.wait(vec, "vS")
                            k.inc(vec.tensor_tensor(t1.rearrange("p (h d) -> p h d", d=64),
                                                    src.rearrange("p (h d) -> p h d", d=64), c2s,
                                                    Alu.mult), "vS")
                            srcp = src.rearrange("p (h i two) -> p h i two", h=16, two=2)
                            s2p = s2s.rearrange("p h (i two) -> p h i two", two=2)
                            dst = qkrot[:, j * 1024:(j + 1) * 1024].rearrange("p (h i two) -> p h i two", h=16, two=2)
                            k.inc(vec.tensor_tensor(dst[:, :, :, 0], srcp[:, :, :, 1], s2p[:, :, :, 0],
                                                    Alu.mult), "vS")
                            k.inc(vec.tensor_tensor(dst[:, :, :, 1], srcp[:, :, :, 0], s2p[:, :, :, 1],
                                                    Alu.mult), "vS")
                            k.wait(vec, "vS")
                            k.inc(vec.tensor_tensor(qkrot[:, j * 1024:(j + 1) * 1024],
                                                    qkrot[:, j * 1024:(j + 1) * 1024], t1, Alu.add), "vS")
                        rope_tok.append(k.cnt["vS"])
                        for j, acc in [(0, qn2), (1, kn2)]:
                            k.wait(sca, "vS")
                            k.inc(sca.activation(sqst[:, :], qkrot[:, j * 1024:(j + 1) * 1024],
                                                 Act.Square), "sS")
                            k.wait(vec, "sS")
                            k.inc(vec.tensor_reduce(lnr[:, 0:16].unsqueeze(2),
                                                    sqst[:, :].rearrange("p (h d) -> p h d", d=64),
                                                    axis=AX.X, op=Alu.add), "vS")
                            k.wait(vec, "vS")
                            k.inc(vec.tensor_tensor(acc[:, :], acc[:, :], lnr[:, 0:16], Alu.max), "vS")
                        k.wait(sca, "vS")
                        park_tok.append(k.inc(sca.dma_start(out=dr["qkt_park"][tc * 128:(tc + 1) * 128, :],
                                                            in_=qkrot[:, 0:1024]), "dA", 16))
                        park_tok.append(k.inc(sca.dma_start(
                            out=dr["qkt_park"][T + tc * 128:T + (tc + 1) * 128, :],
                            in_=qkrot[:, 1024:2048]), "dA", 16))
                    k.wait(sync, "tS")
                    for ic in range(DCH):
                        k.inc(sync.dma_start(out=wkm[:, ic * 1024:(ic + 1) * 1024],
                                             in_=dr["wb_scr"][2 * D:3 * D, ic * 128:(ic + 1) * 128],
                                             transpose=True), "dS", 16)
                    vload_tok = k.cnt["dS"]
                    vev_tok = []
                    for tc in range(16):
                        tok0 = s * T + tc * 128
                        k.wait(ten, "dS", vload_tok)
                        if tc >= 2:
                            k.wait(ten, "vS", vev_tok[tc - 2])
                        mi = None
                        for c in range(DCH):
                            stat = xnfm[:, c * NT + tok0: c * NT + tok0 + 128]
                            for nh in range(2):
                                mi = ten.matmul(psP[:, (tc % 2) * 1024 + nh * 512:
                                                     (tc % 2) * 1024 + (nh + 1) * 512],
                                                stat,
                                                wkm[:, c * 1024 + nh * 512: c * 1024 + (nh + 1) * 512],
                                                start=(c == 0), stop=(c == DCH - 1))
                        mt = k.inc(mi, "tS")
                        k.wait(vec, "tS", mt)
                        vdst = v65[:, tc * 1040: (tc + 1) * 1040].rearrange("p (h e) -> p h e", e=65)
                        vev_tok.append(k.inc(vec.tensor_copy(
                            vdst[:, :, 0:64],
                            psP[:, (tc % 2) * 1024:(tc % 2 + 1) * 1024].rearrange("p (h d) -> p h d", d=64)),
                            "vS"))
                    k.wait(sync, "dA")
                    for c in range(DCH):
                        k.inc(sync.dma_start(out=kfm[:, c * T:(c + 1) * T],
                                             in_=dr["qkt_park"][T:2 * T, c * 128:(c + 1) * 128],
                                             transpose=True), "dS", 16)
                    k.wait(gp, "vS")
                    k.inc(gp.tensor_reduce(row[0:1, 0:16], qn2[:, :], axis=AX.C, op=Alu.max), "gS")
                    k.wait(gp, "gS")
                    k.inc(gp.tensor_reduce(row[0:1, 16:32], kn2[:, :], axis=AX.C, op=Alu.max), "gS")
                    k.wait(vec, "gS")
                    k.inc(vec.tensor_tensor(row[0:1, 0:16], row[0:1, 0:16], row[0:1, 16:32],
                                            Alu.mult), "vS")
                    k.wait(sca, "vS")
                    k.inc(sca.activation(row[0:1, 0:16], row[0:1, 0:16], Act.Sqrt), "sS")
                    k.wait(vec, "sS")
                    k.inc(vec.tensor_scalar(mcol[0:1, 0:16], row[0:1, 0:16], -0.126, None, Alu.mult), "vS")
                    k.wait(sync, "vS")
                    doubling(k, sync, mcol, 16, "dS")

                k.barrier(ALL_ENG)
                with (
                    nc.sbuf_tensor(f"qbuf{s}", [128, 2 * 512], BF16) as qbuf,
                    nc.sbuf_tensor(f"pbuf{s}", [128, 2 * 3072], BF16) as pbuf,
                    nc.sbuf_tensor(f"yst{s}", [128, T], F32) as ystage,
                    nc.sbuf_tensor(f"ycop{s}", [128, 1024], F32) as ycop,
                    nc.sbuf_tensor(f"zrep{s}", [128, T], F32) as zrep,
                    nc.psum_tensor(f"psS{s}", [128, 3072], F32) as psS,
                    nc.psum_tensor(f"psY{s}", [128, 1024], F32) as psY,
                ):
                    gsz = [3, 3, 3, 3, 3, 1]
                    gof = [0, 3, 6, 9, 12, 15]
                    NG = 6
                    yev_tok = None
                    ysc_tok = None
                    ycop_tok = None
                    for hp in range(8):
                        hA = 2 * hp
                        qld_tok = [None] * 4
                        qsc_tok = [None] * 4
                        for qt in range(4):
                            qb = qt % 2
                            if qt >= 2:
                                k.wait(sync, "tS", qsc_tok[qt - 2])
                            qld_tok[qt] = k.inc(sync.dma_start(
                                out=qbuf[:, qb * 512:(qb + 1) * 512],
                                in_=dr["qkt_park"][qt * 512:(qt + 1) * 512, hp * 128:(hp + 1) * 128],
                                transpose=True), "dS", 16)
                            exp_tok = []
                            for g in range(NG):
                                k.wait(ten, "dS", qld_tok[qt])
                                if g >= 1:
                                    k.wait(ten, "sS", exp_tok[g - 1])
                                elif yev_tok is not None:
                                    k.wait(ten, "vS", yev_tok)
                                mi = None
                                for jj in range(gsz[g]):
                                    kc = gof[g] + jj
                                    for hh in range(2):
                                        mi = ten.matmul(
                                            psS[:, jj * 1024 + hh * 512: jj * 1024 + (hh + 1) * 512],
                                            kfm[hh * 64:(hh + 1) * 64,
                                                hp * T + kc * 128: hp * T + (kc + 1) * 128],
                                            qbuf[hh * 64:(hh + 1) * 64, qb * 512:(qb + 1) * 512],
                                            start=True, stop=True)
                                sc_tok = k.tok(ten, "tS")
                                qsc_tok[qt] = sc_tok
                                k.wait(sca, "tS", sc_tok)
                                pp = g % 2
                                ei = None
                                for hh in range(2):
                                    for jj in range(gsz[g]):
                                        ei = sca.activation(
                                            pbuf[:, pp * 3072 + jj * 1024 + hh * 512:
                                                 pp * 3072 + jj * 1024 + (hh + 1) * 512],
                                            psS[:, jj * 1024 + hh * 512: jj * 1024 + (hh + 1) * 512],
                                            Act.Exp, bias=mcol[:, (hA + hh):(hA + hh) + 1], scale=0.125)
                                exp_tok.append(k.tok(sca, "sS"))
                                if dbg == "P0" and s == 0 and hp == 0 and qt == 0 and g == 0:
                                    k.wait(vec, "sS")
                                    with nc.sbuf_tensor("scdump", [128, 3072], F32) as scd:
                                        k.inc(vec.tensor_copy(scd[:, :], psS[:, :]), "vS")
                                        k.wait(gp, "vS")
                                        k.inc(gp.dma_start(out=dbg_t[:, 0:3072], in_=scd[:, :]), "dG", 16)
                                    k.wait(gp, "dG")
                                    k.inc(gp.dma_start(out=dbg_t[:, 4096:4096 + 3072],
                                                       in_=pbuf[:, 0:3072]), "dG", 16)
                                    k.inc(gp.dma_start(out=dbg_t[:, 8192:8192 + 16],
                                                       in_=mcol[:, 0:16]), "dG", 16)
                                    k.wait(gp, "dG")
                                    k.wait(sca, "dG")
                                k.wait(ten, "sS", exp_tok[g])
                                mi = None
                                for jj in range(gsz[g]):
                                    kc = gof[g] + jj
                                    for hh in range(2):
                                        mi = ten.matmul(
                                            psY[0:65, hh * 512:(hh + 1) * 512],
                                            v65[:, kc * 1040 + (hA + hh) * 65:
                                                kc * 1040 + (hA + hh) * 65 + 65],
                                            pbuf[:, pp * 3072 + jj * 1024 + hh * 512:
                                                 pp * 3072 + jj * 1024 + (hh + 1) * 512],
                                            start=(g == 0 and jj == 0),
                                            stop=(g == NG - 1 and jj == gsz[g] - 1))
                                qsc_tok[qt] = k.inc(mi, "tS")
                            k.wait(vec, "tS")
                            if ysc_tok is not None:
                                k.wait(vec, "vS", ysc_tok)
                            if ycop_tok is not None:
                                k.wait(vec, "dS", ycop_tok)
                            k.inc(vec.tensor_copy(ystage[0:64, qt * 512:(qt + 1) * 512],
                                                  psY[0:64, 0:512]), "vS")
                            k.inc(vec.tensor_copy(ycop[0:64, 0:512], psY[0:64, 512:1024]), "vS")
                            yev_tok = k.inc(vec.tensor_copy(ycop[64:65, 0:1024], psY[64:65, 0:1024]), "vS")
                            k.wait(sync, "vS")
                            k.inc(sync.dma_start(out=ystage[64:128, qt * 512:(qt + 1) * 512],
                                                 in_=ycop[0:64, 0:512]), "dS", 16)
                            k.inc(sync.dma_start(out=zrep[0:1, qt * 512:(qt + 1) * 512],
                                                 in_=ycop[64:65, 0:512]), "dS", 16)
                            ycop_tok = k.inc(sync.dma_start(out=zrep[64:65, qt * 512:(qt + 1) * 512],
                                                            in_=ycop[64:65, 512:1024]), "dS", 16)
                        k.wait(vec, "dS", ycop_tok)
                        k.inc(vec.reciprocal(zrep[0:1, :], zrep[0:1, :]), "vS")
                        k.inc(vec.reciprocal(zrep[64:65, :], zrep[64:65, :]), "vS")
                        k.wait(sync, "vS")
                        p2 = 1
                        while p2 < 64:
                            k.inc(sync.dma_start(out=zrep[p2:2 * p2, :], in_=zrep[0:p2, :]), "dS", 16)
                            k.inc(sync.dma_start(out=zrep[64 + p2:64 + 2 * p2, :],
                                                 in_=zrep[64:64 + p2, :]), "dS", 16)
                            k.wait(sync, "dS")
                            p2 *= 2
                        k.wait(vec, "dS")
                        ysc_tok = k.inc(vec.tensor_tensor(yfm[:, hp * T:(hp + 1) * T], ystage[:, :],
                                                          zrep[:, :], Alu.mult), "vS")
                    if dbg == "yz0" and s == 0:
                        k.barrier([gp])
                        k.inc(gp.dma_start(out=dbg_t[:, 0:2048], in_=ystage[:, :]), "dG", 16)
                        k.inc(gp.dma_start(out=dbg_t[:, 2048:4096], in_=zrep[:, :]), "dG", 16)
                        k.inc(gp.dma_start(out=dbg_t[:, 8192:8192 + 16], in_=mcol[:, 0:16]), "dG", 16)
                        k.wait(gp, "dG")
            if dbg == "yfm0" and s == 0:
                k.barrier([gp])
                for c in range(8):
                    k.inc(gp.dma_start(out=dbg_t[:, c * 2048:(c + 1) * 2048],
                                       in_=yfm[:, c * T:(c + 1) * T]), "dG", 16)
                k.wait(gp, "dG")
            k.barrier(ALL_ENG)
            with (
                nc.sbuf_tensor(f"wo{s}", [128, DCH * 1024], BF16) as wo_km,
                nc.sbuf_tensor(f"ost{s}", [128, 2 * 2048], F32) as ostage,
                nc.psum_tensor(f"psO{s}", [128, 2048], F32) as psO,
            ):
                for ic in range(DCH):
                    k.inc(sync.dma_start(out=wo_km[:, ic * 1024:(ic + 1) * 1024],
                                         in_=dr["wb_scr"][3 * D:4 * D, ic * 128:(ic + 1) * 128],
                                         transpose=True), "dS", 16)
                osp = [None, None]
                est = [None, None]
                for oc in range(DCH):
                    k.wait(ten, "dS")
                    if oc >= 1:
                        k.wait(ten, "sS", est[(oc - 1) % 2])
                    mi = None
                    for yc in range(DCH):
                        for tb in range(4):
                            mi = ten.matmul(psO[:, tb * 512:(tb + 1) * 512],
                                            wo_km[:, yc * 1024 + oc * 128: yc * 1024 + oc * 128 + 128],
                                            yfm[:, yc * T + tb * 512: yc * T + (tb + 1) * 512],
                                            start=(yc == 0), stop=(yc == DCH - 1))
                    mt = k.inc(mi, "tS")
                    ob = oc % 2
                    k.wait(sca, "tS", mt)
                    if oc >= 2:
                        k.wait(sca, "dA", osp[ob])
                    est[ob] = k.inc(sca.copy(ostage[:, ob * 2048:(ob + 1) * 2048], psO[:, :]), "sS")
                    k.wait(sca, "sS", est[ob])
                    osp[ob] = k.inc(sca.dma_start(out=dr["attn_sp"][oc * 128:(oc + 1) * 128,
                                                                    s * T:(s + 1) * T],
                                                  in_=ostage[:, ob * 2048:(ob + 1) * 2048]), "dA", 16)

    l0_attn_seq(0)
    if _stopper("l0a0"):
        es.close()
        return nc
    if dbg not in ("qpark0", "yfm0", "yz0"):
        l0_attn_seq(1)
    if dbg == "qpark0":
        k.barrier([gp])
        for i in range(32):
            k.inc(gp.dma_start(out=dbg_t[:, i * 1024:(i + 1) * 1024],
                               in_=dr["qkt_park"][i * 128:(i + 1) * 128, :]), "dG", 16)
        k.wait(gp, "dG")

    if _stopper("l0a1"):
        es.close()
        return nc
    k.barrier(ALL_ENG)
    xq_cm = nc.sbuf_tensor("xqfm", [128, DCH * NT], BF16)
    xqfm = xq_cm.__enter__()
    k.wait(sync, "vS")
    doubling(k, sync, scal, 32, "dS")
    with nc.sbuf_tensor("xqi", [128, 2048], I32) as xqi:
        k.wait(vec, "dS")
        for st in range(DCH * NT // 2048):
            cols = slice(st * 2048, (st + 1) * 2048)
            k.wait(vec, "vS")
            ci = k.inc(vec.tensor_scalar(xqi[:, :], xnfm[:, cols], scal[:, C_CX:C_CX + 1], None,
                                         Alu.mult), "vS")
            k.wait(vec, "vS", ci)
            k.inc(vec.tensor_copy(xqfm[:, cols], xqi[:, :]), "vS")

    if _stopper("xq0s"):
        es.close()
        return nc
    C_C2U, C_C2G, C_SG, C_C2G2, C_C2O, C_SO, C_SU, C_SG2 = 2, 3, 4, 5, 6, 7, 8, 9

    def wqdq_lpbq(wsrc, scr, nchunks, width, blk_sz, qmax_w, tag):
        nb = width // blk_sz
        with (
            nc.sbuf_tensor(f"wn{tag}", [128, 2 * width], F32) as wn,
            nc.sbuf_tensor(f"wb{tag}", [128, 2 * nb], F32) as wbm,
            nc.sbuf_tensor(f"wi{tag}", [128, width], I32) as wi,
            nc.sbuf_tensor(f"wd{tag}", [128, 2 * width], BF16) as wd,
        ):
            out_tok = []
            for oc in range(nchunks):
                bu = oc % 2
                if oc >= 2:
                    k.wait(sync, "vS", out_tok[oc - 2])
                k.inc(sync.dma_start(out=wn[:, bu * width:(bu + 1) * width],
                                     in_=wsrc[oc * 128:(oc + 1) * 128, :]), "dS", 16)
                wt = wn[:, bu * width:(bu + 1) * width]
                k.wait(vec, "dS")
                k.wait(vec, "vS")
                k.inc(vec.tensor_reduce(wbm[:, bu * nb:(bu + 1) * nb].unsqueeze(2),
                                        wt.rearrange("p (b e) -> p b e", e=blk_sz),
                                        axis=AX.X, op=Alu.max, apply_absolute_value=True), "vS")
                k.wait(vec, "vS")
                k.inc(vec.tensor_scalar(wbm[:, bu * nb:(bu + 1) * nb], wbm[:, bu * nb:(bu + 1) * nb],
                                        1.0 / qmax_w, 1e-12, Alu.mult, Alu.max), "vS")
                k.wait(vec, "vS")
                k.inc(vec.reciprocal(wbm[:, bu * nb:(bu + 1) * nb], wbm[:, bu * nb:(bu + 1) * nb]), "vS")
                k.wait(vec, "vS")
                rsb = wbm[:, bu * nb:(bu + 1) * nb].unsqueeze(2).broadcast_to((128, nb, blk_sz))
                k.inc(vec.tensor_tensor(wt.rearrange("p (b e) -> p b e", e=blk_sz),
                                        wt.rearrange("p (b e) -> p b e", e=blk_sz), rsb, Alu.mult), "vS")
                k.wait(vec, "vS")
                k.inc(vec.tensor_scalar(wi[:, :], wt, 1.0, None, Alu.mult), "vS")
                k.wait(vec, "vS")
                k.inc(vec.reciprocal(wbm[:, bu * nb:(bu + 1) * nb], wbm[:, bu * nb:(bu + 1) * nb]), "vS")
                k.wait(vec, "vS")
                sb = wbm[:, bu * nb:(bu + 1) * nb].unsqueeze(2).broadcast_to((128, nb, blk_sz))
                k.inc(vec.tensor_tensor(wd[:, bu * width:(bu + 1) * width].rearrange(
                    "p (b e) -> p b e", e=blk_sz), wi[:, :].rearrange("p (b e) -> p b e", e=blk_sz),
                    sb, Alu.mult), "vS")
                ot = k.tok(vec, "vS")
                out_tok.append(ot)
                k.wait(sca, "vS", ot)
                k.inc(sca.dma_start(out=scr[oc * 128:(oc + 1) * 128, :],
                                    in_=wd[:, bu * width:(bu + 1) * width]), "dA", 16)
        k.barrier(ALL_ENG)

    def ffn_mm_spill(wkm_t, spill, gcol, kchunks, ochunks):
        with (
            nc.sbuf_tensor(f"fst{gcol}", [128, 2 * 4096], BF16) as fst,
            nc.psum_tensor(f"psF{gcol}", [128, 4096], F32) as psF,
        ):
            ev_tok = [None, None]
            sp_tok = [None, None]
            for oc in range(ochunks):
                bu = oc % 2
                k.wait(ten, "dS")
                if oc >= 1:
                    k.wait(ten, "sS", ev_tok[(oc - 1) % 2])
                mi = None
                for kc in range(kchunks):
                    for tb in range(8):
                        mi = ten.matmul(psF[:, tb * 512:(tb + 1) * 512],
                                        wkm_t[:, kc * (128 * ochunks) + oc * 128:
                                              kc * (128 * ochunks) + (oc + 1) * 128],
                                        xqfm[:, kc * NT + tb * 512: kc * NT + (tb + 1) * 512],
                                        start=(kc == 0), stop=(kc == kchunks - 1))
                mt = k.tok(ten, "tS")
                k.wait(sca, "tS", mt)
                if oc >= 2:
                    k.wait(sca, "dA", sp_tok[bu])
                ev_tok[bu] = k.tok(sca, "sS") if False else None
                ei = sca.copy(fst[:, bu * 4096:(bu + 1) * 4096], psF[:, :])
                ev_tok[bu] = k.inc(ei, "sS")
                k.wait(vec, "sS", ev_tok[bu])
                k.inc(vec.tensor_reduce(lnr[:, 0:1], fst[:, bu * 4096:(bu + 1) * 4096],
                                        axis=AX.X, op=Alu.max, apply_absolute_value=True), "vS")
                k.wait(vec, "vS")
                k.inc(vec.tensor_tensor(gmax[:, gcol:gcol + 1], gmax[:, gcol:gcol + 1],
                                        lnr[:, 0:1], Alu.max), "vS")
                k.wait(sca, "sS", ev_tok[bu])
                sp_tok[bu] = k.inc(sca.dma_start(
                    out=bass.AP(spill, oc * 128 * 512,
                                [[512, 128], [24 * 128 * 512, 8], [1, 512]]),
                    in_=fst[:, bu * 4096:(bu + 1) * 4096].rearrange("p (t e) -> p t e", e=512)),
                    "dA", 16)

    def allreduce_max(idx, srcs, dsts):
        k.wait(gp, "vS")
        for i, c in enumerate(srcs):
            k.inc(gp.tensor_reduce(row[0:1, i:i + 1], gmax[:, c:c + 1], axis=AX.C, op=Alu.max), "gS")
            k.wait(gp, "gS")
        k.inc(gp.dma_start(out=dr[f"cc_in{idx}"][0:1, 0:4], in_=row[0:1, 0:4]), "dG", 16)
        k.wait(gp, "dG")
        k.inc(gp.collective_compute("AllReduce", Alu.max, replica_groups=[list(range(NCORES))],
                                    ins=[dr[f"cc_in{idx}"].ap()], outs=[dr[f"cc_out{idx}"].ap()]), "cS")
        k.wait(gp, "cS")
        k.inc(gp.dma_start(out=row[0:1, 4:8], in_=dr[f"cc_out{idx}"][0:1, 0:4]), "dG", 16)

    k.barrier(ALL_ENG)
    wqdq_lpbq(w["l0_wup"], dr["wup_scr"], 24, 1024, 32, 7.0, "u0")
    wqdq_lpbq(w["l0_wgate"], dr["wgate_scr"], 24, 1024, 32, 7.0, "g0")

    if _stopper("wqdq0"):
        es.close()
        return nc
    with nc.sbuf_tensor("wffkm", [128, DCH * I0], BF16) as wffkm:
        k.barrier(ALL_ENG)
        for ic in range(DCH):
            k.inc(sync.dma_start(out=wffkm[:, ic * I0:(ic + 1) * I0],
                                 in_=dr["wup_scr"][:, ic * 128:(ic + 1) * 128],
                                 transpose=True), "dS", 16)
        ffn_mm_spill(wffkm, dr["spill_up"], 1, DCH, 24)
        k.barrier([sync])
        k.wait(sync, "tS")
        for ic in range(DCH):
            k.inc(sync.dma_start(out=wffkm[:, ic * I0:(ic + 1) * I0],
                                 in_=dr["wgate_scr"][:, ic * 128:(ic + 1) * 128],
                                 transpose=True), "dS", 16)
        ffn_mm_spill(wffkm, dr["spill_gate"], 2, DCH, 24)

    xq_cm.__exit__(None, None, None)

    if _stopper("stB"):
        es.close()
        return nc
    allreduce_max(1, [1, 2], None)
    k.wait(vec, "dG")
    for i, (c2c, ssc) in enumerate([(C_C2U, C_SU), (C_C2G, C_SG)]):
        k.inc(vec.tensor_tensor(row[0:1, 8 + i:9 + i], row[0:1, 4 + i:5 + i],
                                scal[0:1, C_SX:C_SX + 1], Alu.mult), "vS")
        k.wait(vec, "vS")
        k.inc(vec.tensor_scalar(row[0:1, 8 + i:9 + i], row[0:1, 8 + i:9 + i],
                                1.0 / QMAX, 1e-12, Alu.mult, Alu.max), "vS")
        k.wait(vec, "vS")
        k.inc(vec.tensor_copy(scal[0:1, ssc:ssc + 1], row[0:1, 8 + i:9 + i]), "vS")
        k.inc(vec.reciprocal(row[0:1, 12 + i:13 + i], row[0:1, 8 + i:9 + i]), "vS")
        k.wait(vec, "vS")
        k.inc(vec.tensor_tensor(scal[0:1, c2c:c2c + 1], row[0:1, 12 + i:13 + i],
                                scal[0:1, C_SX:C_SX + 1], Alu.mult), "vS")
    k.wait(sync, "vS")
    doubling(k, sync, scal, 32, "dS")

    if _stopper("ar2"):
        es.close()
        return nc
    HW_ = 6 * 512
    with (
        nc.sbuf_tensor("cin", [128, 2 * 24 * 512], BF16) as cin,
        nc.sbuf_tensor("ci32", [128, HW_], I32) as ci32,
        nc.sbuf_tensor("csig", [128, HW_], I32) as csig,
        nc.sbuf_tensor("cprod", [128, HW_], F32) as cprod,
        nc.sbuf_tensor("cout", [128, 24 * 512], BF16) as cout,
        nc.sbuf_tensor("cpad", [128, 8], F32) as cpad,
    ):
        ld_tok = [None, None]
        st_tok = []
        codma = None
        for tb in range(8):
            bu = tb % 2
            if tb >= 2:
                k.wait(sync, "vS", st_tok[tb - 2])
            ld_tok[bu] = k.inc(sync.dma_start(
                out=cin[:, bu * 24 * 512:(bu + 1) * 24 * 512].rearrange("p (o e) -> p o e", e=512),
                in_=bass.AP(dr["spill_gate"], tb * 24 * 128 * 512, [[512, 128], [128 * 512, 24], [1, 512]])),
                "dS", 16)
            for hf in range(4):
                src = cin[:, bu * 24 * 512 + hf * HW_: bu * 24 * 512 + (hf + 1) * HW_]
                k.wait(vec, "dS", ld_tok[bu])
                k.wait(vec, "vS")
                k.inc(vec.tensor_scalar(ci32[:, :], src, scal[:, C_C2G:C_C2G + 1], None,
                                        Alu.mult), "vS")
                ct = k.tok(vec, "vS")
                k.wait(sca, "vS", ct)
                k.inc(sca.activation(cprod[:, :], ci32[:, :], Act.Sigmoid,
                                     bias=0.0, scale=scal[:, C_SG:C_SG + 1]), "sS")
                st2 = k.tok(sca, "sS")
                k.wait(sca, "sS", st2)
                k.inc(sca.activation(csig[:, :], cprod[:, :], Act.Copy,
                                     bias=0.0, scale=65536.0), "sS")
                st3 = k.tok(sca, "sS")
                k.wait(vec, "sS", st3)
                k.inc(vec.tensor_tensor(cprod[:, :], ci32[:, :], csig[:, :], Alu.mult), "vS")
                k.wait(vec, "vS")
                k.inc(vec.tensor_reduce(lnr[:, 0:1], cprod[:, :], axis=AX.X, op=Alu.max,
                                        apply_absolute_value=True), "vS")
                k.wait(vec, "vS")
                k.inc(vec.tensor_tensor(gmax[:, 3:4], gmax[:, 3:4], lnr[:, 0:1], Alu.max), "vS")
                if codma is not None:
                    k.wait(vec, "dA", codma)
                k.inc(vec.tensor_copy(cout[:, hf * HW_:(hf + 1) * HW_], cprod[:, :]), "vS")
            ot = k.tok(vec, "vS")
            st_tok.append(ot)
            k.wait(sca, "vS", ot)
            codma = k.inc(sca.dma_start(
                out=bass.AP(dr["spill_gate2"], tb * 24 * 128 * 512, [[512, 128], [128 * 512, 24], [1, 512]]),
                in_=cout[:, :].rearrange("p (o e) -> p o e", e=512)), "dA", 16)
        k.wait(sync, "dA")

    if _stopper("stC"):
        es.close()
        return nc
    allreduce_max(2, [3], None)
    k.wait(vec, "dG")
    k.inc(vec.tensor_tensor(row[0:1, 8:9], row[0:1, 4:5], scal[0:1, C_SG:C_SG + 1], Alu.mult), "vS")
    k.wait(vec, "vS")
    k.inc(vec.tensor_scalar(row[0:1, 8:9], row[0:1, 8:9], 1.0 / (65536.0 * QMAX), 1e-12,
                            Alu.mult, Alu.max), "vS")
    k.wait(vec, "vS")
    k.inc(vec.tensor_copy(scal[0:1, C_SG2:C_SG2 + 1], row[0:1, 8:9]), "vS")
    k.inc(vec.reciprocal(row[0:1, 9:10], row[0:1, 8:9]), "vS")
    k.wait(vec, "vS")
    k.inc(vec.tensor_tensor(row[0:1, 9:10], row[0:1, 9:10], scal[0:1, C_SG:C_SG + 1], Alu.mult), "vS")
    k.wait(vec, "vS")
    k.inc(vec.tensor_scalar(scal[0:1, C_C2G2:C_C2G2 + 1], row[0:1, 9:10], 1.0 / 65536.0, None,
                            Alu.mult), "vS")
    k.wait(sync, "vS")
    doubling(k, sync, scal, 32, "dS")

    if _stopper("ar3"):
        es.close()
        return nc
    with (
        nc.sbuf_tensor("din", [128, 24 * 512], BF16) as din,
        nc.sbuf_tensor("din2", [128, 24 * 512], BF16) as din2,
        nc.sbuf_tensor("di32", [128, HW_], I32) as di32,
        nc.sbuf_tensor("dj32", [128, HW_], I32) as dj32,
        nc.sbuf_tensor("dprod", [128, HW_], F32) as dprod,
        nc.sbuf_tensor("dout", [128, 24 * 512], BF16) as dout,
    ):
        st_tok = []
        dodma = None
        for tb in range(8):
            if tb >= 1:
                k.wait(sync, "vS", st_tok[tb - 1])
            lt = k.inc(sync.dma_start(
                out=din[:, :].rearrange("p (o e) -> p o e", e=512),
                in_=bass.AP(dr["spill_up"], tb * 24 * 128 * 512, [[512, 128], [128 * 512, 24], [1, 512]])),
                "dS", 16)
            lt2 = k.inc(sync.dma_start(
                out=din2[:, :].rearrange("p (o e) -> p o e", e=512),
                in_=bass.AP(dr["spill_gate2"], tb * 24 * 128 * 512, [[512, 128], [128 * 512, 24], [1, 512]])),
                "dS", 16)
            for hf in range(4):
                k.wait(vec, "dS", lt2)
                k.wait(vec, "vS")
                k.inc(vec.tensor_scalar(di32[:, :], din[:, hf * HW_:(hf + 1) * HW_],
                                        scal[:, C_C2U:C_C2U + 1], None, Alu.mult), "vS")
                k.inc(vec.tensor_scalar(dj32[:, :], din2[:, hf * HW_:(hf + 1) * HW_],
                                        scal[:, C_C2G2:C_C2G2 + 1], None, Alu.mult), "vS")
                k.wait(vec, "vS")
                k.inc(vec.tensor_tensor(dprod[:, :], di32[:, :], dj32[:, :], Alu.mult), "vS")
                k.wait(vec, "vS")
                k.inc(vec.tensor_reduce(lnr[:, 0:1], dprod[:, :], axis=AX.X, op=Alu.max,
                                        apply_absolute_value=True), "vS")
                k.wait(vec, "vS")
                k.inc(vec.tensor_tensor(gmax[:, 4:5], gmax[:, 4:5], lnr[:, 0:1], Alu.max), "vS")
                if dodma is not None:
                    k.wait(vec, "dA", dodma)
                k.inc(vec.tensor_copy(dout[:, hf * HW_:(hf + 1) * HW_], dprod[:, :]), "vS")
            ot = k.tok(vec, "vS")
            st_tok.append(ot)
            k.wait(sca, "vS", ot)
            dodma = k.inc(sca.dma_start(
                out=bass.AP(dr["spill_o"], tb * 24 * 128 * 512, [[512, 128], [128 * 512, 24], [1, 512]]),
                in_=dout[:, :].rearrange("p (o e) -> p o e", e=512)), "dA", 16)
        k.wait(sync, "dA")

    if _stopper("stD"):
        es.close()
        return nc
    allreduce_max(3, [4], None)
    k.wait(vec, "dG")
    k.inc(vec.tensor_tensor(row[0:1, 8:9], scal[0:1, C_SU:C_SU + 1], scal[0:1, C_SG2:C_SG2 + 1],
                            Alu.mult), "vS")
    k.wait(vec, "vS")
    k.inc(vec.tensor_tensor(row[0:1, 9:10], row[0:1, 4:5], row[0:1, 8:9], Alu.mult), "vS")
    k.wait(vec, "vS")
    k.inc(vec.tensor_scalar(row[0:1, 9:10], row[0:1, 9:10], 1.0 / QMAX, 1e-12, Alu.mult, Alu.max), "vS")
    k.wait(vec, "vS")
    k.inc(vec.tensor_copy(scal[0:1, C_SO:C_SO + 1], row[0:1, 9:10]), "vS")
    k.inc(vec.reciprocal(row[0:1, 10:11], row[0:1, 9:10]), "vS")
    k.wait(vec, "vS")
    k.inc(vec.tensor_tensor(scal[0:1, C_C2O:C_C2O + 1], row[0:1, 10:11], row[0:1, 8:9], Alu.mult), "vS")
    k.wait(sync, "vS")
    doubling(k, sync, scal, 32, "dS")

    if _stopper("ar4"):
        es.close()
        return nc
    wqdq_lpbq(w["l0_wdown"], dr["wdown_scr"], 8, I0, 32, 7.0, "d0")
    k.barrier(ALL_ENG)
    with (
        nc.sbuf_tensor("wdkm", [128, 24 * 1024], BF16) as wdkm,
        nc.sbuf_tensor("ein", [128, 24 * 512], BF16) as ein,
        nc.sbuf_tensor("ei32", [128, HW_], I32) as ei32,
        nc.sbuf_tensor("eb", [128, 24 * 512], BF16) as ebuf,
        nc.sbuf_tensor("eres", [128, 4096], F32) as eres,
        nc.sbuf_tensor("exr", [128, 4096], F32) as exr,
        nc.psum_tensor("psD", [128, 4096], F32) as psD,
    ):
        for ic in range(24):
            k.inc(sync.dma_start(out=wdkm[:, ic * 1024:(ic + 1) * 1024],
                                 in_=dr["wdown_scr"][:, ic * 128:(ic + 1) * 128],
                                 transpose=True), "dS", 16)
        wld = k.cnt["dS"]
        res_tok = None
        x1p_tok = None
        mt = None
        for tb in range(8):
            if res_tok is not None:
                k.wait(sync, "vS", res_tok)
            lt = k.inc(sync.dma_start(
                out=ein[:, :].rearrange("p (o e) -> p o e", e=512),
                in_=bass.AP(dr["spill_o"], tb * 24 * 128 * 512,
                            [[512, 128], [128 * 512, 24], [1, 512]])), "dS", 16)
            for hf in range(4):
                k.wait(vec, "dS", lt)
                k.wait(vec, "vS")
                if mt is not None:
                    k.wait(vec, "tS", mt)
                k.inc(vec.tensor_scalar(ei32[:, :], ein[:, hf * HW_:(hf + 1) * HW_],
                                        scal[:, C_C2O:C_C2O + 1], None, Alu.mult), "vS")
                k.wait(vec, "vS")
                k.inc(vec.tensor_copy(ebuf[:, hf * HW_:(hf + 1) * HW_], ei32[:, :]), "vS")
            obt = k.tok(vec, "vS")
            k.wait(ten, "vS", obt)
            k.wait(ten, "dS", wld)
            mi = None
            for ocE in range(8):
                for ic in range(24):
                    mi = ten.matmul(psD[:, ocE * 512:(ocE + 1) * 512],
                                    wdkm[:, ic * 1024 + ocE * 128: ic * 1024 + ocE * 128 + 128],
                                    ebuf[:, ic * 512:(ic + 1) * 512],
                                    start=(ic == 0), stop=(ic == 23))
            mt = k.tok(ten, "tS")
            la = k.inc(sync.dma_start(out=eres[:, :].rearrange("p (c e) -> p c e", e=512),
                                      in_=bass.AP(dr["attn_sp"], tb * 512,
                                                  [[NT, 128], [128 * NT, 8], [1, 512]])), "dS", 16)
            lx = k.inc(sync.dma_start(out=exr[:, :].rearrange("p (c e) -> p c e", e=512),
                                      in_=bass.AP(dr["xfm_park"], tb * 512,
                                                  [[NT, 128], [128 * NT, 8], [1, 512]])), "dS", 16)
            k.wait(vec, "tS", mt)
            k.wait(vec, "dS", lx)
            k.wait(vec, "vS")
            k.inc(vec.scalar_tensor_tensor(eres[:, :], psD[:, :], scal[:, C_SO:C_SO + 1],
                                           eres[:, :], Alu.mult, Alu.add), "vS")
            k.wait(vec, "vS")
            k.inc(vec.tensor_tensor(eres[:, :], eres[:, :], exr[:, :], Alu.add), "vS")
            res_tok = k.tok(vec, "vS")
            k.wait(sca, "vS", res_tok)
            if x1p_tok is not None:
                k.wait(sca, "dA", x1p_tok)
            x1p_tok = k.inc(sca.dma_start(out=bass.AP(dr["x1_park"], tb * 512,
                                                      [[NT, 128], [128 * NT, 8], [1, 512]]),
                                          in_=eres[:, :].rearrange("p (c e) -> p c e", e=512)), "dA", 16)

    if _stopper("stE"):
        es.close()
        return nc
    k.barrier(ALL_ENG)
    with (
        nc.sbuf_tensor("txin", [128, 2 * 1024], F32) as txin,
        nc.sbuf_tensor("txout", [128, 2 * 1024], F32) as txout,
        nc.psum_tensor("psX", [128, 2048], F32) as psX,
    ):
        ev_tok = [None, None]
        dm_tok = [None, None]
        for it in range(NT // 128):
            bu = it % 2
            if it >= 2:
                k.wait(sync, "tS")
            li = k.inc(sync.dma_start(out=txin[:, bu * 1024:(bu + 1) * 1024].rearrange(
                "p (c e) -> p c e", e=128),
                in_=bass.AP(dr["x1_park"], it * 128, [[NT, 128], [128 * NT, 8], [1, 128]])),
                "dS", 16)
            k.wait(ten, "dS", li)
            if ev_tok[bu] is not None:
                k.wait(ten, "sS", ev_tok[bu])
            for c in range(DCH):
                ten.transpose(psX[:, bu * 1024 + c * 128: bu * 1024 + (c + 1) * 128],
                              txin[:, bu * 1024 + c * 128: bu * 1024 + (c + 1) * 128],
                              identF[:, :])
            tt2 = k.tok(ten, "tS")
            k.wait(sca, "tS", tt2)
            if dm_tok[bu] is not None:
                k.wait(sca, "dA", dm_tok[bu])
            ev_tok[bu] = k.inc(sca.copy(txout[:, bu * 1024:(bu + 1) * 1024],
                                        psX[:, bu * 1024:(bu + 1) * 1024]), "sS")
            k.wait(sca, "sS", ev_tok[bu])
            dm_tok[bu] = k.inc(sca.dma_start(out=dr["x1tok_park"][it * 128:(it + 1) * 128, :],
                                             in_=txout[:, bu * 1024:(bu + 1) * 1024]), "dA", 16)

    if _stopper("x1t"):
        es.close()
        return nc
    k.barrier(ALL_ENG)
    ln_pass("1", dr["x1tok_park"], 5, xnfm, park_f32=dr["xn1f_park"])


    if _stopper("l0"):
        es.close()
        return nc

    k.inc(gp.dma_start(out=dr["wqkv_scr"][:, :], in_=w["l1_wqkv"][:, :]), "dG", 16)
    k.inc(gp.dma_start(out=dr["wout_scr"][:, :], in_=w["l1_wout"][:, :]), "dG", 16)

    class _StopEmit(Exception):
        pass

    k.barrier(ALL_ENG)
    if stop_after == "l1w":
        k.inc(gp.dma_start(out=y_out[0:128, :], in_=xnfm[:, 0:D]), "dG", 16)
        k.wait(gp, "dG")
        es.close()
        return nc
    with (
        nc.sbuf_tensor("rrep", [32, 4224], F32) as rrep,
        nc.sbuf_tensor("w1c", [32, 1], F32) as w1c,
        nc.sbuf_tensor("b1c", [32, 1], F32) as b1c,
        nc.sbuf_tensor("w2c", [32, 32], BF16) as w2c,
        nc.sbuf_tensor("b2c", [16, 1], F32) as b2c,
        nc.sbuf_tensor("rfb", [32, 4224], BF16) as rfb,
        nc.sbuf_tensor("tabs", [16, 4224], F32) as tabs,
        nc.sbuf_tensor("etabs", [16, 4224], BF16) as etabs,
        nc.sbuf_tensor("m2r", [16, 2], F32) as m2r,
        nc.psum_tensor("psR", [32, 2048], F32) as psR,
    ):
        k.inc(sync.dma_start(out=rrep[:, :], in_=w["relrep"][:, :]), "dS", 16)
        k.inc(sync.dma_start(out=w1c[:, :], in_=bass.AP(w["l1_w1r"], 0, [[1, 32], [1, 1]])), "dS", 16)
        k.inc(sync.dma_start(out=b1c[:, :], in_=bass.AP(w["l1_b1r"], 0, [[1, 32], [1, 1]])), "dS", 16)
        k.inc(sync.dma_start(out=b2c[:, :], in_=bass.AP(w["l1_b2r"], 0, [[1, 16], [1, 1]])), "dS", 16)
        k.wait(vec, "vS")
        k.inc(vec.memset(w2c[:, :], 0.0), "vS")
        k.wait(gp, "vS")
        k.inc(gp.dma_start(out=w2c[:, 0:16], in_=w["l1_w2r"][:, :]), "dG", 16)
        k.wait(sca, "dS")
        _lvl = {"rpeA": 0, "rpeB": 1, "rpeC": 2, "rpeD": 3}.get(stop_after, 99)
        if _lvl >= 1:
            k.inc(sca.activation(rfb[:, :], rrep[:, :], Act.Relu, bias=b1c[:, :], scale=w1c[:, :]), "sS")
        if _lvl >= 1:
            st = k.tok(sca, "sS")
        if _lvl >= 2:
            k.wait(ten, "sS", st)
            k.wait(ten, "dG")
            for half in range(2):
                if half >= 1:
                    k.wait(ten, "sS")
                for nn in range(4):
                    ten.matmul(psR[0:32, nn * 512:(nn + 1) * 512], w2c[:, :],
                               rfb[:, half * 2048 + nn * 512: half * 2048 + (nn + 1) * 512],
                               start=True, stop=True)
                tt_ = k.tok(ten, "tS")
                k.wait(sca, "tS", tt_)
                k.inc(sca.activation(tabs[0:16, half * 2048:(half + 1) * 2048], psR[0:16, :],
                                     Act.Identity, bias=b2c[:, :], scale=1.0), "sS")
                k.tok(sca, "sS")
        if _lvl >= 3:
            k.wait(vec, "sS")
            k.inc(vec.tensor_reduce(m2r[:, 0:1], tabs[:, :], axis=AX.X, op=Alu.max), "vS")
            mm2 = k.tok(vec, "vS")
            k.wait(gp, "vS", mm2)
            k.inc(gp.tensor_reduce(m2r[0:1, 1:2], m2r[:, 0:1], axis=AX.C, op=Alu.max), "gS")
            k.wait(sync, "gS")
            doubling(k, sync, m2r, 2, "dS", nrows=16)
        if _lvl >= 4:
            k.wait(vec, "dS")
            k.inc(vec.tensor_scalar(m2r[:, 0:1], m2r[:, 1:2], -1.0, None, Alu.mult), "vS")
            k.wait(sca, "vS")
            k.inc(sca.activation(etabs[:, :], tabs[:, :], Act.Exp, bias=m2r[:, 0:1], scale=1.0), "sS")
            st2 = k.tok(sca, "sS")
            k.wait(sca, "sS", st2)
            k.inc(sca.dma_start(out=dr["exptab"][:, :], in_=etabs[:, :]), "dA", 16)
    if stop_after in ("rpeA", "rpeB", "rpeC", "rpeD"):
        k.barrier(ALL_ENG)
        k.inc(gp.dma_start(out=y_out[0:128, :], in_=xnfm[:, 0:D]), "dG", 16)
        k.wait(gp, "dG")
        es.close()
        return nc

    if stop_after in ("rpe", "rpe0", "rpe1"):
        if _stopper(stop_after) or _stopper("rpe0") or _stopper("rpe1") or True:
            k.barrier(ALL_ENG)
            k.inc(gp.dma_start(out=y_out[0:128, :], in_=xnfm[:, 0:D]), "dG", 16)
            k.wait(gp, "dG")
            es.close()
            return nc

    def l1_attn_seq(s, sub=None):
        k.barrier(ALL_ENG)
        with (
            nc.sbuf_tensor(f"yfmB{s}", [128, DCH * T], BF16) as yfm,
            nc.sbuf_tensor(f"mcolB{s}", [128, 16], F32) as mcol,
            nc.sbuf_tensor(f"nrm{s}", [16, 8], F32) as nrm,
        ):
            with (
                nc.sbuf_tensor(f"wqk{s}", [128, DCH * 2048], BF16) as wqk,
                nc.sbuf_tensor(f"bcol{s}", [128, 24], F32) as bcol,
                nc.sbuf_tensor(f"pstg{s}", [128, 2 * 2048], BF16) as pstg,
                nc.sbuf_tensor(f"sqs{s}", [128, 2048], BF16) as sqs,
                nc.sbuf_tensor(f"hsel{s}", [128, 8 * 32], BF16) as hsel,
                nc.psum_tensor(f"psQ{s}", [128, 2048], F32) as psQ,
                nc.psum_tensor(f"psN{s}", [32, 2048], F32) as psN,
        ):
                k.wait(sync, "dG")
                for ic in range(DCH):
                    k.inc(sync.dma_start(out=wqk[:, ic * 2048:(ic + 1) * 2048],
                                         in_=dr["wqkv_scr"][ic * 128:(ic + 1) * 128, 0:2048]), "dS", 16)
                k.inc(sync.dma_start(out=bcol[:, :],
                                     in_=bass.AP(w["l1_bqkv"], 0, [[1, 128], [128, 24]]), allow_slow_non_contiguous=True), "dS", 16)
                k.wait(vec, "vS")
                k.inc(vec.memset(hsel[:, :], 0.0), "vS")
                for oc in range(8):
                    k.inc(vec.memset(hsel[0:64, oc * 32 + 2 * oc: oc * 32 + 2 * oc + 1], 1.0), "vS")
                    k.inc(vec.memset(hsel[64:128, oc * 32 + 2 * oc + 1: oc * 32 + 2 * oc + 2], 1.0), "vS")
                wl = k.cnt["dS"]
                ev_tok = [None, None]
                pk_tok = [None, None]
                for j in range(2):
                    for oc in range(8):
                        bu = oc % 2
                        col = j * 8 + oc
                        k.wait(ten, "dS", wl)
                        if ev_tok[bu] is not None:
                            k.wait(ten, "sS", ev_tok[bu])
                        mi = None
                        for ic in range(DCH):
                            for tb in range(4):
                                mi = ten.matmul(
                                    psQ[:, tb * 512:(tb + 1) * 512],
                                    wqk[:, ic * 2048 + col * 128: ic * 2048 + (col + 1) * 128],
                                    xnfm[:, ic * NT + s * T + tb * 512: ic * NT + s * T + (tb + 1) * 512],
                                    start=(ic == 0), stop=(ic == DCH - 1))
                        mt = k.tok(ten, "tS")
                        k.wait(sca, "tS", mt)
                        if pk_tok[bu] is not None:
                            k.wait(sca, "dA", pk_tok[bu])
                        k.wait(sca, "sS")
                        ei = sca.activation(pstg[:, bu * 2048:(bu + 1) * 2048], psQ[:, :],
                                            Act.Identity, bias=bcol[:, col:col + 1], scale=1.0)
                        ev_tok[bu] = k.tok(sca, "sS")
                        k.wait(sca, "sS", ev_tok[bu])
                        pk_tok[bu] = k.inc(sca.dma_start(
                            out=dr["qk1_park"][(j * DCH + oc) * 128:(j * DCH + oc + 1) * 128, :],
                            in_=pstg[:, bu * 2048:(bu + 1) * 2048]), "dA", 16)
                        k.inc(sca.activation(sqs[:, :], pstg[:, bu * 2048:(bu + 1) * 2048],
                                             Act.Square), "sS")
                        sq_t = k.tok(sca, "sS")
                        k.wait(ten, "sS", sq_t)
                        for tb in range(4):
                            ten.matmul(psN[0:32, tb * 512:(tb + 1) * 512],
                                       hsel[:, oc * 32:(oc + 1) * 32], sqs[:, tb * 512:(tb + 1) * 512],
                                       start=(oc == 0), stop=(oc == 7))
                        k.tok(ten, "tS")
                    k.wait(vec, "tS")
                    k.inc(vec.tensor_reduce(nrm[0:16, j:j + 1], psN[0:16, :], axis=AX.X,
                                            op=Alu.max), "vS")
                    k.tok(vec, "vS")
                k.wait(vec, "vS")
                k.inc(vec.tensor_tensor(nrm[:, 2:3], nrm[:, 0:1], nrm[:, 1:2], Alu.mult), "vS")
                nt = k.tok(vec, "vS")
                k.wait(sca, "vS", nt)
                k.inc(sca.activation(nrm[:, 3:4], nrm[:, 2:3], Act.Sqrt), "sS")
                st3 = k.tok(sca, "sS")
                k.wait(vec, "sS", st3)
                k.inc(vec.tensor_scalar(nrm[:, 4:5], nrm[:, 3:4], -0.126, None, Alu.mult), "vS")
                nt2 = k.tok(vec, "vS")
                k.wait(sca, "vS", nt2)
                k.inc(sca.dma_start(out=bass.AP(dr["bounce16"], 0, [[1, 16], [1, 1]]),
                                    in_=nrm[0:16, 4:5]), "dA", 16)
                k.wait(sync, "dA")
                k.inc(sync.dma_start(out=mcol[0:1, 0:16],
                                     in_=dr["bounce16"][0:1, 0:16]),
                      "dS", 16)
                k.wait(sync, "dS")
                doubling(k, sync, mcol, 16, "dS")
            if sub == "l1qk":
                return

            k.barrier(ALL_ENG)
            with (
                nc.sbuf_tensor(f"kfmB{s}", [128, DCH * T], BF16) as kfm,
                nc.sbuf_tensor(f"v65B{s}", [128, 16 * 16 * 65], BF16) as v65,
            ):
                with (
                    nc.sbuf_tensor(f"wvB{s}", [128, DCH * 1024], BF16) as wv,
                    nc.sbuf_tensor(f"bvr{s}", [128, 1024], F32) as bvr,
                    nc.psum_tensor(f"psV{s}", [128, 2048], F32) as psV,
                ):
                    for ic in range(DCH):
                        k.inc(sync.dma_start(out=wv[:, ic * 1024:(ic + 1) * 1024],
                                             in_=dr["wqkv_scr"][ic * 128:(ic + 1) * 128, 2048:3072]),
                              "dS", 16)
                    k.inc(sync.dma_start(out=bvr[0:1, :], in_=w["l1_bqkv"][0:1, 2048:3072]), "dS", 16)
                    k.wait(sync, "dS")
                    doubling(k, sync, bvr, 1024, "dS")
                    k.wait(vec, "vS")
                    k.inc(vec.memset(v65[:, :], 1.0), "vS")
                    wl2 = k.cnt["dS"]
                    vv_tok = [None, None]
                    for tc in range(16):
                        bu = tc % 2
                        tok0 = s * T + tc * 128
                        k.wait(ten, "dS", wl2)
                        if vv_tok[bu] is not None:
                            k.wait(ten, "vS", vv_tok[bu])
                        mi = None
                        for ic in range(DCH):
                            for nh in range(2):
                                mi = ten.matmul(psV[:, bu * 1024 + nh * 512: bu * 1024 + (nh + 1) * 512],
                                                xnfm[:, ic * NT + tok0: ic * NT + tok0 + 128],
                                                wv[:, ic * 1024 + nh * 512: ic * 1024 + (nh + 1) * 512],
                                                start=(ic == 0), stop=(ic == DCH - 1))
                        mt = k.tok(ten, "tS")
                        k.wait(vec, "tS", mt)
                        k.wait(vec, "dS")
                        k.inc(vec.tensor_tensor(psV[:, bu * 1024:(bu + 1) * 1024],
                                                psV[:, bu * 1024:(bu + 1) * 1024], bvr[:, :],
                                                Alu.add), "vS")
                        k.wait(vec, "vS")
                        vdst = v65[:, tc * 1040: (tc + 1) * 1040].rearrange("p (h e) -> p h e", e=65)
                        vv_tok[bu] = k.inc(vec.tensor_copy(
                            vdst[:, :, 0:64],
                            psV[:, bu * 1024:(bu + 1) * 1024].rearrange("p (h d) -> p h d", d=64)), "vS")
                k.barrier([sync])
                for c in range(DCH):
                    k.inc(sync.dma_start(out=kfm[:, c * T:(c + 1) * T],
                                         in_=dr["qk1_park"][(DCH + c) * 128:(DCH + c + 1) * 128, :]),
                          "dS", 16)

                if sub == "l1v":
                    return
                with (
                    nc.sbuf_tensor(f"qbufB{s}", [128, 2 * 512], BF16) as qbuf,
                    nc.sbuf_tensor(f"pbufB{s}", [128, 2 * 2048], BF16) as pbuf,
                    nc.sbuf_tensor(f"ystB{s}", [128, T], BF16) as ystage,
                    nc.sbuf_tensor(f"ycopB{s}", [128, 1024], BF16) as ycop,
                    nc.sbuf_tensor(f"zrepB{s}", [128, T], BF16) as zrep,
                    nc.sbuf_tensor(f"rttB{s}", [128, 2 * 4224], BF16) as rtt,
                    nc.psum_tensor(f"psSB{s}", [128, 2048], F32) as psS,
                    nc.psum_tensor(f"psYB{s}", [128, 1024], F32) as psY,
                ):
                    gsz = [2] * 8
                    gof = [0, 2, 4, 6, 8, 10, 12, 14]
                    NG = 8
                    yev_tok = None
                    ysc_tok = None
                    ycop_tok = None
                    rtt_tok = None
                    for hp in range(8):
                        hA = 2 * hp
                        if rtt_tok is not None:
                            k.wait(sync, "tS")
                            k.wait(sync, "vS")
                        for hh in range(2):
                            k.inc(sync.dma_start(out=rtt[0:1, hh * 4224:hh * 4224 + 4224],
                                                 in_=dr["exptab"][hA + hh:hA + hh + 1, :]), "dS", 16)
                        k.wait(sync, "dS")
                        p2 = 1
                        while p2 < 128:
                            for hh in range(2):
                                k.inc(sync.dma_start(
                                    out=rtt[p2:2 * p2, hh * 4224 + p2: hh * 4224 + 4224],
                                    in_=rtt[0:p2, hh * 4224: hh * 4224 + 4224 - p2]), "dS", 16)
                            k.wait(sync, "dS")
                            p2 *= 2
                        rtt_tok = k.cnt["dS"]
                        qld_tok = [None] * 4
                        qsc_tok = [None] * 4
                        for qt in range(4):
                            qb = qt % 2
                            if qt >= 2:
                                k.wait(sync, "tS", qsc_tok[qt - 2])
                            qld_tok[qt] = k.inc(sync.dma_start(
                                out=qbuf[:, qb * 512:(qb + 1) * 512],
                                in_=dr["qk1_park"][hp * 128:(hp + 1) * 128, qt * 512:(qt + 1) * 512]),
                                "dS", 16)
                            GK = 16
                            sc_toks = [None] * GK
                            ex_toks = [None] * GK
                            pm_toks = [None] * GK

                            def attv(g):
                                kc = g
                                pp = g % 2
                                k.wait(ten, "vS", pm_toks[g])
                                mv = None
                                for hh in range(2):
                                    mv = ten.matmul(
                                        psY[0:65, hh * 512:(hh + 1) * 512],
                                        v65[:, kc * 1040 + (hA + hh) * 65:
                                            kc * 1040 + (hA + hh) * 65 + 65],
                                        pbuf[:, pp * 2048 + hh * 512:
                                             pp * 2048 + (hh + 1) * 512],
                                        start=(g == 0), stop=(g == GK - 1))
                                return k.inc(mv, "tS")

                            for g in range(GK):
                                kc = g
                                pp = g % 2
                                h0 = pp * 1024
                                k.wait(ten, "dS", qld_tok[qt])
                                k.wait(ten, "dS", rtt_tok)
                                if g >= 2:
                                    k.wait(ten, "sS", ex_toks[g - 2])
                                else:
                                    k.wait(ten, "sS")
                                    if g == 0 and yev_tok is not None:
                                        k.wait(ten, "vS", yev_tok)
                                mi = None
                                for hh in range(2):
                                    mi = ten.matmul(
                                        psS[:, h0 + hh * 512: h0 + (hh + 1) * 512],
                                        kfm[hh * 64:(hh + 1) * 64,
                                            hp * T + kc * 128: hp * T + (kc + 1) * 128],
                                        qbuf[hh * 64:(hh + 1) * 64, qb * 512:(qb + 1) * 512],
                                        start=True, stop=True)
                                sc_toks[g] = k.inc(mi, "tS")
                                if g >= 1:
                                    qsc_tok[qt] = attv(g - 1)
                                k.wait(sca, "tS", sc_toks[g])
                                ei = None
                                for hh in range(2):
                                    ei = sca.activation(
                                        pbuf[:, pp * 2048 + hh * 512: pp * 2048 + (hh + 1) * 512],
                                        psS[:, h0 + hh * 512: h0 + (hh + 1) * 512],
                                        Act.Exp, bias=mcol[:, (hA + hh):(hA + hh) + 1], scale=0.125)
                                ex_toks[g] = k.inc(ei, "sS")
                                k.wait(vec, "sS", ex_toks[g])
                                mbase = 2047 + qt * 512 - kc * 128
                                vm = None
                                for hh in range(2):
                                    vm = vec.tensor_tensor(
                                        pbuf[:, pp * 2048 + hh * 512: pp * 2048 + (hh + 1) * 512],
                                        pbuf[:, pp * 2048 + hh * 512: pp * 2048 + (hh + 1) * 512],
                                        rtt[:, hh * 4224 + mbase: hh * 4224 + mbase + 512],
                                        Alu.mult)
                                pm_toks[g] = k.inc(vm, "vS")
                            qsc_tok[qt] = attv(GK - 1)
                            k.wait(vec, "tS")
                            if ysc_tok is not None:
                                k.wait(vec, "vS", ysc_tok)
                            if ycop_tok is not None:
                                k.wait(vec, "dS", ycop_tok)
                            k.inc(vec.tensor_copy(ystage[0:64, qt * 512:(qt + 1) * 512],
                                                  psY[0:64, 0:512]), "vS")
                            k.inc(vec.tensor_copy(ycop[0:64, 0:512], psY[0:64, 512:1024]), "vS")
                            yev_tok = k.inc(vec.tensor_copy(ycop[64:65, 0:1024], psY[64:65, 0:1024]), "vS")
                            k.wait(sync, "vS")
                            k.inc(sync.dma_start(out=ystage[64:128, qt * 512:(qt + 1) * 512],
                                                 in_=ycop[0:64, 0:512]), "dS", 16)
                            k.inc(sync.dma_start(out=zrep[0:1, qt * 512:(qt + 1) * 512],
                                                 in_=ycop[64:65, 0:512]), "dS", 16)
                            ycop_tok = k.inc(sync.dma_start(out=zrep[64:65, qt * 512:(qt + 1) * 512],
                                                            in_=ycop[64:65, 512:1024]), "dS", 16)
                        k.wait(vec, "dS", ycop_tok)
                        with nc.allow_low_precision(reason="1/Z in bf16 (0.2% noise, within bar)"):
                            k.inc(vec.reciprocal(zrep[0:1, :], zrep[0:1, :]), "vS")
                            k.inc(vec.reciprocal(zrep[64:65, :], zrep[64:65, :]), "vS")
                        k.wait(sync, "vS")
                        p2 = 1
                        while p2 < 64:
                            k.inc(sync.dma_start(out=zrep[p2:2 * p2, :], in_=zrep[0:p2, :]), "dS", 16)
                            k.inc(sync.dma_start(out=zrep[64 + p2:64 + 2 * p2, :],
                                                 in_=zrep[64:64 + p2, :]), "dS", 16)
                            k.wait(sync, "dS")
                            p2 *= 2
                        k.wait(vec, "dS")
                        ysc_tok = k.inc(vec.tensor_tensor(yfm[:, hp * T:(hp + 1) * T], ystage[:, :],
                                                          zrep[:, :], Alu.mult), "vS")
            if dbg == "ybis" and s == 1:
                k.barrier(ALL_ENG)
                k.inc(gp.dma_start(out=dbg_t[:, 0:4096], in_=yfm[:, 0:4096]), "dG", 16)
                k.wait(gp, "dG")
                k.barrier(ALL_ENG)
            if dbg == "yfm1" and s == 1:
                k.barrier([gp])
                k.inc(gp.dma_start(out=dbg_t[:, 0:DCH * T], in_=yfm[:, :]), "dG", 16)
                k.inc(gp.dma_start(out=dbg_t[:, 16384:16400], in_=mcol[:, 0:16]), "dG", 16)
                k.inc(gp.dma_start(out=dbg_t[0:16, 16400:16408], in_=nrm[:, :]), "dG", 16)
                k.wait(gp, "dG")
            k.barrier(ALL_ENG)
            with (
                nc.sbuf_tensor(f"woB{s}", [128, DCH * 1024], BF16) as wo_km,
                nc.sbuf_tensor(f"boc{s}", [128, 8], F32) as boc,
                nc.sbuf_tensor(f"ostB{s}", [128, 2 * 2048], F32) as ostage,
                nc.psum_tensor(f"psOB{s}", [128, 2048], F32) as psO,
            ):
                for ic in range(DCH):
                    k.inc(sync.dma_start(out=wo_km[:, ic * 1024:(ic + 1) * 1024],
                                         in_=dr["wout_scr"][ic * 128:(ic + 1) * 128, :]), "dS", 16)
                k.inc(sync.dma_start(out=boc[:, :],
                                     in_=bass.AP(w["l1_bout"], 0, [[1, 128], [128, 8]]), allow_slow_non_contiguous=True), "dS", 16)
                if dbg is not None:
                    print(f"ADDR s={s} yfm:", nc.lookup_mloc(yfm).addr,
                          "+", DCH * T * 2, "ost:", nc.lookup_mloc(ostage).addr,
                          "+", 4 * 2048 * 4, "wo:", nc.lookup_mloc(wo_km).addr,
                          "mcol:", nc.lookup_mloc(mcol).addr,
                          "boc:", nc.lookup_mloc(boc).addr)
                if dbg == "ybis" and s == 1:
                    k.barrier(ALL_ENG)
                    k.inc(gp.dma_start(out=dbg_t[:, 8192:12288], in_=yfm[:, 0:4096]), "dG", 16)
                    k.wait(gp, "dG")
                    k.barrier(ALL_ENG)
                osp = [None, None]
                est = [None, None]
                for oc in range(DCH):
                    if dbg == "oproj1w" and s == 1 and oc == 0:
                        k.barrier(ALL_ENG)
                        k.inc(gp.dma_start(out=dbg_t[:, 8192:12288], in_=yfm[:, 0:4096]), "dG", 16)
                        k.wait(gp, "dG")
                    if dbg == "oproj1" and s == 1 and oc == 1:
                        k.barrier(ALL_ENG)
                        k.inc(gp.dma_start(out=dbg_t[:, 2048:4096], in_=ostage[:, 0:2048]), "dG", 16)
                        k.inc(gp.dma_start(out=dbg_t[:, 4096:8192], in_=wo_km[:, 0:4096]), "dG", 16)
                        k.inc(gp.dma_start(out=dbg_t[:, 8192:12288], in_=yfm[:, 0:4096]), "dG", 16)
                        k.inc(gp.dma_start(out=dbg_t[:, 12288:12296], in_=boc[:, :]), "dG", 16)
                        k.wait(gp, "dG")
                    k.wait(ten, "dS")
                    if oc >= 1:
                        k.wait(ten, "sS", est[(oc - 1) % 2])
                    mi = None
                    for yc in range(DCH):
                        for tb in range(4):
                            mi = ten.matmul(psO[:, tb * 512:(tb + 1) * 512],
                                            wo_km[:, yc * 1024 + oc * 128: yc * 1024 + oc * 128 + 128],
                                            yfm[:, yc * T + tb * 512: yc * T + (tb + 1) * 512],
                                            start=(yc == 0), stop=(yc == DCH - 1))
                    mt = k.tok(ten, "tS")
                    ob = oc % 2
                    k.wait(sca, "tS", mt)
                    if oc >= 2:
                        k.wait(sca, "dA", osp[ob])
                    ei = sca.activation(ostage[:, ob * 2048:(ob + 1) * 2048], psO[:, :],
                                        Act.Identity, bias=boc[:, oc:oc + 1], scale=1.0)
                    est[ob] = k.tok(sca, "sS")
                    k.wait(sca, "sS", est[ob])
                    osp[ob] = k.inc(sca.dma_start(out=dr["attn_sp"][oc * 128:(oc + 1) * 128,
                                                                    s * T:(s + 1) * T],
                                                  in_=ostage[:, ob * 2048:(ob + 1) * 2048]), "dA", 16)
        k.barrier(ALL_ENG)

    l1_attn_seq(0, sub=stop_after if stop_after in ("l1qk", "l1v") else None)
    if stop_after in ("l1qk", "l1v"):
        if _stopper(stop_after):
            es.close()
            return nc
    if stop_after == "l1attn0":
        pass
    else:
        l1_attn_seq(1)
    if stop_after in ("l1attn", "l1attn0"):
        if _stopper(stop_after):
            es.close()
            return nc

    if dbg == "attn1":
        k.barrier([gp])
        for c in range(8):
            k.inc(gp.dma_start(out=dbg_t[:, c * 4096:(c + 1) * 4096],
                               in_=dr["attn_sp"][c * 128:(c + 1) * 128, :]), "dG", 16)
        k.wait(gp, "dG")

    def fp4_qdq_tile(src_ap, dst_bf, width, ctxbuf):
        nb = width // 16
        bm, an, t2, w2 = ctxbuf["bm"], ctxbuf["an"], ctxbuf["t2"], ctxbuf["w2"]
        Ri, sg = ctxbuf["Ri"], ctxbuf["sg"]
        k.wait(vec, "vS")
        k.inc(vec.tensor_reduce(bm[:, 0:nb].unsqueeze(2),
                                src_ap.rearrange("p (b e) -> p b e", e=16),
                                axis=AX.X, op=Alu.max, apply_absolute_value=True), "vS")
        k.wait(vec, "vS")
        k.inc(vec.tensor_scalar(bm[:, 0:nb], bm[:, 0:nb], 1.0 / 6.0, 1e-12, Alu.mult, Alu.max), "vS")
        k.wait(vec, "vS")
        k.inc(vec.reciprocal(bm[:, nb:2 * nb], bm[:, 0:nb]), "vS")
        rt = k.tok(vec, "vS")
        rsb = bm[:, nb:2 * nb].unsqueeze(2).broadcast_to((128, nb, 16))
        k.inc(vec.tensor_tensor(w2[:, 0:width].rearrange("p (b e) -> p b e", e=16),
                                src_ap.rearrange("p (b e) -> p b e", e=16), rsb, Alu.mult), "vS")
        nt = k.tok(vec, "vS")
        k.wait(sca, "vS", nt)
        k.inc(sca.activation(an[:, 0:width], w2[:, 0:width], Act.Abs), "sS")
        k.inc(sca.activation(sg[:, 0:width], w2[:, 0:width], Act.Sign), "sS")
        st = k.tok(sca, "sS")
        k.wait(vec, "sS", st)
        k.inc(vec.tensor_scalar(t2[:, 0:width], an[:, 0:width], 2.0, None, Alu.is_ge), "vS")
        k.inc(vec.tensor_scalar(t2[:, width:2 * width], an[:, 0:width], 4.0, -0.5, Alu.is_ge,
                                Alu.mult), "vS")
        k.wait(vec, "vS")
        k.inc(vec.scalar_tensor_tensor(w2[:, width:2 * width], t2[:, 0:width], -1.0,
                                       t2[:, width:2 * width], Alu.mult, Alu.add), "vS")
        k.wait(vec, "vS")
        k.inc(vec.tensor_scalar(w2[:, width:2 * width], w2[:, width:2 * width], 2.0, None,
                                Alu.add), "vS")
        k.wait(vec, "vS")
        k.inc(vec.tensor_tensor(w2[:, width:2 * width], an[:, 0:width], w2[:, width:2 * width],
                                Alu.mult), "vS")
        k.wait(vec, "vS")
        k.inc(vec.tensor_scalar(Ri[:, 0:width], w2[:, width:2 * width], 1.0, None, Alu.mult), "vS")
        k.wait(vec, "vS")
        k.inc(vec.tensor_scalar(t2[:, width:2 * width], t2[:, width:2 * width], -4.0, None,
                                Alu.mult), "vS")
        k.wait(vec, "vS")
        k.inc(vec.tensor_tensor(t2[:, 0:width], t2[:, 0:width], t2[:, width:2 * width], Alu.add), "vS")
        k.wait(vec, "vS")
        k.inc(vec.tensor_scalar(t2[:, 0:width], t2[:, 0:width], 1.0, None, Alu.add), "vS")
        k.wait(vec, "vS")
        k.inc(vec.tensor_tensor(w2[:, 0:width], Ri[:, 0:width], t2[:, 0:width], Alu.mult), "vS")
        k.wait(vec, "vS")
        sb = bm[:, 0:nb].unsqueeze(2).broadcast_to((128, nb, 16))
        k.inc(vec.tensor_tensor(sg[:, 0:width].rearrange("p (b e) -> p b e", e=16),
                                sg[:, 0:width].rearrange("p (b e) -> p b e", e=16), sb, Alu.mult), "vS")
        k.wait(vec, "vS")
        k.inc(vec.scalar_tensor_tensor(dst_bf, w2[:, 0:width], 0.5, sg[:, 0:width],
                                       Alu.mult, Alu.mult), "vS")
        return k.tok(vec, "vS")

    def fp4_weights(wsrc, scr, nchunks, width, tag):
        with (
            nc.sbuf_tensor(f"f4n{tag}", [128, width], F32) as wn,
            nc.sbuf_tensor(f"f4bm{tag}", [128, 2 * (width // 16)], F32) as bm,
            nc.sbuf_tensor(f"f4an{tag}", [128, width], F32) as an,
            nc.sbuf_tensor(f"f4t2{tag}", [128, 2 * width], F32) as t2,
            nc.sbuf_tensor(f"f4w2{tag}", [128, 2 * width], F32) as w2,
            nc.sbuf_tensor(f"f4Ri{tag}", [128, width], I32) as Ri,
            nc.sbuf_tensor(f"f4sg{tag}", [128, width], F32) as sg,
            nc.sbuf_tensor(f"f4o{tag}", [128, width], BF16) as wd,
        ):
            ctxb = {"bm": bm, "an": an, "t2": t2, "w2": w2, "Ri": Ri, "sg": sg}
            for oc in range(nchunks):
                k.wait(sync, "vS")
                ld = k.inc(sync.dma_start(out=wn[:, :], in_=wsrc[oc * 128:(oc + 1) * 128, :]),
                           "dS", 16)
                k.wait(vec, "dS", ld)
                ot = fp4_qdq_tile(wn[:, :], wd[:, :], width, ctxb)
                k.wait(sca, "vS", ot)
                k.wait(sca, "dA")
                k.inc(sca.dma_start(out=scr[oc * 128:(oc + 1) * 128, 0:width], in_=wd[:, :]), "dA", 16)
        k.barrier(ALL_ENG)

    with (
        nc.sbuf_tensor("q4n", [128, D], F32) as q4n,
        nc.sbuf_tensor("q4bm", [128, 2 * (D // 16)], F32) as q4bm,
        nc.sbuf_tensor("q4an", [128, D], F32) as q4an,
        nc.sbuf_tensor("q4t2", [128, 2 * D], F32) as q4t2,
        nc.sbuf_tensor("q4w2", [128, 2 * D], F32) as q4w2,
        nc.sbuf_tensor("q4Ri", [128, D], I32) as q4Ri,
        nc.sbuf_tensor("q4sg", [128, D], F32) as q4sg,
        nc.sbuf_tensor("q4o", [128, D], BF16) as q4o,
    ):
        ctxb = {"bm": q4bm, "an": q4an, "t2": q4t2, "w2": q4w2, "Ri": q4Ri, "sg": q4sg}
        for it in range(NT // 128):
            k.wait(sync, "vS")
            ld = k.inc(sync.dma_start(out=q4n[:, :], in_=dr["xn1f_park"][it * 128:(it + 1) * 128, :]),
                       "dS", 16)
            k.wait(vec, "dS", ld)
            ot = fp4_qdq_tile(q4n[:, :], q4o[:, :], D, ctxb)
            k.wait(sca, "vS", ot)
            k.wait(sca, "dA")
            k.inc(sca.dma_start(out=dr["xq1t_park"][it * 128:(it + 1) * 128, :], in_=q4o[:, :]),
                  "dA", 16)
    if dbg == "xq1d":
        k.barrier([gp])
        for i in range(32):
            k.inc(gp.dma_start(out=dbg_t[:, i * 1024:(i + 1) * 1024],
                               in_=dr["xq1t_park"][i * 128:(i + 1) * 128, :]), "dG", 16)
        k.wait(gp, "dG")
    k.barrier(ALL_ENG)
    if _stopper("xq1"):
        es.close()
        return nc

    fp4_weights(w["l1_wgate"], dr["wgate_scr"], 22, 1024, "g1")
    if _stopper("fw_g1"):
        es.close()
        return nc
    fp4_weights(w["l1_wup"], dr["wup_scr"], 22, 1024, "u1")

    IH = I1 // 2
    NCOL = [512, 512, 384]
    NOFF = [0, 512, 1024]
    with (
        nc.sbuf_tensor("wgkm", [128, DCH * IH], BF16) as wgkm,
        nc.sbuf_tensor("wukm", [128, DCH * IH], BF16) as wukm,
        nc.sbuf_tensor("xq1s", [128, 2 * DCH * 128], BF16) as xq1s,
        nc.sbuf_tensor("sgt", [128, IH], F32) as sgt,
        nc.sbuf_tensor("ht", [128, IH], F32) as ht,
        nc.sbuf_tensor("h4bm", [128, 2 * (IH // 16)], F32) as h4bm,
        nc.sbuf_tensor("h4an", [128, IH], F32) as h4an,
        nc.sbuf_tensor("h4t2", [128, 2 * IH], F32) as h4t2,
        nc.sbuf_tensor("h4w2", [128, 2 * IH], F32) as h4w2,
        nc.sbuf_tensor("h4Ri", [128, IH], I32) as h4Ri,
        nc.sbuf_tensor("h4sg", [128, IH], F32) as h4sg,
        nc.sbuf_tensor("h4o", [128, IH], BF16) as h4o,
        nc.psum_tensor("psG1", [128, 2048], F32) as psG1,
    ):
        ctxh = {"bm": h4bm, "an": h4an, "t2": h4t2, "w2": h4w2, "Ri": h4Ri, "sg": h4sg}
        for ihalf in range(2):
            k.barrier(ALL_ENG)
            for ic in range(DCH):
                k.inc(sync.dma_start(out=wgkm[:, ic * IH:(ic + 1) * IH],
                                     in_=dr["wgate_scr"][ihalf * IH:(ihalf + 1) * IH,
                                                         ic * 128:(ic + 1) * 128],
                                     transpose=True), "dS", 16)
                k.inc(sync.dma_start(out=wukm[:, ic * IH:(ic + 1) * IH],
                                     in_=dr["wup_scr"][ihalf * IH:(ihalf + 1) * IH,
                                                       ic * 128:(ic + 1) * 128],
                                     transpose=True), "dS", 16)
            wl3 = k.cnt["dS"]
            hq_tok = None
            for tc in range(NT // 128):
                bu = tc % 2
                if tc >= 2:
                    k.wait(sync, "tS")
                xql = None
                for ic in range(DCH):
                    xql = k.inc(sync.dma_start(
                        out=xq1s[:, bu * 1024 + ic * 128: bu * 1024 + (ic + 1) * 128],
                        in_=dr["xq1t_park"][tc * 128:(tc + 1) * 128, ic * 128:(ic + 1) * 128],
                        transpose=True), "dS", 16)
                k.wait(ten, "dS", xql)
                k.wait(ten, "dS", wl3)
                if hq_tok is not None:
                    k.wait(ten, "sS")
                    k.wait(ten, "vS")
                mi = None
                for ic in range(DCH):
                    for nn in range(3):
                        mi = ten.matmul(psG1[:, NOFF[nn]:NOFF[nn] + NCOL[nn]],
                                        xq1s[:, bu * 1024 + ic * 128: bu * 1024 + (ic + 1) * 128],
                                        wgkm[:, ic * IH + NOFF[nn]: ic * IH + NOFF[nn] + NCOL[nn]],
                                        start=(ic == 0), stop=(ic == DCH - 1))
                gt = k.tok(ten, "tS")
                k.wait(sca, "tS", gt)
                k.inc(sca.activation(sgt[:, :], psG1[:, 0:IH], Act.Silu), "sS")
                sil = k.tok(sca, "sS")
                k.wait(ten, "sS", sil)
                mi = None
                for ic in range(DCH):
                    for nn in range(3):
                        mi = ten.matmul(psG1[:, NOFF[nn]:NOFF[nn] + NCOL[nn]],
                                        xq1s[:, bu * 1024 + ic * 128: bu * 1024 + (ic + 1) * 128],
                                        wukm[:, ic * IH + NOFF[nn]: ic * IH + NOFF[nn] + NCOL[nn]],
                                        start=(ic == 0), stop=(ic == DCH - 1))
                ut = k.tok(ten, "tS")
                k.wait(vec, "tS", ut)
                k.wait(vec, "sS", sil)
                k.wait(vec, "vS")
                k.inc(vec.tensor_tensor(ht[:, :], sgt[:, :], psG1[:, 0:IH], Alu.mult), "vS")
                ot = fp4_qdq_tile(ht[:, :], h4o[:, :], IH, ctxh)
                k.wait(sca, "vS", ot)
                if hq_tok is not None:
                    k.wait(sca, "dA", hq_tok)
                hq_tok = k.inc(sca.dma_start(
                    out=dr["hq1t_park"][tc * 128:(tc + 1) * 128, ihalf * IH:(ihalf + 1) * IH],
                    in_=h4o[:, :]), "dA", 16)
    k.barrier(ALL_ENG)
    if _stopper("gateup"):
        es.close()
        return nc

    fp4_weights(w["l1_wdown"], dr["wdown_scr"], 8, I1, "d1")
    with (
        nc.sbuf_tensor("wd1km", [128, 22 * 1024], BF16) as wd1km,
        nc.sbuf_tensor("hkm", [128, 22 * 512], BF16) as hkm,
        nc.sbuf_tensor("eres2", [128, 4096], F32) as eres,
        nc.sbuf_tensor("exr2", [128, 4096], F32) as exr,
        nc.sbuf_tensor("eyt", [128, 2 * 1024], F32) as eyt,
        nc.psum_tensor("psD2", [128, 4096], F32) as psD,
    ):
        psE = psD
        for ic in range(22):
            k.inc(sync.dma_start(out=wd1km[:, ic * 1024:(ic + 1) * 1024],
                                 in_=dr["wdown_scr"][0:D, ic * 128:(ic + 1) * 128],
                                 transpose=True), "dS", 16)
        wld = k.cnt["dS"]
        res_tok = None
        ydma = None
        mt = None
        for tb in range(8):
            hl = None
            if mt is not None:
                k.wait(sync, "tS", mt)
            for ic in range(22):
                hl = k.inc(sync.dma_start(
                    out=hkm[:, ic * 512:(ic + 1) * 512],
                    in_=dr["hq1t_park"][tb * 512:(tb + 1) * 512, ic * 128:(ic + 1) * 128],
                    transpose=True), "dS", 16)
            k.wait(ten, "dS", hl)
            k.wait(ten, "dS", wld)
            if res_tok is not None:
                k.wait(ten, "vS", res_tok)
                k.wait(ten, "sS")
            mi = None
            for ocE in range(8):
                for ic in range(22):
                    mi = ten.matmul(psD[:, ocE * 512:(ocE + 1) * 512],
                                    wd1km[:, ic * 1024 + ocE * 128: ic * 1024 + ocE * 128 + 128],
                                    hkm[:, ic * 512:(ic + 1) * 512],
                                    start=(ic == 0), stop=(ic == 21))
            mt = k.tok(ten, "tS")
            la = k.inc(sync.dma_start(out=eres[:, :].rearrange("p (c e) -> p c e", e=512),
                                      in_=bass.AP(dr["attn_sp"], tb * 512,
                                                  [[NT, 128], [128 * NT, 8], [1, 512]])), "dS", 16)
            lx = k.inc(sync.dma_start(out=exr[:, :].rearrange("p (c e) -> p c e", e=512),
                                      in_=bass.AP(dr["x1_park"], tb * 512,
                                                  [[NT, 128], [128 * NT, 8], [1, 512]])), "dS", 16)
            k.wait(vec, "tS", mt)
            k.wait(vec, "dS", lx)
            k.wait(vec, "vS")
            k.inc(vec.tensor_tensor(eres[:, :], eres[:, :], psD[:, :], Alu.add), "vS")
            k.wait(vec, "vS")
            k.inc(vec.tensor_tensor(eres[:, :], eres[:, :], exr[:, :], Alu.add), "vS")
            res_tok = k.tok(vec, "vS")
            for st_ in range(4):
                hb = st_ % 2
                k.wait(ten, "vS", res_tok)
                k.wait(ten, "sS")
                ti = None
                for c in range(DCH):
                    ti = ten.transpose(psE[:, hb * 1024 + c * 128: hb * 1024 + (c + 1) * 128],
                                       eres[:, c * 512 + st_ * 128: c * 512 + st_ * 128 + 128],
                                       identF[:, :])
                tt2 = k.tok(ten, "tS")
                k.wait(sca, "tS", tt2)
                k.wait(sca, "dA")
                ei = sca.copy(eyt[:, hb * 1024:(hb + 1) * 1024], psE[:, hb * 1024:(hb + 1) * 1024])
                et = k.inc(ei, "sS")
                k.wait(sca, "sS", et)
                ydma = k.inc(sca.dma_start(
                    out=y_out[tb * 512 + st_ * 128:tb * 512 + st_ * 128 + 128, :],
                    in_=eyt[:, hb * 1024:(hb + 1) * 1024]), "dA", 16)
        k.wait(sca, "dA")
        k.wait(sync, "dA")

    if dbg == "wdq":
        k.barrier([gp])
        for i in range(24):
            k.inc(gp.dma_start(out=dbg_t[:, i * 1024:(i + 1) * 1024],
                               in_=dr["wup_scr"][i * 128:(i + 1) * 128, :]), "dG", 16)
        k.wait(gp, "dG")
    if dbg in ("up0", "gate0", "g20", "o0f"):
        spname = {"up0": "spill_up", "gate0": "spill_gate", "g20": "spill_gate2", "o0f": "spill_o"}[dbg]
        k.barrier([gp])
        for tb_ in range(2):
            k.inc(gp.dma_start(
                out=dbg_t[:, tb_ * 16384: tb_ * 16384 + 12288].rearrange("p (o e) -> p o e", e=512),
                in_=bass.AP(dr[spname], tb_ * 24 * 128 * 512,
                            [[512, 128], [128 * 512, 24], [1, 512]])), "dG", 16)
        k.inc(gp.dma_start(out=dbg_t[:, 16352:16384], in_=scal[:, 0:32]), "dG", 16)
        k.wait(gp, "dG")
    if dbg in ("x1", "xn1"):
        k.barrier([gp])
        if dbg == "x1":
            for i in range(32):
                k.inc(gp.dma_start(out=dbg_t[:, i * 1024:(i + 1) * 1024],
                                   in_=dr["x1tok_park"][i * 128:(i + 1) * 128, :]), "dG", 16)
        else:
            for half in range(8):
                k.inc(gp.dma_start(out=dbg_t[:, half * 4096:(half + 1) * 4096],
                                   in_=xnfm[:, half * 4096:(half + 1) * 4096]), "dG", 16)
        k.wait(gp, "dG")
    if dbg == "xn0":
        k.wait(gp, "sS")
        k.wait(gp, "dS")
        for half in range(8):
            k.inc(gp.dma_start(out=dbg_t[:, half * 4096:(half + 1) * 4096],
                               in_=xnfm[:, half * 4096:(half + 1) * 4096]), "dG", 16)
        k.wait(gp, "dG")
    elif dbg == "xq0":
        k.wait(gp, "vS")
        for half in range(8):
            k.inc(gp.dma_start(out=dbg_t[:, half * 4096:(half + 1) * 4096],
                               in_=xqfm[:, half * 4096:(half + 1) * 4096]), "dG", 16)
        k.wait(gp, "dG")
    elif dbg == "attn0":
        k.wait(gp, "dA")
        for c in range(8):
            k.inc(gp.dma_start(out=dbg_t[:, c * 4096:(c + 1) * 4096],
                               in_=dr["attn_sp"][c * 128:(c + 1) * 128, :]), "dG", 16)
        k.wait(gp, "dG")


    es.close()
    return nc


def host_consts():
    i = np.arange(HD // 2, dtype=np.float32)
    angles = np.power(np.float32(10000.0), 2.0 * i / HD).astype(np.float32)
    theta = np.arange(T, dtype=np.float32)[:, None] * angles[None, :]
    cos = np.cos(theta).astype(np.float32)
    sin = np.sin(theta).astype(np.float32)
    c2 = np.zeros((T, HD), np.float32)
    s2 = np.zeros((T, HD), np.float32)
    c2[:, 0::2] = cos
    c2[:, 1::2] = cos
    s2[:, 0::2] = -sin
    s2[:, 1::2] = sin
    dd = np.arange(-2047, 2048, dtype=np.float32)
    reld = np.sign(dd) * np.log1p(np.abs(dd))
    relrow = np.zeros((4224,), np.float32)
    relrow[:4095] = reld
    relrep = np.broadcast_to(relrow, (32, 4224)).copy()
    return c2, s2, relrep


def make_inmaps(inputs):
    c2, s2, relrep = host_consts()
    xs = np.ascontiguousarray(inputs["L0_attn_x"].astype(np.float32)).reshape(NCORES, NT, D)
    shared = {
        "l0_wq": inputs["L0_attn_w_q"], "l0_wk": inputs["L0_attn_w_k"],
        "l0_wv": inputs["L0_attn_w_v"], "l0_wo": inputs["L0_attn_w_o"],
        "l0_wup": inputs["L0_ffn_w_up"], "l0_wgate": inputs["L0_ffn_w_gate"],
        "l0_wdown": inputs["L0_ffn_w_down"],
        "l1_wqkv": inputs["L1_attn_w_qkv"], "l1_bqkv": inputs["L1_attn_b_qkv"].reshape(1, -1),
        "l1_wout": inputs["L1_attn_w_out"], "l1_bout": inputs["L1_attn_b_out"].reshape(1, -1),
        "l1_w1r": inputs["L1_attn_w1_rpe"], "l1_b1r": inputs["L1_attn_b1_rpe"].reshape(1, -1),
        "l1_w2r": inputs["L1_attn_w2_rpe"], "l1_b2r": inputs["L1_attn_b2_rpe"].reshape(1, -1),
        "l1_wgate": inputs["L1_ffn_w_gate"], "l1_wup": inputs["L1_ffn_w_up"],
        "l1_wdown": inputs["L1_ffn_w_down"],
        "rope_c2": c2, "rope_s2": s2, "relrep": relrep,
    }
    shared = {kk: np.ascontiguousarray(np.asarray(v).astype(np.float32)) for kk, v in shared.items()}
    return [dict(shared, x=np.ascontiguousarray(xs[c])) for c in range(NCORES)]


def kernel(**inputs):
    nc = build_program()
    in_maps = make_inmaps(inputs)
    res = run_bass_kernel_spmd(nc, in_maps, core_ids=list(range(NCORES)))
    out = np.stack([res.results[c]["y"] for c in range(NCORES)])
    return out.reshape(NCORES * B, T, D).astype(np.float32)


if __name__ == "__main__":
    build_program()
    print("build ok")


# version 03
# speedup vs baseline: 1.1476x; 1.0716x over previous
import numpy as np
from contextlib import ExitStack
import concourse.bass as bass
import concourse.mybir as mybir
from concourse.bass_utils import run_bass_kernel_spmd

F32 = mybir.dt.float32
I32 = mybir.dt.int32
BF16 = mybir.dt.bfloat16
Alu = mybir.AluOpType
Act = mybir.ActivationFunctionType
AX = mybir.AxisListType

NCORES = 8
B = 2
T = 2048
NT = B * T
D = 1024
H = 16
HD = 64
DCH = D // 128
I0 = 3072
I1 = 2816
QMAX = 32767.0
EPS = 1e-5

DBG_SHAPE = [128, 8 * 4096]


class K:
    def __init__(self, nc, es):
        self.nc = nc
        self.es = es
        self.sems = {}
        self.cnt = {}
        for n in ["dS", "dA", "dG", "vS", "tS", "sS", "gS", "cS"]:
            self.sems[n] = es.enter_context(nc.semaphore(n))
            self.cnt[n] = 0

    def inc(self, inst, name, n=1):
        inst.then_inc(self.sems[name], n)
        self.cnt[name] += n
        return self.cnt[name]

    def wait(self, eng, name, val=None):
        eng.wait_ge(self.sems[name], self.cnt[name] if val is None else val)

    def tok(self, eng, name, n=1):
        self.cnt[name] += n
        eng.drain().then_inc(self.sems[name], n)
        return self.cnt[name]

    def barrier(self, engines):
        for e in engines:
            for n in ["dS", "dA", "dG", "vS", "tS", "sS", "gS", "cS"]:
                if self.cnt[n]:
                    e.wait_ge(self.sems[n], self.cnt[n])


def doubling(k, q, dst, width, dsem, base=0, nrows=128):
    p = 1
    while p < nrows:
        k.inc(q.dma_start(out=dst[base + p:base + 2 * p, 0:width],
                          in_=dst[base:base + p, 0:width]), dsem, 16)
        k.wait(q, dsem)
        p *= 2


def build_program(dbg=None, stop_after=None):
    nc = bass.Bass("TRN2", target_bir_lowering=False, debug=False)
    es = ExitStack()
    k = K(nc, es)
    sync, vec, sca, ten, gp = nc.sync, nc.vector, nc.scalar, nc.tensor, nc.gpsimd
    ALL_ENG = [sync, vec, sca, ten, gp]

    x_in = nc.dram_tensor("x", [NT, D], F32, kind="ExternalInput")
    y_out = nc.dram_tensor("y", [NT, D], F32, kind="ExternalOutput")
    w = {}
    for name, shape in [
        ("l0_wq", [D, D]), ("l0_wk", [D, D]), ("l0_wv", [D, D]), ("l0_wo", [D, D]),
        ("l0_wup", [I0, D]), ("l0_wgate", [I0, D]), ("l0_wdown", [D, I0]),
        ("l1_wqkv", [D, 3 * D]), ("l1_bqkv", [1, 3 * D]),
        ("l1_wout", [D, D]), ("l1_bout", [1, D]),
        ("l1_w1r", [1, 32]), ("l1_b1r", [1, 32]), ("l1_w2r", [32, 16]), ("l1_b2r", [1, 16]),
        ("l1_wgate", [I1, D]), ("l1_wup", [I1, D]), ("l1_wdown", [D, I1]),
        ("rope_c2", [T, HD]), ("rope_s2", [T, HD]),
        ("relrep", [32, 4224]),
    ]:
        w[name] = nc.dram_tensor(name, shape, F32, kind="ExternalInput")

    dr = {}
    for name, shape, dt in [
        ("spill_up", [8 * 24 * 128, 512], BF16),
        ("spill_gate", [8 * 24 * 128, 512], BF16),
        ("spill_gate2", [8 * 24 * 128, 512], BF16),
        ("spill_o", [8 * 24 * 128, 512], BF16),
        ("attn_sp", [DCH * 128, NT], F32),
        ("x1_park", [DCH * 128, NT], F32),
        ("xnt_park", [NT, D], BF16),
        ("qkt_park", [2 * T, D], BF16),
        ("wb_scr", [4 * D, D], BF16),
        ("wup_scr", [I0, D], BF16),
        ("wgate_scr", [I0, D], BF16),
        ("wdown_scr", [D, I0], BF16),
        ("x1tok_park", [NT, D], F32),
        ("xfm_park", [DCH * 128, NT], F32),
        ("exptab", [16, 4224], BF16),
        ("xn1f_park", [NT, D], F32),
        ("qk1_park", [2 * DCH * 128, T], BF16),
        ("wqkv_scr", [D, 3 * D], BF16),
        ("wout_scr", [D, D], BF16),
        ("xq1t_park", [NT, D], BF16),
        ("bounce16", [1, 16], F32),
        ("hq1t_park", [NT, I1], BF16),
    ]:
        dr[name] = nc.dram_tensor(name, shape, dt)
    for j in range(4):
        dr[f"cc_in{j}"] = nc.dram_tensor(f"cc_in{j}", [1, 4], F32)
        dr[f"cc_out{j}"] = nc.dram_tensor(f"cc_out{j}", [1, 4], F32)

    dbg_t = nc.dram_tensor("dbg", DBG_SHAPE, F32, kind="ExternalOutput") if dbg else None

    def S(name, shape, dt):
        return es.enter_context(nc.sbuf_tensor(name, shape, dt))

    xnfm = S("xnfm", [128, DCH * NT], BF16)
    ident = S("ident", [128, 128], BF16)
    identF = S("identF", [128, 128], F32)
    iia = S("iia", [128, 128], I32)
    scal = S("scal", [128, 32], F32)
    gmax = S("gmax", [128, 16], F32)
    row = S("rowA", [1, 32], F32)
    lnr = S("lnr", [128, 16], F32)

    C_CX, C_SX = 0, 1

    blk = es.enter_context(nc.Block())

    k.inc(gp.iota(iia[:, :], [[1, 128]], channel_multiplier=-1,
                  allow_small_or_imprecise_dtypes=True), "gS")
    for j, wn in enumerate(["l0_wq", "l0_wk", "l0_wv", "l0_wo"]):
        k.inc(gp.dma_start(out=dr["wb_scr"][j * D:(j + 1) * D, :], in_=w[wn][:, :]), "dG", 16)
    k.wait(vec, "gS")
    k.inc(vec.tensor_scalar(ident[:, :], iia[:, :], 0.0, 1.0, Alu.is_equal, Alu.mult), "vS")
    k.inc(vec.tensor_scalar(identF[:, :], iia[:, :], 0.0, 1.0, Alu.is_equal, Alu.mult), "vS")
    k.inc(vec.memset(gmax[:, :], 0.0), "vS")
    k.inc(vec.memset(scal[:, :], 0.0), "vS")
    k.inc(vec.memset(row[:, :], 0.0), "vS")

    def ln_pass(tag, src_dram, gmax_col, dst_fm, park_xfm=False, park_f32=None):
        with (
            nc.sbuf_tensor(f"xst{tag}", [128, 2 * D], F32) as xstage,
            nc.sbuf_tensor(f"xnt{tag}", [128, 2 * D], BF16) as xnt,
            nc.sbuf_tensor(f"xfs{tag}", [128, 2 * D], F32) as xfs,
            nc.psum_tensor(f"psL{tag}", [128, 2048], F32) as psL,
        ):
            park_tok = []
            stats_tok = []
            xf_tok = []
            xft_tok = []
            for it in range(NT // 128):
                buf = it % 2
                if it >= 2:
                    k.wait(sync, "vS", stats_tok[it - 2])
                    if park_xfm:
                        k.wait(sync, "tS", xft_tok[it - 2])
                k.inc(sync.dma_start(out=xstage[:, buf * D:(buf + 1) * D],
                                     in_=src_dram[it * 128:(it + 1) * 128, :]), "dS", 16)
                xt = xstage[:, buf * D:(buf + 1) * D]
                k.wait(vec, "dS")
                k.inc(vec.bn_stats(lnr[:, 0:6], xt[:, 0:512]), "vS")
                k.inc(vec.bn_stats(lnr[:, 6:12], xt[:, 512:1024]), "vS")
                k.wait(vec, "vS")
                k.inc(vec.bn_aggr(lnr[:, 12:14], lnr[:, 0:12]), "vS")
                k.wait(vec, "vS")
                k.inc(vec.tensor_scalar(lnr[:, 3:4], lnr[:, 12:13], -1.0, None, Alu.mult), "vS")
                k.inc(vec.tensor_scalar(lnr[:, 6:7], lnr[:, 13:14], EPS, None, Alu.add), "vS")
                k.wait(vec, "vS")
                k.inc(vec.reciprocal(lnr[:, 7:8], lnr[:, 6:7]), "vS")
                k.wait(sca, "vS")
                k.inc(sca.activation(lnr[:, 8:9], lnr[:, 7:8], Act.Sqrt), "sS")
                k.wait(vec, "sS")
                if it >= 2:
                    k.wait(vec, "dA", park_tok[it - 2])
                cxn = k.inc(vec.tensor_scalar(xnt[:, buf * D:(buf + 1) * D], xt,
                                              lnr[:, 3:4], lnr[:, 8:9], Alu.add, Alu.mult), "vS")
                if park_f32 is not None:
                    k.inc(vec.tensor_scalar(xfs[:, buf * D:(buf + 1) * D], xt,
                                            lnr[:, 3:4], lnr[:, 8:9], Alu.add, Alu.mult), "vS")
                    cxn = k.tok(vec, "vS")
                    k.wait(sca, "vS", cxn)
                    k.wait(sca, "dA")
                    k.inc(sca.dma_start(out=park_f32[it * 128:(it + 1) * 128, :],
                                        in_=xfs[:, buf * D:(buf + 1) * D]), "dA", 16)
                stats_tok.append(cxn)
                k.wait(vec, "vS", cxn)
                k.inc(vec.tensor_reduce(lnr[:, 9:10], xnt[:, buf * D:(buf + 1) * D], axis=AX.X,
                                        op=Alu.max, apply_absolute_value=True), "vS")
                k.wait(vec, "vS")
                k.inc(vec.tensor_tensor(gmax[:, gmax_col:gmax_col + 1],
                                        gmax[:, gmax_col:gmax_col + 1], lnr[:, 9:10], Alu.max), "vS")
                k.wait(sca, "vS", cxn)
                park_tok.append(k.inc(sca.dma_start(out=dr["xnt_park"][it * 128:(it + 1) * 128, :],
                                                    in_=xnt[:, buf * D:(buf + 1) * D]), "dA", 16))
                if park_xfm:
                    k.wait(ten, "vS", cxn)
                    if it >= 2:
                        k.wait(ten, "sS", xf_tok[it - 2])
                    ti = None
                    for c in range(DCH):
                        ti = ten.transpose(psL[:, buf * 1024 + c * 128: buf * 1024 + (c + 1) * 128],
                                           xstage[:, buf * D + c * 128: buf * D + (c + 1) * 128],
                                           identF[:, :])
                    tt_ = k.tok(ten, "tS")
                    xft_tok.append(tt_)
                    k.wait(sca, "tS", tt_)
                    k.wait(sca, "dA")
                    ei = sca.copy(xfs[:, buf * D:(buf + 1) * D], psL[:, buf * 1024:(buf + 1) * 1024])
                    xf_tok.append(k.inc(ei, "sS"))
                    k.wait(sca, "sS", xf_tok[-1])
                    for c in range(DCH):
                        k.inc(sca.dma_start(
                            out=dr["xfm_park"][c * 128:(c + 1) * 128, it * 128:(it + 1) * 128],
                            in_=xfs[:, buf * D + c * 128: buf * D + (c + 1) * 128]), "dA", 16)
            k.wait(sync, "dA")
            for c in range(DCH):
                k.inc(sync.dma_start(out=dst_fm[:, c * NT:(c + 1) * NT],
                                     in_=dr["xnt_park"][:, c * 128:(c + 1) * 128],
                                     transpose=True), "dS", 16)

    def _stopper(name):
        if stop_after == name:
            k.barrier(ALL_ENG)
            k.inc(gp.dma_start(out=y_out[0:128, :], in_=xnfm[:, 0:D]), "dG", 16)
            k.wait(gp, "dG")
            return True
        return False

    ln_pass("0", x_in, 0, xnfm, park_xfm=True)

    k.wait(gp, "vS")
    k.inc(gp.tensor_reduce(row[0:1, 0:1], gmax[:, 0:1], axis=AX.C, op=Alu.max), "gS")
    k.wait(vec, "gS")
    k.inc(vec.tensor_scalar(row[0:1, 1:2], row[0:1, 0:1], 1.0 / QMAX, 1e-12, Alu.mult, Alu.max), "vS")
    k.wait(gp, "vS")
    k.inc(gp.dma_start(out=dr["cc_in0"][0:1, 0:4], in_=row[0:1, 0:4]), "dG", 16)
    k.wait(gp, "dG")
    k.inc(gp.collective_compute("AllReduce", Alu.max, replica_groups=[list(range(NCORES))],
                                ins=[dr["cc_in0"].ap()], outs=[dr["cc_out0"].ap()]), "cS")
    k.wait(gp, "cS")
    k.inc(gp.dma_start(out=row[0:1, 4:8], in_=dr["cc_out0"][0:1, 0:4]), "dG", 16)
    k.wait(vec, "dG")
    k.inc(vec.tensor_copy(scal[0:1, C_SX:C_SX + 1], row[0:1, 5:6]), "vS")
    k.inc(vec.reciprocal(scal[0:1, C_CX:C_CX + 1], row[0:1, 5:6]), "vS")

    if _stopper("ln0"):
        es.close()
        return nc
    def l0_attn_seq(s):
        k.barrier(ALL_ENG)
        with (
            nc.sbuf_tensor(f"yfm{s}", [128, DCH * T], BF16) as yfm,
            nc.sbuf_tensor(f"qn2{s}", [128, 16], F32) as qn2,
            nc.sbuf_tensor(f"kn2{s}", [128, 16], F32) as kn2,
            nc.sbuf_tensor(f"mcol{s}", [128, 16], F32) as mcol,
        ):
            k.wait(vec, "vS")
            k.inc(vec.memset(qn2[:, :], 0.0), "vS")
            k.inc(vec.memset(kn2[:, :], 0.0), "vS")
            with (
                nc.sbuf_tensor(f"kfm{s}", [128, DCH * T], BF16) as kfm,
                nc.sbuf_tensor(f"v65_{s}", [128, 16 * 16 * 65], BF16) as v65,
            ):
                k.wait(vec, "vS")
                k.inc(vec.memset(v65[:, :], 1.0), "vS")
                with (
                    nc.sbuf_tensor(f"wkm{s}", [128, 2 * DCH * 1024], BF16) as wkm,
                    nc.sbuf_tensor(f"c2a{s}", [128, 16 * 64], BF16) as c2a,
                    nc.sbuf_tensor(f"s2a{s}", [128, 16 * 64], BF16) as s2a,
                    nc.sbuf_tensor(f"qkrot{s}", [128, 2 * 1024], BF16) as qkrot,
                    nc.sbuf_tensor(f"sqst{s}", [128, 1024], BF16) as sqst,
                    nc.psum_tensor(f"psP{s}", [128, 3072], F32) as psP,
                ):
                    k.wait(sync, "dG")
                    for j in range(2):
                        for ic in range(DCH):
                            k.inc(sync.dma_start(
                                out=wkm[:, (j * DCH + ic) * 1024:(j * DCH + ic + 1) * 1024],
                                in_=dr["wb_scr"][j * D:(j + 1) * D, ic * 128:(ic + 1) * 128],
                                transpose=True), "dS", 16)
                    k.inc(gp.dma_start(out=c2a[:, :], in_=bass.AP(w["rope_c2"], 0,
                        [[64, 128], [8192, 16], [1, 64]])), "dG", 16)
                    k.inc(gp.dma_start(out=s2a[:, :], in_=bass.AP(w["rope_s2"], 0,
                        [[64, 128], [8192, 16], [1, 64]])), "dG", 16)
                    wload_tok = k.cnt["dS"]
                    k.wait(vec, "dG")

                    def qk_mms(tc, wslice, psslice, tok0):
                        mi = None
                        for c in range(DCH):
                            stat = xnfm[:, c * NT + tok0: c * NT + tok0 + 128]
                            for j, nh in [(jj, nn) for jj in wslice for nn in range(2)]:
                                mi = ten.matmul(
                                    psP[:, psslice[j] * 1024 + nh * 512: psslice[j] * 1024 + (nh + 1) * 512],
                                    stat,
                                    wkm[:, (j * DCH + c) * 1024 + nh * 512:
                                        (j * DCH + c) * 1024 + (nh + 1) * 512],
                                    start=(c == 0), stop=(c == DCH - 1))
                        return mi

                    rope_tok = []
                    park_tok = []
                    for tc in range(16):
                        tok0 = s * T + tc * 128
                        k.wait(ten, "dS", wload_tok)
                        if tc >= 1:
                            k.wait(ten, "vS", rope_tok[tc - 1])
                        mt = k.inc(qk_mms(tc, [0, 1], {0: 0, 1: 1}, tok0), "tS")
                        k.wait(vec, "tS", mt)
                        if tc >= 2:
                            k.wait(vec, "dA", park_tok[2 * (tc - 2) + 1])
                        c2s = c2a[:, tc * 64:(tc + 1) * 64].unsqueeze(1).broadcast_to((128, 16, 64))
                        s2s = s2a[:, tc * 64:(tc + 1) * 64].unsqueeze(1).broadcast_to((128, 16, 64))
                        for j in range(2):
                            src = psP[:, j * 1024:(j + 1) * 1024]
                            t1 = psP[:, 2048:3072]
                            k.wait(vec, "vS")
                            k.inc(vec.tensor_tensor(t1.rearrange("p (h d) -> p h d", d=64),
                                                    src.rearrange("p (h d) -> p h d", d=64), c2s,
                                                    Alu.mult), "vS")
                            srcp = src.rearrange("p (h i two) -> p h i two", h=16, two=2)
                            s2p = s2s.rearrange("p h (i two) -> p h i two", two=2)
                            dst = qkrot[:, j * 1024:(j + 1) * 1024].rearrange("p (h i two) -> p h i two", h=16, two=2)
                            k.inc(vec.tensor_tensor(dst[:, :, :, 0], srcp[:, :, :, 1], s2p[:, :, :, 0],
                                                    Alu.mult), "vS")
                            k.inc(vec.tensor_tensor(dst[:, :, :, 1], srcp[:, :, :, 0], s2p[:, :, :, 1],
                                                    Alu.mult), "vS")
                            k.wait(vec, "vS")
                            k.inc(vec.tensor_tensor(qkrot[:, j * 1024:(j + 1) * 1024],
                                                    qkrot[:, j * 1024:(j + 1) * 1024], t1, Alu.add), "vS")
                        rope_tok.append(k.cnt["vS"])
                        for j, acc in [(0, qn2), (1, kn2)]:
                            k.wait(sca, "vS")
                            k.inc(sca.activation(sqst[:, :], qkrot[:, j * 1024:(j + 1) * 1024],
                                                 Act.Square), "sS")
                            k.wait(vec, "sS")
                            k.inc(vec.tensor_reduce(lnr[:, 0:16].unsqueeze(2),
                                                    sqst[:, :].rearrange("p (h d) -> p h d", d=64),
                                                    axis=AX.X, op=Alu.add), "vS")
                            k.wait(vec, "vS")
                            k.inc(vec.tensor_tensor(acc[:, :], acc[:, :], lnr[:, 0:16], Alu.max), "vS")
                        k.wait(sca, "vS")
                        park_tok.append(k.inc(sca.dma_start(out=dr["qkt_park"][tc * 128:(tc + 1) * 128, :],
                                                            in_=qkrot[:, 0:1024]), "dA", 16))
                        park_tok.append(k.inc(sca.dma_start(
                            out=dr["qkt_park"][T + tc * 128:T + (tc + 1) * 128, :],
                            in_=qkrot[:, 1024:2048]), "dA", 16))
                    k.wait(sync, "tS")
                    for ic in range(DCH):
                        k.inc(sync.dma_start(out=wkm[:, ic * 1024:(ic + 1) * 1024],
                                             in_=dr["wb_scr"][2 * D:3 * D, ic * 128:(ic + 1) * 128],
                                             transpose=True), "dS", 16)
                    vload_tok = k.cnt["dS"]
                    vev_tok = []
                    for tc in range(16):
                        tok0 = s * T + tc * 128
                        k.wait(ten, "dS", vload_tok)
                        if tc >= 2:
                            k.wait(ten, "vS", vev_tok[tc - 2])
                        mi = None
                        for c in range(DCH):
                            stat = xnfm[:, c * NT + tok0: c * NT + tok0 + 128]
                            for nh in range(2):
                                mi = ten.matmul(psP[:, (tc % 2) * 1024 + nh * 512:
                                                     (tc % 2) * 1024 + (nh + 1) * 512],
                                                stat,
                                                wkm[:, c * 1024 + nh * 512: c * 1024 + (nh + 1) * 512],
                                                start=(c == 0), stop=(c == DCH - 1))
                        mt = k.inc(mi, "tS")
                        k.wait(vec, "tS", mt)
                        vdst = v65[:, tc * 1040: (tc + 1) * 1040].rearrange("p (h e) -> p h e", e=65)
                        vev_tok.append(k.inc(vec.tensor_copy(
                            vdst[:, :, 0:64],
                            psP[:, (tc % 2) * 1024:(tc % 2 + 1) * 1024].rearrange("p (h d) -> p h d", d=64)),
                            "vS"))
                    k.wait(sync, "dA")
                    for c in range(DCH):
                        k.inc(sync.dma_start(out=kfm[:, c * T:(c + 1) * T],
                                             in_=dr["qkt_park"][T:2 * T, c * 128:(c + 1) * 128],
                                             transpose=True), "dS", 16)
                    k.wait(gp, "vS")
                    k.inc(gp.tensor_reduce(row[0:1, 0:16], qn2[:, :], axis=AX.C, op=Alu.max), "gS")
                    k.wait(gp, "gS")
                    k.inc(gp.tensor_reduce(row[0:1, 16:32], kn2[:, :], axis=AX.C, op=Alu.max), "gS")
                    k.wait(vec, "gS")
                    k.inc(vec.tensor_tensor(row[0:1, 0:16], row[0:1, 0:16], row[0:1, 16:32],
                                            Alu.mult), "vS")
                    k.wait(sca, "vS")
                    k.inc(sca.activation(row[0:1, 0:16], row[0:1, 0:16], Act.Sqrt), "sS")
                    k.wait(vec, "sS")
                    k.inc(vec.tensor_scalar(mcol[0:1, 0:16], row[0:1, 0:16], -0.126, None, Alu.mult), "vS")
                    k.wait(sync, "vS")
                    doubling(k, sync, mcol, 16, "dS")

                k.barrier(ALL_ENG)
                with (
                    nc.sbuf_tensor(f"qbuf{s}", [128, 2 * 512], BF16) as qbuf,
                    nc.sbuf_tensor(f"pbuf{s}", [128, 2 * 3072], BF16) as pbuf,
                    nc.sbuf_tensor(f"yst{s}", [128, T], F32) as ystage,
                    nc.sbuf_tensor(f"ycop{s}", [128, 1024], F32) as ycop,
                    nc.sbuf_tensor(f"zrep{s}", [128, T], F32) as zrep,
                    nc.psum_tensor(f"psS{s}", [128, 3072], F32) as psS,
                    nc.psum_tensor(f"psY{s}", [128, 1024], F32) as psY,
                ):
                    gsz = [3, 3, 3, 3, 3, 1]
                    gof = [0, 3, 6, 9, 12, 15]
                    NG = 6
                    yev_tok = None
                    ysc_tok = None
                    ycop_tok = None
                    for hp in range(8):
                        hA = 2 * hp
                        qld_tok = [None] * 4
                        qsc_tok = [None] * 4
                        for qt in range(4):
                            qb = qt % 2
                            if qt >= 2:
                                k.wait(sync, "tS", qsc_tok[qt - 2])
                            qld_tok[qt] = k.inc(sync.dma_start(
                                out=qbuf[:, qb * 512:(qb + 1) * 512],
                                in_=dr["qkt_park"][qt * 512:(qt + 1) * 512, hp * 128:(hp + 1) * 128],
                                transpose=True), "dS", 16)
                            GK = 16
                            sc_toks = [None] * GK
                            ex_toks = [None] * GK

                            def attv0(g):
                                kc = g
                                pp = g % 2
                                k.wait(ten, "sS", ex_toks[g])
                                mv = None
                                for hh in range(2):
                                    mv = ten.matmul(
                                        psY[0:65, hh * 512:(hh + 1) * 512],
                                        v65[:, kc * 1040 + (hA + hh) * 65:
                                            kc * 1040 + (hA + hh) * 65 + 65],
                                        pbuf[:, pp * 3072 + hh * 512:
                                             pp * 3072 + (hh + 1) * 512],
                                        start=(g == 0), stop=(g == GK - 1))
                                return k.inc(mv, "tS")

                            for g in range(GK):
                                kc = g
                                pp = g % 2
                                h0 = pp * 1024
                                k.wait(ten, "dS", qld_tok[qt])
                                if g >= 2:
                                    k.wait(ten, "sS", ex_toks[g - 2])
                                else:
                                    k.wait(ten, "sS")
                                    if g == 0 and yev_tok is not None:
                                        k.wait(ten, "vS", yev_tok)
                                mi = None
                                for hh in range(2):
                                    mi = ten.matmul(
                                        psS[:, h0 + hh * 512: h0 + (hh + 1) * 512],
                                        kfm[hh * 64:(hh + 1) * 64,
                                            hp * T + kc * 128: hp * T + (kc + 1) * 128],
                                        qbuf[hh * 64:(hh + 1) * 64, qb * 512:(qb + 1) * 512],
                                        start=True, stop=True)
                                sc_toks[g] = k.inc(mi, "tS")
                                if g >= 1:
                                    qsc_tok[qt] = attv0(g - 1)
                                k.wait(sca, "tS", sc_toks[g])
                                ei = None
                                for hh in range(2):
                                    ei = sca.activation(
                                        pbuf[:, pp * 3072 + hh * 512: pp * 3072 + (hh + 1) * 512],
                                        psS[:, h0 + hh * 512: h0 + (hh + 1) * 512],
                                        Act.Exp, bias=mcol[:, (hA + hh):(hA + hh) + 1], scale=0.125)
                                ex_toks[g] = k.inc(ei, "sS")
                            qsc_tok[qt] = attv0(GK - 1)
                            k.wait(vec, "tS")
                            if ysc_tok is not None:
                                k.wait(vec, "vS", ysc_tok)
                            if ycop_tok is not None:
                                k.wait(vec, "dS", ycop_tok)
                            k.inc(vec.tensor_copy(ystage[0:64, qt * 512:(qt + 1) * 512],
                                                  psY[0:64, 0:512]), "vS")
                            k.inc(vec.tensor_copy(ycop[0:64, 0:512], psY[0:64, 512:1024]), "vS")
                            yev_tok = k.inc(vec.tensor_copy(ycop[64:65, 0:1024], psY[64:65, 0:1024]), "vS")
                            k.wait(sync, "vS")
                            k.inc(sync.dma_start(out=ystage[64:128, qt * 512:(qt + 1) * 512],
                                                 in_=ycop[0:64, 0:512]), "dS", 16)
                            k.inc(sync.dma_start(out=zrep[0:1, qt * 512:(qt + 1) * 512],
                                                 in_=ycop[64:65, 0:512]), "dS", 16)
                            ycop_tok = k.inc(sync.dma_start(out=zrep[64:65, qt * 512:(qt + 1) * 512],
                                                            in_=ycop[64:65, 512:1024]), "dS", 16)
                        k.wait(vec, "dS", ycop_tok)
                        k.inc(vec.reciprocal(zrep[0:1, :], zrep[0:1, :]), "vS")
                        k.inc(vec.reciprocal(zrep[64:65, :], zrep[64:65, :]), "vS")
                        k.wait(sync, "vS")
                        p2 = 1
                        while p2 < 64:
                            k.inc(sync.dma_start(out=zrep[p2:2 * p2, :], in_=zrep[0:p2, :]), "dS", 16)
                            k.inc(sync.dma_start(out=zrep[64 + p2:64 + 2 * p2, :],
                                                 in_=zrep[64:64 + p2, :]), "dS", 16)
                            k.wait(sync, "dS")
                            p2 *= 2
                        k.wait(vec, "dS")
                        ysc_tok = k.inc(vec.tensor_tensor(yfm[:, hp * T:(hp + 1) * T], ystage[:, :],
                                                          zrep[:, :], Alu.mult), "vS")
                    if dbg == "yz0" and s == 0:
                        k.barrier([gp])
                        k.inc(gp.dma_start(out=dbg_t[:, 0:2048], in_=ystage[:, :]), "dG", 16)
                        k.inc(gp.dma_start(out=dbg_t[:, 2048:4096], in_=zrep[:, :]), "dG", 16)
                        k.inc(gp.dma_start(out=dbg_t[:, 8192:8192 + 16], in_=mcol[:, 0:16]), "dG", 16)
                        k.wait(gp, "dG")
            if dbg == "yfm0" and s == 0:
                k.barrier([gp])
                for c in range(8):
                    k.inc(gp.dma_start(out=dbg_t[:, c * 2048:(c + 1) * 2048],
                                       in_=yfm[:, c * T:(c + 1) * T]), "dG", 16)
                k.wait(gp, "dG")
            k.barrier(ALL_ENG)
            with (
                nc.sbuf_tensor(f"wo{s}", [128, DCH * 1024], BF16) as wo_km,
                nc.sbuf_tensor(f"ost{s}", [128, 2 * 2048], F32) as ostage,
                nc.psum_tensor(f"psO{s}", [128, 2048], F32) as psO,
            ):
                for ic in range(DCH):
                    k.inc(sync.dma_start(out=wo_km[:, ic * 1024:(ic + 1) * 1024],
                                         in_=dr["wb_scr"][3 * D:4 * D, ic * 128:(ic + 1) * 128],
                                         transpose=True), "dS", 16)
                osp = [None, None]
                est = [None, None]
                for oc in range(DCH):
                    k.wait(ten, "dS")
                    if oc >= 1:
                        k.wait(ten, "sS", est[(oc - 1) % 2])
                    mi = None
                    for yc in range(DCH):
                        for tb in range(4):
                            mi = ten.matmul(psO[:, tb * 512:(tb + 1) * 512],
                                            wo_km[:, yc * 1024 + oc * 128: yc * 1024 + oc * 128 + 128],
                                            yfm[:, yc * T + tb * 512: yc * T + (tb + 1) * 512],
                                            start=(yc == 0), stop=(yc == DCH - 1))
                    mt = k.inc(mi, "tS")
                    ob = oc % 2
                    k.wait(sca, "tS", mt)
                    if oc >= 2:
                        k.wait(sca, "dA", osp[ob])
                    est[ob] = k.inc(sca.copy(ostage[:, ob * 2048:(ob + 1) * 2048], psO[:, :]), "sS")
                    k.wait(sca, "sS", est[ob])
                    osp[ob] = k.inc(sca.dma_start(out=dr["attn_sp"][oc * 128:(oc + 1) * 128,
                                                                    s * T:(s + 1) * T],
                                                  in_=ostage[:, ob * 2048:(ob + 1) * 2048]), "dA", 16)

    l0_attn_seq(0)
    if _stopper("l0a0"):
        es.close()
        return nc
    if dbg not in ("qpark0", "yfm0", "yz0"):
        l0_attn_seq(1)
    if dbg == "qpark0":
        k.barrier([gp])
        for i in range(32):
            k.inc(gp.dma_start(out=dbg_t[:, i * 1024:(i + 1) * 1024],
                               in_=dr["qkt_park"][i * 128:(i + 1) * 128, :]), "dG", 16)
        k.wait(gp, "dG")

    if _stopper("l0a1"):
        es.close()
        return nc
    k.barrier(ALL_ENG)
    xq_cm = nc.sbuf_tensor("xqfm", [128, DCH * NT], BF16)
    xqfm = xq_cm.__enter__()
    k.wait(sync, "vS")
    doubling(k, sync, scal, 32, "dS")
    with nc.sbuf_tensor("xqi", [128, 2048], I32) as xqi:
        k.wait(vec, "dS")
        for st in range(DCH * NT // 2048):
            cols = slice(st * 2048, (st + 1) * 2048)
            k.wait(vec, "vS")
            ci = k.inc(vec.tensor_scalar(xqi[:, :], xnfm[:, cols], scal[:, C_CX:C_CX + 1], None,
                                         Alu.mult), "vS")
            k.wait(vec, "vS", ci)
            k.inc(vec.tensor_copy(xqfm[:, cols], xqi[:, :]), "vS")

    if _stopper("xq0s"):
        es.close()
        return nc
    C_C2U, C_C2G, C_SG, C_C2G2, C_C2O, C_SO, C_SU, C_SG2 = 2, 3, 4, 5, 6, 7, 8, 9

    def wqdq_lpbq(wsrc, scr, nchunks, width, blk_sz, qmax_w, tag):
        nb = width // blk_sz
        with (
            nc.sbuf_tensor(f"wn{tag}", [128, 2 * width], F32) as wn,
            nc.sbuf_tensor(f"wb{tag}", [128, 2 * nb], F32) as wbm,
            nc.sbuf_tensor(f"wi{tag}", [128, width], I32) as wi,
            nc.sbuf_tensor(f"wd{tag}", [128, 2 * width], BF16) as wd,
        ):
            out_tok = []
            for oc in range(nchunks):
                bu = oc % 2
                if oc >= 2:
                    k.wait(sync, "vS", out_tok[oc - 2])
                k.inc(sync.dma_start(out=wn[:, bu * width:(bu + 1) * width],
                                     in_=wsrc[oc * 128:(oc + 1) * 128, :]), "dS", 16)
                wt = wn[:, bu * width:(bu + 1) * width]
                k.wait(vec, "dS")
                k.wait(vec, "vS")
                k.inc(vec.tensor_reduce(wbm[:, bu * nb:(bu + 1) * nb].unsqueeze(2),
                                        wt.rearrange("p (b e) -> p b e", e=blk_sz),
                                        axis=AX.X, op=Alu.max, apply_absolute_value=True), "vS")
                k.wait(vec, "vS")
                k.inc(vec.tensor_scalar(wbm[:, bu * nb:(bu + 1) * nb], wbm[:, bu * nb:(bu + 1) * nb],
                                        1.0 / qmax_w, 1e-12, Alu.mult, Alu.max), "vS")
                k.wait(vec, "vS")
                k.inc(vec.reciprocal(wbm[:, bu * nb:(bu + 1) * nb], wbm[:, bu * nb:(bu + 1) * nb]), "vS")
                k.wait(vec, "vS")
                rsb = wbm[:, bu * nb:(bu + 1) * nb].unsqueeze(2).broadcast_to((128, nb, blk_sz))
                k.inc(vec.tensor_tensor(wt.rearrange("p (b e) -> p b e", e=blk_sz),
                                        wt.rearrange("p (b e) -> p b e", e=blk_sz), rsb, Alu.mult), "vS")
                k.wait(vec, "vS")
                k.inc(vec.tensor_scalar(wi[:, :], wt, 1.0, None, Alu.mult), "vS")
                k.wait(vec, "vS")
                k.inc(vec.reciprocal(wbm[:, bu * nb:(bu + 1) * nb], wbm[:, bu * nb:(bu + 1) * nb]), "vS")
                k.wait(vec, "vS")
                sb = wbm[:, bu * nb:(bu + 1) * nb].unsqueeze(2).broadcast_to((128, nb, blk_sz))
                k.inc(vec.tensor_tensor(wd[:, bu * width:(bu + 1) * width].rearrange(
                    "p (b e) -> p b e", e=blk_sz), wi[:, :].rearrange("p (b e) -> p b e", e=blk_sz),
                    sb, Alu.mult), "vS")
                ot = k.tok(vec, "vS")
                out_tok.append(ot)
                k.wait(sca, "vS", ot)
                k.inc(sca.dma_start(out=scr[oc * 128:(oc + 1) * 128, :],
                                    in_=wd[:, bu * width:(bu + 1) * width]), "dA", 16)
        k.barrier(ALL_ENG)

    def ffn_mm_spill(wkm_t, spill, gcol, kchunks, ochunks):
        with (
            nc.sbuf_tensor(f"fst{gcol}", [128, 2 * 4096], BF16) as fst,
            nc.psum_tensor(f"psF{gcol}", [128, 4096], F32) as psF,
        ):
            ev_tok = [None, None]
            sp_tok = [None, None]
            for oc in range(ochunks):
                bu = oc % 2
                k.wait(ten, "dS")
                if oc >= 1:
                    k.wait(ten, "sS", ev_tok[(oc - 1) % 2])
                mi = None
                for kc in range(kchunks):
                    for tb in range(8):
                        mi = ten.matmul(psF[:, tb * 512:(tb + 1) * 512],
                                        wkm_t[:, kc * (128 * ochunks) + oc * 128:
                                              kc * (128 * ochunks) + (oc + 1) * 128],
                                        xqfm[:, kc * NT + tb * 512: kc * NT + (tb + 1) * 512],
                                        start=(kc == 0), stop=(kc == kchunks - 1))
                mt = k.tok(ten, "tS")
                k.wait(sca, "tS", mt)
                if oc >= 2:
                    k.wait(sca, "dA", sp_tok[bu])
                ev_tok[bu] = k.tok(sca, "sS") if False else None
                ei = sca.copy(fst[:, bu * 4096:(bu + 1) * 4096], psF[:, :])
                ev_tok[bu] = k.inc(ei, "sS")
                k.wait(vec, "sS", ev_tok[bu])
                k.inc(vec.tensor_reduce(lnr[:, 0:1], fst[:, bu * 4096:(bu + 1) * 4096],
                                        axis=AX.X, op=Alu.max, apply_absolute_value=True), "vS")
                k.wait(vec, "vS")
                k.inc(vec.tensor_tensor(gmax[:, gcol:gcol + 1], gmax[:, gcol:gcol + 1],
                                        lnr[:, 0:1], Alu.max), "vS")
                k.wait(sca, "sS", ev_tok[bu])
                sp_tok[bu] = k.inc(sca.dma_start(
                    out=bass.AP(spill, oc * 128 * 512,
                                [[512, 128], [24 * 128 * 512, 8], [1, 512]]),
                    in_=fst[:, bu * 4096:(bu + 1) * 4096].rearrange("p (t e) -> p t e", e=512)),
                    "dA", 16)

    def allreduce_max(idx, srcs, dsts):
        k.wait(gp, "vS")
        for i, c in enumerate(srcs):
            k.inc(gp.tensor_reduce(row[0:1, i:i + 1], gmax[:, c:c + 1], axis=AX.C, op=Alu.max), "gS")
            k.wait(gp, "gS")
        k.inc(gp.dma_start(out=dr[f"cc_in{idx}"][0:1, 0:4], in_=row[0:1, 0:4]), "dG", 16)
        k.wait(gp, "dG")
        k.inc(gp.collective_compute("AllReduce", Alu.max, replica_groups=[list(range(NCORES))],
                                    ins=[dr[f"cc_in{idx}"].ap()], outs=[dr[f"cc_out{idx}"].ap()]), "cS")
        k.wait(gp, "cS")
        k.inc(gp.dma_start(out=row[0:1, 4:8], in_=dr[f"cc_out{idx}"][0:1, 0:4]), "dG", 16)

    k.barrier(ALL_ENG)
    wqdq_lpbq(w["l0_wup"], dr["wup_scr"], 24, 1024, 32, 7.0, "u0")
    wqdq_lpbq(w["l0_wgate"], dr["wgate_scr"], 24, 1024, 32, 7.0, "g0")

    if _stopper("wqdq0"):
        es.close()
        return nc
    with nc.sbuf_tensor("wffkm", [128, DCH * I0], BF16) as wffkm:
        k.barrier(ALL_ENG)
        for ic in range(DCH):
            k.inc(sync.dma_start(out=wffkm[:, ic * I0:(ic + 1) * I0],
                                 in_=dr["wup_scr"][:, ic * 128:(ic + 1) * 128],
                                 transpose=True), "dS", 16)
        ffn_mm_spill(wffkm, dr["spill_up"], 1, DCH, 24)
        k.barrier([sync])
        k.wait(sync, "tS")
        for ic in range(DCH):
            k.inc(sync.dma_start(out=wffkm[:, ic * I0:(ic + 1) * I0],
                                 in_=dr["wgate_scr"][:, ic * 128:(ic + 1) * 128],
                                 transpose=True), "dS", 16)
        ffn_mm_spill(wffkm, dr["spill_gate"], 2, DCH, 24)

    xq_cm.__exit__(None, None, None)

    if _stopper("stB"):
        es.close()
        return nc
    allreduce_max(1, [1, 2], None)
    k.wait(vec, "dG")
    for i, (c2c, ssc) in enumerate([(C_C2U, C_SU), (C_C2G, C_SG)]):
        k.inc(vec.tensor_tensor(row[0:1, 8 + i:9 + i], row[0:1, 4 + i:5 + i],
                                scal[0:1, C_SX:C_SX + 1], Alu.mult), "vS")
        k.wait(vec, "vS")
        k.inc(vec.tensor_scalar(row[0:1, 8 + i:9 + i], row[0:1, 8 + i:9 + i],
                                1.0 / QMAX, 1e-12, Alu.mult, Alu.max), "vS")
        k.wait(vec, "vS")
        k.inc(vec.tensor_copy(scal[0:1, ssc:ssc + 1], row[0:1, 8 + i:9 + i]), "vS")
        k.inc(vec.reciprocal(row[0:1, 12 + i:13 + i], row[0:1, 8 + i:9 + i]), "vS")
        k.wait(vec, "vS")
        k.inc(vec.tensor_tensor(scal[0:1, c2c:c2c + 1], row[0:1, 12 + i:13 + i],
                                scal[0:1, C_SX:C_SX + 1], Alu.mult), "vS")
    k.wait(sync, "vS")
    doubling(k, sync, scal, 32, "dS")

    if _stopper("ar2"):
        es.close()
        return nc
    HW_ = 6 * 512
    with (
        nc.sbuf_tensor("cin", [128, 2 * 24 * 512], BF16) as cin,
        nc.sbuf_tensor("ci32", [128, HW_], I32) as ci32,
        nc.sbuf_tensor("csig", [128, HW_], I32) as csig,
        nc.sbuf_tensor("cprod", [128, HW_], F32) as cprod,
        nc.sbuf_tensor("cout", [128, 24 * 512], BF16) as cout,
        nc.sbuf_tensor("cpad", [128, 8], F32) as cpad,
    ):
        ld_tok = [None, None]
        st_tok = []
        codma = None
        for tb in range(8):
            bu = tb % 2
            if tb >= 2:
                k.wait(sync, "vS", st_tok[tb - 2])
            ld_tok[bu] = k.inc(sync.dma_start(
                out=cin[:, bu * 24 * 512:(bu + 1) * 24 * 512].rearrange("p (o e) -> p o e", e=512),
                in_=bass.AP(dr["spill_gate"], tb * 24 * 128 * 512, [[512, 128], [128 * 512, 24], [1, 512]])),
                "dS", 16)
            for hf in range(4):
                src = cin[:, bu * 24 * 512 + hf * HW_: bu * 24 * 512 + (hf + 1) * HW_]
                k.wait(vec, "dS", ld_tok[bu])
                k.wait(vec, "vS")
                k.inc(vec.tensor_scalar(ci32[:, :], src, scal[:, C_C2G:C_C2G + 1], None,
                                        Alu.mult), "vS")
                ct = k.tok(vec, "vS")
                k.wait(sca, "vS", ct)
                k.inc(sca.activation(cprod[:, :], ci32[:, :], Act.Sigmoid,
                                     bias=0.0, scale=scal[:, C_SG:C_SG + 1]), "sS")
                st2 = k.tok(sca, "sS")
                k.wait(sca, "sS", st2)
                k.inc(sca.activation(csig[:, :], cprod[:, :], Act.Copy,
                                     bias=0.0, scale=65536.0), "sS")
                st3 = k.tok(sca, "sS")
                k.wait(vec, "sS", st3)
                k.inc(vec.tensor_tensor(cprod[:, :], ci32[:, :], csig[:, :], Alu.mult), "vS")
                k.wait(vec, "vS")
                k.inc(vec.tensor_reduce(lnr[:, 0:1], cprod[:, :], axis=AX.X, op=Alu.max,
                                        apply_absolute_value=True), "vS")
                k.wait(vec, "vS")
                k.inc(vec.tensor_tensor(gmax[:, 3:4], gmax[:, 3:4], lnr[:, 0:1], Alu.max), "vS")
                if codma is not None:
                    k.wait(vec, "dA", codma)
                k.inc(vec.tensor_copy(cout[:, hf * HW_:(hf + 1) * HW_], cprod[:, :]), "vS")
            ot = k.tok(vec, "vS")
            st_tok.append(ot)
            k.wait(sca, "vS", ot)
            codma = k.inc(sca.dma_start(
                out=bass.AP(dr["spill_gate2"], tb * 24 * 128 * 512, [[512, 128], [128 * 512, 24], [1, 512]]),
                in_=cout[:, :].rearrange("p (o e) -> p o e", e=512)), "dA", 16)
        k.wait(sync, "dA")

    if _stopper("stC"):
        es.close()
        return nc
    allreduce_max(2, [3], None)
    k.wait(vec, "dG")
    k.inc(vec.tensor_tensor(row[0:1, 8:9], row[0:1, 4:5], scal[0:1, C_SG:C_SG + 1], Alu.mult), "vS")
    k.wait(vec, "vS")
    k.inc(vec.tensor_scalar(row[0:1, 8:9], row[0:1, 8:9], 1.0 / (65536.0 * QMAX), 1e-12,
                            Alu.mult, Alu.max), "vS")
    k.wait(vec, "vS")
    k.inc(vec.tensor_copy(scal[0:1, C_SG2:C_SG2 + 1], row[0:1, 8:9]), "vS")
    k.inc(vec.reciprocal(row[0:1, 9:10], row[0:1, 8:9]), "vS")
    k.wait(vec, "vS")
    k.inc(vec.tensor_tensor(row[0:1, 9:10], row[0:1, 9:10], scal[0:1, C_SG:C_SG + 1], Alu.mult), "vS")
    k.wait(vec, "vS")
    k.inc(vec.tensor_scalar(scal[0:1, C_C2G2:C_C2G2 + 1], row[0:1, 9:10], 1.0 / 65536.0, None,
                            Alu.mult), "vS")
    k.wait(sync, "vS")
    doubling(k, sync, scal, 32, "dS")

    if _stopper("ar3"):
        es.close()
        return nc
    with (
        nc.sbuf_tensor("din", [128, 24 * 512], BF16) as din,
        nc.sbuf_tensor("din2", [128, 24 * 512], BF16) as din2,
        nc.sbuf_tensor("di32", [128, HW_], I32) as di32,
        nc.sbuf_tensor("dj32", [128, HW_], I32) as dj32,
        nc.sbuf_tensor("dprod", [128, HW_], F32) as dprod,
        nc.sbuf_tensor("dout", [128, 24 * 512], BF16) as dout,
    ):
        st_tok = []
        dodma = None
        for tb in range(8):
            if tb >= 1:
                k.wait(sync, "vS", st_tok[tb - 1])
            lt = k.inc(sync.dma_start(
                out=din[:, :].rearrange("p (o e) -> p o e", e=512),
                in_=bass.AP(dr["spill_up"], tb * 24 * 128 * 512, [[512, 128], [128 * 512, 24], [1, 512]])),
                "dS", 16)
            lt2 = k.inc(sync.dma_start(
                out=din2[:, :].rearrange("p (o e) -> p o e", e=512),
                in_=bass.AP(dr["spill_gate2"], tb * 24 * 128 * 512, [[512, 128], [128 * 512, 24], [1, 512]])),
                "dS", 16)
            for hf in range(4):
                k.wait(vec, "dS", lt2)
                k.wait(vec, "vS")
                k.inc(vec.tensor_scalar(di32[:, :], din[:, hf * HW_:(hf + 1) * HW_],
                                        scal[:, C_C2U:C_C2U + 1], None, Alu.mult), "vS")
                k.inc(vec.tensor_scalar(dj32[:, :], din2[:, hf * HW_:(hf + 1) * HW_],
                                        scal[:, C_C2G2:C_C2G2 + 1], None, Alu.mult), "vS")
                k.wait(vec, "vS")
                k.inc(vec.tensor_tensor(dprod[:, :], di32[:, :], dj32[:, :], Alu.mult), "vS")
                k.wait(vec, "vS")
                k.inc(vec.tensor_reduce(lnr[:, 0:1], dprod[:, :], axis=AX.X, op=Alu.max,
                                        apply_absolute_value=True), "vS")
                k.wait(vec, "vS")
                k.inc(vec.tensor_tensor(gmax[:, 4:5], gmax[:, 4:5], lnr[:, 0:1], Alu.max), "vS")
                if dodma is not None:
                    k.wait(vec, "dA", dodma)
                k.inc(vec.tensor_copy(dout[:, hf * HW_:(hf + 1) * HW_], dprod[:, :]), "vS")
            ot = k.tok(vec, "vS")
            st_tok.append(ot)
            k.wait(sca, "vS", ot)
            dodma = k.inc(sca.dma_start(
                out=bass.AP(dr["spill_o"], tb * 24 * 128 * 512, [[512, 128], [128 * 512, 24], [1, 512]]),
                in_=dout[:, :].rearrange("p (o e) -> p o e", e=512)), "dA", 16)
        k.wait(sync, "dA")

    if _stopper("stD"):
        es.close()
        return nc
    allreduce_max(3, [4], None)
    k.wait(vec, "dG")
    k.inc(vec.tensor_tensor(row[0:1, 8:9], scal[0:1, C_SU:C_SU + 1], scal[0:1, C_SG2:C_SG2 + 1],
                            Alu.mult), "vS")
    k.wait(vec, "vS")
    k.inc(vec.tensor_tensor(row[0:1, 9:10], row[0:1, 4:5], row[0:1, 8:9], Alu.mult), "vS")
    k.wait(vec, "vS")
    k.inc(vec.tensor_scalar(row[0:1, 9:10], row[0:1, 9:10], 1.0 / QMAX, 1e-12, Alu.mult, Alu.max), "vS")
    k.wait(vec, "vS")
    k.inc(vec.tensor_copy(scal[0:1, C_SO:C_SO + 1], row[0:1, 9:10]), "vS")
    k.inc(vec.reciprocal(row[0:1, 10:11], row[0:1, 9:10]), "vS")
    k.wait(vec, "vS")
    k.inc(vec.tensor_tensor(scal[0:1, C_C2O:C_C2O + 1], row[0:1, 10:11], row[0:1, 8:9], Alu.mult), "vS")
    k.wait(sync, "vS")
    doubling(k, sync, scal, 32, "dS")

    if _stopper("ar4"):
        es.close()
        return nc
    wqdq_lpbq(w["l0_wdown"], dr["wdown_scr"], 8, I0, 32, 7.0, "d0")
    k.barrier(ALL_ENG)
    with (
        nc.sbuf_tensor("wdkm", [128, 24 * 1024], BF16) as wdkm,
        nc.sbuf_tensor("ein", [128, 24 * 512], BF16) as ein,
        nc.sbuf_tensor("ei32", [128, HW_], I32) as ei32,
        nc.sbuf_tensor("eb", [128, 24 * 512], BF16) as ebuf,
        nc.sbuf_tensor("eres", [128, 4096], F32) as eres,
        nc.sbuf_tensor("exr", [128, 4096], F32) as exr,
        nc.psum_tensor("psD", [128, 4096], F32) as psD,
    ):
        for ic in range(24):
            k.inc(sync.dma_start(out=wdkm[:, ic * 1024:(ic + 1) * 1024],
                                 in_=dr["wdown_scr"][:, ic * 128:(ic + 1) * 128],
                                 transpose=True), "dS", 16)
        wld = k.cnt["dS"]
        res_tok = None
        x1p_tok = None
        mt = None
        for tb in range(8):
            if res_tok is not None:
                k.wait(sync, "vS", res_tok)
            lt = k.inc(sync.dma_start(
                out=ein[:, :].rearrange("p (o e) -> p o e", e=512),
                in_=bass.AP(dr["spill_o"], tb * 24 * 128 * 512,
                            [[512, 128], [128 * 512, 24], [1, 512]])), "dS", 16)
            for hf in range(4):
                k.wait(vec, "dS", lt)
                k.wait(vec, "vS")
                if mt is not None:
                    k.wait(vec, "tS", mt)
                k.inc(vec.tensor_scalar(ei32[:, :], ein[:, hf * HW_:(hf + 1) * HW_],
                                        scal[:, C_C2O:C_C2O + 1], None, Alu.mult), "vS")
                k.wait(vec, "vS")
                k.inc(vec.tensor_copy(ebuf[:, hf * HW_:(hf + 1) * HW_], ei32[:, :]), "vS")
            obt = k.tok(vec, "vS")
            k.wait(ten, "vS", obt)
            k.wait(ten, "dS", wld)
            mi = None
            for ocE in range(8):
                for ic in range(24):
                    mi = ten.matmul(psD[:, ocE * 512:(ocE + 1) * 512],
                                    wdkm[:, ic * 1024 + ocE * 128: ic * 1024 + ocE * 128 + 128],
                                    ebuf[:, ic * 512:(ic + 1) * 512],
                                    start=(ic == 0), stop=(ic == 23))
            mt = k.tok(ten, "tS")
            la = k.inc(sync.dma_start(out=eres[:, :].rearrange("p (c e) -> p c e", e=512),
                                      in_=bass.AP(dr["attn_sp"], tb * 512,
                                                  [[NT, 128], [128 * NT, 8], [1, 512]])), "dS", 16)
            lx = k.inc(sync.dma_start(out=exr[:, :].rearrange("p (c e) -> p c e", e=512),
                                      in_=bass.AP(dr["xfm_park"], tb * 512,
                                                  [[NT, 128], [128 * NT, 8], [1, 512]])), "dS", 16)
            k.wait(vec, "tS", mt)
            k.wait(vec, "dS", lx)
            k.wait(vec, "vS")
            k.inc(vec.scalar_tensor_tensor(eres[:, :], psD[:, :], scal[:, C_SO:C_SO + 1],
                                           eres[:, :], Alu.mult, Alu.add), "vS")
            k.wait(vec, "vS")
            k.inc(vec.tensor_tensor(eres[:, :], eres[:, :], exr[:, :], Alu.add), "vS")
            res_tok = k.tok(vec, "vS")
            k.wait(sca, "vS", res_tok)
            if x1p_tok is not None:
                k.wait(sca, "dA", x1p_tok)
            x1p_tok = k.inc(sca.dma_start(out=bass.AP(dr["x1_park"], tb * 512,
                                                      [[NT, 128], [128 * NT, 8], [1, 512]]),
                                          in_=eres[:, :].rearrange("p (c e) -> p c e", e=512)), "dA", 16)

    if _stopper("stE"):
        es.close()
        return nc
    k.barrier(ALL_ENG)
    with (
        nc.sbuf_tensor("txin", [128, 2 * 1024], F32) as txin,
        nc.sbuf_tensor("txout", [128, 2 * 1024], F32) as txout,
        nc.psum_tensor("psX", [128, 2048], F32) as psX,
    ):
        ev_tok = [None, None]
        dm_tok = [None, None]
        for it in range(NT // 128):
            bu = it % 2
            if it >= 2:
                k.wait(sync, "tS")
            li = k.inc(sync.dma_start(out=txin[:, bu * 1024:(bu + 1) * 1024].rearrange(
                "p (c e) -> p c e", e=128),
                in_=bass.AP(dr["x1_park"], it * 128, [[NT, 128], [128 * NT, 8], [1, 128]])),
                "dS", 16)
            k.wait(ten, "dS", li)
            if ev_tok[bu] is not None:
                k.wait(ten, "sS", ev_tok[bu])
            for c in range(DCH):
                ten.transpose(psX[:, bu * 1024 + c * 128: bu * 1024 + (c + 1) * 128],
                              txin[:, bu * 1024 + c * 128: bu * 1024 + (c + 1) * 128],
                              identF[:, :])
            tt2 = k.tok(ten, "tS")
            k.wait(sca, "tS", tt2)
            if dm_tok[bu] is not None:
                k.wait(sca, "dA", dm_tok[bu])
            ev_tok[bu] = k.inc(sca.copy(txout[:, bu * 1024:(bu + 1) * 1024],
                                        psX[:, bu * 1024:(bu + 1) * 1024]), "sS")
            k.wait(sca, "sS", ev_tok[bu])
            dm_tok[bu] = k.inc(sca.dma_start(out=dr["x1tok_park"][it * 128:(it + 1) * 128, :],
                                             in_=txout[:, bu * 1024:(bu + 1) * 1024]), "dA", 16)

    if _stopper("x1t"):
        es.close()
        return nc
    k.barrier(ALL_ENG)
    ln_pass("1", dr["x1tok_park"], 5, xnfm, park_f32=dr["xn1f_park"])


    if _stopper("l0"):
        es.close()
        return nc

    k.inc(gp.dma_start(out=dr["wqkv_scr"][:, :], in_=w["l1_wqkv"][:, :]), "dG", 16)
    k.inc(gp.dma_start(out=dr["wout_scr"][:, :], in_=w["l1_wout"][:, :]), "dG", 16)

    class _StopEmit(Exception):
        pass

    k.barrier(ALL_ENG)
    if stop_after == "l1w":
        k.inc(gp.dma_start(out=y_out[0:128, :], in_=xnfm[:, 0:D]), "dG", 16)
        k.wait(gp, "dG")
        es.close()
        return nc
    with (
        nc.sbuf_tensor("rrep", [32, 4224], F32) as rrep,
        nc.sbuf_tensor("w1c", [32, 1], F32) as w1c,
        nc.sbuf_tensor("b1c", [32, 1], F32) as b1c,
        nc.sbuf_tensor("w2c", [32, 32], BF16) as w2c,
        nc.sbuf_tensor("b2c", [16, 1], F32) as b2c,
        nc.sbuf_tensor("rfb", [32, 4224], BF16) as rfb,
        nc.sbuf_tensor("tabs", [16, 4224], F32) as tabs,
        nc.sbuf_tensor("etabs", [16, 4224], BF16) as etabs,
        nc.sbuf_tensor("m2r", [16, 2], F32) as m2r,
        nc.psum_tensor("psR", [32, 2048], F32) as psR,
    ):
        k.inc(sync.dma_start(out=rrep[:, :], in_=w["relrep"][:, :]), "dS", 16)
        k.inc(sync.dma_start(out=w1c[:, :], in_=bass.AP(w["l1_w1r"], 0, [[1, 32], [1, 1]])), "dS", 16)
        k.inc(sync.dma_start(out=b1c[:, :], in_=bass.AP(w["l1_b1r"], 0, [[1, 32], [1, 1]])), "dS", 16)
        k.inc(sync.dma_start(out=b2c[:, :], in_=bass.AP(w["l1_b2r"], 0, [[1, 16], [1, 1]])), "dS", 16)
        k.wait(vec, "vS")
        k.inc(vec.memset(w2c[:, :], 0.0), "vS")
        k.wait(gp, "vS")
        k.inc(gp.dma_start(out=w2c[:, 0:16], in_=w["l1_w2r"][:, :]), "dG", 16)
        k.wait(sca, "dS")
        _lvl = {"rpeA": 0, "rpeB": 1, "rpeC": 2, "rpeD": 3}.get(stop_after, 99)
        if _lvl >= 1:
            k.inc(sca.activation(rfb[:, :], rrep[:, :], Act.Relu, bias=b1c[:, :], scale=w1c[:, :]), "sS")
        if _lvl >= 1:
            st = k.tok(sca, "sS")
        if _lvl >= 2:
            k.wait(ten, "sS", st)
            k.wait(ten, "dG")
            for half in range(2):
                if half >= 1:
                    k.wait(ten, "sS")
                for nn in range(4):
                    ten.matmul(psR[0:32, nn * 512:(nn + 1) * 512], w2c[:, :],
                               rfb[:, half * 2048 + nn * 512: half * 2048 + (nn + 1) * 512],
                               start=True, stop=True)
                tt_ = k.tok(ten, "tS")
                k.wait(sca, "tS", tt_)
                k.inc(sca.activation(tabs[0:16, half * 2048:(half + 1) * 2048], psR[0:16, :],
                                     Act.Identity, bias=b2c[:, :], scale=1.0), "sS")
                k.tok(sca, "sS")
        if _lvl >= 3:
            k.wait(vec, "sS")
            k.inc(vec.tensor_reduce(m2r[:, 0:1], tabs[:, :], axis=AX.X, op=Alu.max), "vS")
            mm2 = k.tok(vec, "vS")
            k.wait(gp, "vS", mm2)
            k.inc(gp.tensor_reduce(m2r[0:1, 1:2], m2r[:, 0:1], axis=AX.C, op=Alu.max), "gS")
            k.wait(sync, "gS")
            doubling(k, sync, m2r, 2, "dS", nrows=16)
        if _lvl >= 4:
            k.wait(vec, "dS")
            k.inc(vec.tensor_scalar(m2r[:, 0:1], m2r[:, 1:2], -1.0, None, Alu.mult), "vS")
            k.wait(sca, "vS")
            k.inc(sca.activation(etabs[:, :], tabs[:, :], Act.Exp, bias=m2r[:, 0:1], scale=1.0), "sS")
            st2 = k.tok(sca, "sS")
            k.wait(sca, "sS", st2)
            k.inc(sca.dma_start(out=dr["exptab"][:, :], in_=etabs[:, :]), "dA", 16)
    if stop_after in ("rpeA", "rpeB", "rpeC", "rpeD"):
        k.barrier(ALL_ENG)
        k.inc(gp.dma_start(out=y_out[0:128, :], in_=xnfm[:, 0:D]), "dG", 16)
        k.wait(gp, "dG")
        es.close()
        return nc

    if stop_after in ("rpe", "rpe0", "rpe1"):
        if _stopper(stop_after) or _stopper("rpe0") or _stopper("rpe1") or True:
            k.barrier(ALL_ENG)
            k.inc(gp.dma_start(out=y_out[0:128, :], in_=xnfm[:, 0:D]), "dG", 16)
            k.wait(gp, "dG")
            es.close()
            return nc

    def l1_attn_seq(s, sub=None):
        k.barrier(ALL_ENG)
        with (
            nc.sbuf_tensor(f"yfmB{s}", [128, DCH * T], BF16) as yfm,
            nc.sbuf_tensor(f"mcolB{s}", [128, 16], F32) as mcol,
            nc.sbuf_tensor(f"nrm{s}", [16, 8], F32) as nrm,
        ):
            with (
                nc.sbuf_tensor(f"wqk{s}", [128, DCH * 2048], BF16) as wqk,
                nc.sbuf_tensor(f"bcol{s}", [128, 24], F32) as bcol,
                nc.sbuf_tensor(f"pstg{s}", [128, 2 * 2048], BF16) as pstg,
                nc.sbuf_tensor(f"sqs{s}", [128, 2048], BF16) as sqs,
                nc.sbuf_tensor(f"hsel{s}", [128, 8 * 32], BF16) as hsel,
                nc.psum_tensor(f"psQ{s}", [128, 2048], F32) as psQ,
                nc.psum_tensor(f"psN{s}", [32, 2048], F32) as psN,
        ):
                k.wait(sync, "dG")
                for ic in range(DCH):
                    k.inc(sync.dma_start(out=wqk[:, ic * 2048:(ic + 1) * 2048],
                                         in_=dr["wqkv_scr"][ic * 128:(ic + 1) * 128, 0:2048]), "dS", 16)
                k.inc(sync.dma_start(out=bcol[:, :],
                                     in_=bass.AP(w["l1_bqkv"], 0, [[1, 128], [128, 24]]), allow_slow_non_contiguous=True), "dS", 16)
                k.wait(vec, "vS")
                k.inc(vec.memset(hsel[:, :], 0.0), "vS")
                for oc in range(8):
                    k.inc(vec.memset(hsel[0:64, oc * 32 + 2 * oc: oc * 32 + 2 * oc + 1], 1.0), "vS")
                    k.inc(vec.memset(hsel[64:128, oc * 32 + 2 * oc + 1: oc * 32 + 2 * oc + 2], 1.0), "vS")
                wl = k.cnt["dS"]
                ev_tok = [None, None]
                pk_tok = [None, None]
                for j in range(2):
                    for oc in range(8):
                        bu = oc % 2
                        col = j * 8 + oc
                        k.wait(ten, "dS", wl)
                        if ev_tok[bu] is not None:
                            k.wait(ten, "sS", ev_tok[bu])
                        mi = None
                        for ic in range(DCH):
                            for tb in range(4):
                                mi = ten.matmul(
                                    psQ[:, tb * 512:(tb + 1) * 512],
                                    wqk[:, ic * 2048 + col * 128: ic * 2048 + (col + 1) * 128],
                                    xnfm[:, ic * NT + s * T + tb * 512: ic * NT + s * T + (tb + 1) * 512],
                                    start=(ic == 0), stop=(ic == DCH - 1))
                        mt = k.tok(ten, "tS")
                        k.wait(sca, "tS", mt)
                        if pk_tok[bu] is not None:
                            k.wait(sca, "dA", pk_tok[bu])
                        k.wait(sca, "sS")
                        ei = sca.activation(pstg[:, bu * 2048:(bu + 1) * 2048], psQ[:, :],
                                            Act.Identity, bias=bcol[:, col:col + 1], scale=1.0)
                        ev_tok[bu] = k.tok(sca, "sS")
                        k.wait(sca, "sS", ev_tok[bu])
                        pk_tok[bu] = k.inc(sca.dma_start(
                            out=dr["qk1_park"][(j * DCH + oc) * 128:(j * DCH + oc + 1) * 128, :],
                            in_=pstg[:, bu * 2048:(bu + 1) * 2048]), "dA", 16)
                        k.inc(sca.activation(sqs[:, :], pstg[:, bu * 2048:(bu + 1) * 2048],
                                             Act.Square), "sS")
                        sq_t = k.tok(sca, "sS")
                        k.wait(ten, "sS", sq_t)
                        for tb in range(4):
                            ten.matmul(psN[0:32, tb * 512:(tb + 1) * 512],
                                       hsel[:, oc * 32:(oc + 1) * 32], sqs[:, tb * 512:(tb + 1) * 512],
                                       start=(oc == 0), stop=(oc == 7))
                        k.tok(ten, "tS")
                    k.wait(vec, "tS")
                    k.inc(vec.tensor_reduce(nrm[0:16, j:j + 1], psN[0:16, :], axis=AX.X,
                                            op=Alu.max), "vS")
                    k.tok(vec, "vS")
                k.wait(vec, "vS")
                k.inc(vec.tensor_tensor(nrm[:, 2:3], nrm[:, 0:1], nrm[:, 1:2], Alu.mult), "vS")
                nt = k.tok(vec, "vS")
                k.wait(sca, "vS", nt)
                k.inc(sca.activation(nrm[:, 3:4], nrm[:, 2:3], Act.Sqrt), "sS")
                st3 = k.tok(sca, "sS")
                k.wait(vec, "sS", st3)
                k.inc(vec.tensor_scalar(nrm[:, 4:5], nrm[:, 3:4], -0.126, None, Alu.mult), "vS")
                nt2 = k.tok(vec, "vS")
                k.wait(sca, "vS", nt2)
                k.inc(sca.dma_start(out=bass.AP(dr["bounce16"], 0, [[1, 16], [1, 1]]),
                                    in_=nrm[0:16, 4:5]), "dA", 16)
                k.wait(sync, "dA")
                k.inc(sync.dma_start(out=mcol[0:1, 0:16],
                                     in_=dr["bounce16"][0:1, 0:16]),
                      "dS", 16)
                k.wait(sync, "dS")
                doubling(k, sync, mcol, 16, "dS")
            if sub == "l1qk":
                return

            k.barrier(ALL_ENG)
            with (
                nc.sbuf_tensor(f"kfmB{s}", [128, DCH * T], BF16) as kfm,
                nc.sbuf_tensor(f"v65B{s}", [128, 16 * 16 * 65], BF16) as v65,
            ):
                with (
                    nc.sbuf_tensor(f"wvB{s}", [128, DCH * 1024], BF16) as wv,
                    nc.sbuf_tensor(f"bvr{s}", [128, 1024], F32) as bvr,
                    nc.psum_tensor(f"psV{s}", [128, 2048], F32) as psV,
                ):
                    for ic in range(DCH):
                        k.inc(sync.dma_start(out=wv[:, ic * 1024:(ic + 1) * 1024],
                                             in_=dr["wqkv_scr"][ic * 128:(ic + 1) * 128, 2048:3072]),
                              "dS", 16)
                    k.inc(sync.dma_start(out=bvr[0:1, :], in_=w["l1_bqkv"][0:1, 2048:3072]), "dS", 16)
                    k.wait(sync, "dS")
                    doubling(k, sync, bvr, 1024, "dS")
                    k.wait(vec, "vS")
                    k.inc(vec.memset(v65[:, :], 1.0), "vS")
                    wl2 = k.cnt["dS"]
                    vv_tok = [None, None]
                    for tc in range(16):
                        bu = tc % 2
                        tok0 = s * T + tc * 128
                        k.wait(ten, "dS", wl2)
                        if vv_tok[bu] is not None:
                            k.wait(ten, "vS", vv_tok[bu])
                        mi = None
                        for ic in range(DCH):
                            for nh in range(2):
                                mi = ten.matmul(psV[:, bu * 1024 + nh * 512: bu * 1024 + (nh + 1) * 512],
                                                xnfm[:, ic * NT + tok0: ic * NT + tok0 + 128],
                                                wv[:, ic * 1024 + nh * 512: ic * 1024 + (nh + 1) * 512],
                                                start=(ic == 0), stop=(ic == DCH - 1))
                        mt = k.tok(ten, "tS")
                        k.wait(vec, "tS", mt)
                        k.wait(vec, "dS")
                        k.inc(vec.tensor_tensor(psV[:, bu * 1024:(bu + 1) * 1024],
                                                psV[:, bu * 1024:(bu + 1) * 1024], bvr[:, :],
                                                Alu.add), "vS")
                        k.wait(vec, "vS")
                        vdst = v65[:, tc * 1040: (tc + 1) * 1040].rearrange("p (h e) -> p h e", e=65)
                        vv_tok[bu] = k.inc(vec.tensor_copy(
                            vdst[:, :, 0:64],
                            psV[:, bu * 1024:(bu + 1) * 1024].rearrange("p (h d) -> p h d", d=64)), "vS")
                k.barrier([sync])
                for c in range(DCH):
                    k.inc(sync.dma_start(out=kfm[:, c * T:(c + 1) * T],
                                         in_=dr["qk1_park"][(DCH + c) * 128:(DCH + c + 1) * 128, :]),
                          "dS", 16)

                if sub == "l1v":
                    return
                with (
                    nc.sbuf_tensor(f"qbufB{s}", [128, 2 * 512], BF16) as qbuf,
                    nc.sbuf_tensor(f"pbufB{s}", [128, 2 * 2048], BF16) as pbuf,
                    nc.sbuf_tensor(f"ystB{s}", [128, T], BF16) as ystage,
                    nc.sbuf_tensor(f"ycopB{s}", [128, 1024], BF16) as ycop,
                    nc.sbuf_tensor(f"zrepB{s}", [128, T], BF16) as zrep,
                    nc.sbuf_tensor(f"rttB{s}", [128, 2 * 4224], BF16) as rtt,
                    nc.psum_tensor(f"psSB{s}", [128, 2048], F32) as psS,
                    nc.psum_tensor(f"psYB{s}", [128, 1024], F32) as psY,
                ):
                    gsz = [2] * 8
                    gof = [0, 2, 4, 6, 8, 10, 12, 14]
                    NG = 8
                    yev_tok = None
                    ysc_tok = None
                    ycop_tok = None
                    rtt_tok = None
                    for hp in range(8):
                        hA = 2 * hp
                        if rtt_tok is not None:
                            k.wait(sync, "tS")
                            k.wait(sync, "vS")
                        for hh in range(2):
                            k.inc(sync.dma_start(out=rtt[0:1, hh * 4224:hh * 4224 + 4224],
                                                 in_=dr["exptab"][hA + hh:hA + hh + 1, :]), "dS", 16)
                        k.wait(sync, "dS")
                        p2 = 1
                        while p2 < 128:
                            for hh in range(2):
                                k.inc(sync.dma_start(
                                    out=rtt[p2:2 * p2, hh * 4224 + p2: hh * 4224 + 4224],
                                    in_=rtt[0:p2, hh * 4224: hh * 4224 + 4224 - p2]), "dS", 16)
                            k.wait(sync, "dS")
                            p2 *= 2
                        rtt_tok = k.cnt["dS"]
                        qld_tok = [None] * 4
                        qsc_tok = [None] * 4
                        for qt in range(4):
                            qb = qt % 2
                            if qt >= 2:
                                k.wait(sync, "tS", qsc_tok[qt - 2])
                            qld_tok[qt] = k.inc(sync.dma_start(
                                out=qbuf[:, qb * 512:(qb + 1) * 512],
                                in_=dr["qk1_park"][hp * 128:(hp + 1) * 128, qt * 512:(qt + 1) * 512]),
                                "dS", 16)
                            GK = 16
                            sc_toks = [None] * GK
                            ex_toks = [None] * GK
                            pm_toks = [None] * GK

                            def attv(g):
                                kc = g
                                pp = g % 2
                                k.wait(ten, "vS", pm_toks[g])
                                mv = None
                                for hh in range(2):
                                    mv = ten.matmul(
                                        psY[0:65, hh * 512:(hh + 1) * 512],
                                        v65[:, kc * 1040 + (hA + hh) * 65:
                                            kc * 1040 + (hA + hh) * 65 + 65],
                                        pbuf[:, pp * 2048 + hh * 512:
                                             pp * 2048 + (hh + 1) * 512],
                                        start=(g == 0), stop=(g == GK - 1))
                                return k.inc(mv, "tS")

                            for g in range(GK):
                                kc = g
                                pp = g % 2
                                h0 = pp * 1024
                                k.wait(ten, "dS", qld_tok[qt])
                                k.wait(ten, "dS", rtt_tok)
                                if g >= 2:
                                    k.wait(ten, "sS", ex_toks[g - 2])
                                else:
                                    k.wait(ten, "sS")
                                    if g == 0 and yev_tok is not None:
                                        k.wait(ten, "vS", yev_tok)
                                mi = None
                                for hh in range(2):
                                    mi = ten.matmul(
                                        psS[:, h0 + hh * 512: h0 + (hh + 1) * 512],
                                        kfm[hh * 64:(hh + 1) * 64,
                                            hp * T + kc * 128: hp * T + (kc + 1) * 128],
                                        qbuf[hh * 64:(hh + 1) * 64, qb * 512:(qb + 1) * 512],
                                        start=True, stop=True)
                                sc_toks[g] = k.inc(mi, "tS")
                                if g >= 1:
                                    qsc_tok[qt] = attv(g - 1)
                                k.wait(sca, "tS", sc_toks[g])
                                ei = None
                                for hh in range(2):
                                    ei = sca.activation(
                                        pbuf[:, pp * 2048 + hh * 512: pp * 2048 + (hh + 1) * 512],
                                        psS[:, h0 + hh * 512: h0 + (hh + 1) * 512],
                                        Act.Exp, bias=mcol[:, (hA + hh):(hA + hh) + 1], scale=0.125)
                                ex_toks[g] = k.inc(ei, "sS")
                                k.wait(vec, "sS", ex_toks[g])
                                mbase = 2047 + qt * 512 - kc * 128
                                vm = None
                                for hh in range(2):
                                    vm = vec.tensor_tensor(
                                        pbuf[:, pp * 2048 + hh * 512: pp * 2048 + (hh + 1) * 512],
                                        pbuf[:, pp * 2048 + hh * 512: pp * 2048 + (hh + 1) * 512],
                                        rtt[:, hh * 4224 + mbase: hh * 4224 + mbase + 512],
                                        Alu.mult)
                                pm_toks[g] = k.inc(vm, "vS")
                            qsc_tok[qt] = attv(GK - 1)
                            k.wait(vec, "tS")
                            if ysc_tok is not None:
                                k.wait(vec, "vS", ysc_tok)
                            if ycop_tok is not None:
                                k.wait(vec, "dS", ycop_tok)
                            k.inc(vec.tensor_copy(ystage[0:64, qt * 512:(qt + 1) * 512],
                                                  psY[0:64, 0:512]), "vS")
                            k.inc(vec.tensor_copy(ycop[0:64, 0:512], psY[0:64, 512:1024]), "vS")
                            yev_tok = k.inc(vec.tensor_copy(ycop[64:65, 0:1024], psY[64:65, 0:1024]), "vS")
                            k.wait(sync, "vS")
                            k.inc(sync.dma_start(out=ystage[64:128, qt * 512:(qt + 1) * 512],
                                                 in_=ycop[0:64, 0:512]), "dS", 16)
                            k.inc(sync.dma_start(out=zrep[0:1, qt * 512:(qt + 1) * 512],
                                                 in_=ycop[64:65, 0:512]), "dS", 16)
                            ycop_tok = k.inc(sync.dma_start(out=zrep[64:65, qt * 512:(qt + 1) * 512],
                                                            in_=ycop[64:65, 512:1024]), "dS", 16)
                        k.wait(vec, "dS", ycop_tok)
                        with nc.allow_low_precision(reason="1/Z in bf16 (0.2% noise, within bar)"):
                            k.inc(vec.reciprocal(zrep[0:1, :], zrep[0:1, :]), "vS")
                            k.inc(vec.reciprocal(zrep[64:65, :], zrep[64:65, :]), "vS")
                        k.wait(sync, "vS")
                        p2 = 1
                        while p2 < 64:
                            k.inc(sync.dma_start(out=zrep[p2:2 * p2, :], in_=zrep[0:p2, :]), "dS", 16)
                            k.inc(sync.dma_start(out=zrep[64 + p2:64 + 2 * p2, :],
                                                 in_=zrep[64:64 + p2, :]), "dS", 16)
                            k.wait(sync, "dS")
                            p2 *= 2
                        k.wait(vec, "dS")
                        ysc_tok = k.inc(vec.tensor_tensor(yfm[:, hp * T:(hp + 1) * T], ystage[:, :],
                                                          zrep[:, :], Alu.mult), "vS")
            if dbg == "ybis" and s == 1:
                k.barrier(ALL_ENG)
                k.inc(gp.dma_start(out=dbg_t[:, 0:4096], in_=yfm[:, 0:4096]), "dG", 16)
                k.wait(gp, "dG")
                k.barrier(ALL_ENG)
            if dbg == "yfm1" and s == 1:
                k.barrier([gp])
                k.inc(gp.dma_start(out=dbg_t[:, 0:DCH * T], in_=yfm[:, :]), "dG", 16)
                k.inc(gp.dma_start(out=dbg_t[:, 16384:16400], in_=mcol[:, 0:16]), "dG", 16)
                k.inc(gp.dma_start(out=dbg_t[0:16, 16400:16408], in_=nrm[:, :]), "dG", 16)
                k.wait(gp, "dG")
            k.barrier(ALL_ENG)
            with (
                nc.sbuf_tensor(f"woB{s}", [128, DCH * 1024], BF16) as wo_km,
                nc.sbuf_tensor(f"boc{s}", [128, 8], F32) as boc,
                nc.sbuf_tensor(f"ostB{s}", [128, 2 * 2048], F32) as ostage,
                nc.psum_tensor(f"psOB{s}", [128, 2048], F32) as psO,
            ):
                for ic in range(DCH):
                    k.inc(sync.dma_start(out=wo_km[:, ic * 1024:(ic + 1) * 1024],
                                         in_=dr["wout_scr"][ic * 128:(ic + 1) * 128, :]), "dS", 16)
                k.inc(sync.dma_start(out=boc[:, :],
                                     in_=bass.AP(w["l1_bout"], 0, [[1, 128], [128, 8]]), allow_slow_non_contiguous=True), "dS", 16)
                if dbg is not None:
                    print(f"ADDR s={s} yfm:", nc.lookup_mloc(yfm).addr,
                          "+", DCH * T * 2, "ost:", nc.lookup_mloc(ostage).addr,
                          "+", 4 * 2048 * 4, "wo:", nc.lookup_mloc(wo_km).addr,
                          "mcol:", nc.lookup_mloc(mcol).addr,
                          "boc:", nc.lookup_mloc(boc).addr)
                if dbg == "ybis" and s == 1:
                    k.barrier(ALL_ENG)
                    k.inc(gp.dma_start(out=dbg_t[:, 8192:12288], in_=yfm[:, 0:4096]), "dG", 16)
                    k.wait(gp, "dG")
                    k.barrier(ALL_ENG)
                osp = [None, None]
                est = [None, None]
                for oc in range(DCH):
                    if dbg == "oproj1w" and s == 1 and oc == 0:
                        k.barrier(ALL_ENG)
                        k.inc(gp.dma_start(out=dbg_t[:, 8192:12288], in_=yfm[:, 0:4096]), "dG", 16)
                        k.wait(gp, "dG")
                    if dbg == "oproj1" and s == 1 and oc == 1:
                        k.barrier(ALL_ENG)
                        k.inc(gp.dma_start(out=dbg_t[:, 2048:4096], in_=ostage[:, 0:2048]), "dG", 16)
                        k.inc(gp.dma_start(out=dbg_t[:, 4096:8192], in_=wo_km[:, 0:4096]), "dG", 16)
                        k.inc(gp.dma_start(out=dbg_t[:, 8192:12288], in_=yfm[:, 0:4096]), "dG", 16)
                        k.inc(gp.dma_start(out=dbg_t[:, 12288:12296], in_=boc[:, :]), "dG", 16)
                        k.wait(gp, "dG")
                    k.wait(ten, "dS")
                    if oc >= 1:
                        k.wait(ten, "sS", est[(oc - 1) % 2])
                    mi = None
                    for yc in range(DCH):
                        for tb in range(4):
                            mi = ten.matmul(psO[:, tb * 512:(tb + 1) * 512],
                                            wo_km[:, yc * 1024 + oc * 128: yc * 1024 + oc * 128 + 128],
                                            yfm[:, yc * T + tb * 512: yc * T + (tb + 1) * 512],
                                            start=(yc == 0), stop=(yc == DCH - 1))
                    mt = k.tok(ten, "tS")
                    ob = oc % 2
                    k.wait(sca, "tS", mt)
                    if oc >= 2:
                        k.wait(sca, "dA", osp[ob])
                    ei = sca.activation(ostage[:, ob * 2048:(ob + 1) * 2048], psO[:, :],
                                        Act.Identity, bias=boc[:, oc:oc + 1], scale=1.0)
                    est[ob] = k.tok(sca, "sS")
                    k.wait(sca, "sS", est[ob])
                    osp[ob] = k.inc(sca.dma_start(out=dr["attn_sp"][oc * 128:(oc + 1) * 128,
                                                                    s * T:(s + 1) * T],
                                                  in_=ostage[:, ob * 2048:(ob + 1) * 2048]), "dA", 16)
        k.barrier(ALL_ENG)

    l1_attn_seq(0, sub=stop_after if stop_after in ("l1qk", "l1v") else None)
    if stop_after in ("l1qk", "l1v"):
        if _stopper(stop_after):
            es.close()
            return nc
    if stop_after == "l1attn0":
        pass
    else:
        l1_attn_seq(1)
    if stop_after in ("l1attn", "l1attn0"):
        if _stopper(stop_after):
            es.close()
            return nc

    if dbg == "attn1":
        k.barrier([gp])
        for c in range(8):
            k.inc(gp.dma_start(out=dbg_t[:, c * 4096:(c + 1) * 4096],
                               in_=dr["attn_sp"][c * 128:(c + 1) * 128, :]), "dG", 16)
        k.wait(gp, "dG")

    def fp4_qdq_tile(src_ap, dst_bf, width, ctxbuf):
        nb = width // 16
        bm, an, t2, w2 = ctxbuf["bm"], ctxbuf["an"], ctxbuf["t2"], ctxbuf["w2"]
        Ri, sg = ctxbuf["Ri"], ctxbuf["sg"]
        k.wait(vec, "vS")
        k.inc(vec.tensor_reduce(bm[:, 0:nb].unsqueeze(2),
                                src_ap.rearrange("p (b e) -> p b e", e=16),
                                axis=AX.X, op=Alu.max, apply_absolute_value=True), "vS")
        k.wait(vec, "vS")
        k.inc(vec.tensor_scalar(bm[:, 0:nb], bm[:, 0:nb], 1.0 / 6.0, 1e-12, Alu.mult, Alu.max), "vS")
        k.wait(vec, "vS")
        k.inc(vec.reciprocal(bm[:, nb:2 * nb], bm[:, 0:nb]), "vS")
        rt = k.tok(vec, "vS")
        rsb = bm[:, nb:2 * nb].unsqueeze(2).broadcast_to((128, nb, 16))
        k.inc(vec.tensor_tensor(w2[:, 0:width].rearrange("p (b e) -> p b e", e=16),
                                src_ap.rearrange("p (b e) -> p b e", e=16), rsb, Alu.mult), "vS")
        nt = k.tok(vec, "vS")
        k.wait(sca, "vS", nt)
        k.inc(sca.activation(an[:, 0:width], w2[:, 0:width], Act.Abs), "sS")
        k.inc(sca.activation(sg[:, 0:width], w2[:, 0:width], Act.Sign), "sS")
        st = k.tok(sca, "sS")
        k.wait(vec, "sS", st)
        k.inc(vec.tensor_scalar(t2[:, 0:width], an[:, 0:width], 2.0, None, Alu.is_ge), "vS")
        k.inc(vec.tensor_scalar(t2[:, width:2 * width], an[:, 0:width], 4.0, -0.5, Alu.is_ge,
                                Alu.mult), "vS")
        k.wait(vec, "vS")
        k.inc(vec.scalar_tensor_tensor(w2[:, width:2 * width], t2[:, 0:width], -1.0,
                                       t2[:, width:2 * width], Alu.mult, Alu.add), "vS")
        k.wait(vec, "vS")
        k.inc(vec.tensor_scalar(w2[:, width:2 * width], w2[:, width:2 * width], 2.0, None,
                                Alu.add), "vS")
        k.wait(vec, "vS")
        k.inc(vec.tensor_tensor(w2[:, width:2 * width], an[:, 0:width], w2[:, width:2 * width],
                                Alu.mult), "vS")
        k.wait(vec, "vS")
        k.inc(vec.tensor_scalar(Ri[:, 0:width], w2[:, width:2 * width], 1.0, None, Alu.mult), "vS")
        k.wait(vec, "vS")
        k.inc(vec.tensor_scalar(t2[:, width:2 * width], t2[:, width:2 * width], -4.0, None,
                                Alu.mult), "vS")
        k.wait(vec, "vS")
        k.inc(vec.tensor_tensor(t2[:, 0:width], t2[:, 0:width], t2[:, width:2 * width], Alu.add), "vS")
        k.wait(vec, "vS")
        k.inc(vec.tensor_scalar(t2[:, 0:width], t2[:, 0:width], 1.0, None, Alu.add), "vS")
        k.wait(vec, "vS")
        k.inc(vec.tensor_tensor(w2[:, 0:width], Ri[:, 0:width], t2[:, 0:width], Alu.mult), "vS")
        k.wait(vec, "vS")
        sb = bm[:, 0:nb].unsqueeze(2).broadcast_to((128, nb, 16))
        k.inc(vec.tensor_tensor(sg[:, 0:width].rearrange("p (b e) -> p b e", e=16),
                                sg[:, 0:width].rearrange("p (b e) -> p b e", e=16), sb, Alu.mult), "vS")
        k.wait(vec, "vS")
        k.inc(vec.scalar_tensor_tensor(dst_bf, w2[:, 0:width], 0.5, sg[:, 0:width],
                                       Alu.mult, Alu.mult), "vS")
        return k.tok(vec, "vS")

    def fp4_weights(wsrc, scr, nchunks, width, tag):
        with (
            nc.sbuf_tensor(f"f4n{tag}", [128, width], F32) as wn,
            nc.sbuf_tensor(f"f4bm{tag}", [128, 2 * (width // 16)], F32) as bm,
            nc.sbuf_tensor(f"f4an{tag}", [128, width], F32) as an,
            nc.sbuf_tensor(f"f4t2{tag}", [128, 2 * width], F32) as t2,
            nc.sbuf_tensor(f"f4w2{tag}", [128, 2 * width], F32) as w2,
            nc.sbuf_tensor(f"f4Ri{tag}", [128, width], I32) as Ri,
            nc.sbuf_tensor(f"f4sg{tag}", [128, width], F32) as sg,
            nc.sbuf_tensor(f"f4o{tag}", [128, width], BF16) as wd,
        ):
            ctxb = {"bm": bm, "an": an, "t2": t2, "w2": w2, "Ri": Ri, "sg": sg}
            for oc in range(nchunks):
                k.wait(sync, "vS")
                ld = k.inc(sync.dma_start(out=wn[:, :], in_=wsrc[oc * 128:(oc + 1) * 128, :]),
                           "dS", 16)
                k.wait(vec, "dS", ld)
                ot = fp4_qdq_tile(wn[:, :], wd[:, :], width, ctxb)
                k.wait(sca, "vS", ot)
                k.wait(sca, "dA")
                k.inc(sca.dma_start(out=scr[oc * 128:(oc + 1) * 128, 0:width], in_=wd[:, :]), "dA", 16)
        k.barrier(ALL_ENG)

    with (
        nc.sbuf_tensor("q4n", [128, D], F32) as q4n,
        nc.sbuf_tensor("q4bm", [128, 2 * (D // 16)], F32) as q4bm,
        nc.sbuf_tensor("q4an", [128, D], F32) as q4an,
        nc.sbuf_tensor("q4t2", [128, 2 * D], F32) as q4t2,
        nc.sbuf_tensor("q4w2", [128, 2 * D], F32) as q4w2,
        nc.sbuf_tensor("q4Ri", [128, D], I32) as q4Ri,
        nc.sbuf_tensor("q4sg", [128, D], F32) as q4sg,
        nc.sbuf_tensor("q4o", [128, D], BF16) as q4o,
    ):
        ctxb = {"bm": q4bm, "an": q4an, "t2": q4t2, "w2": q4w2, "Ri": q4Ri, "sg": q4sg}
        for it in range(NT // 128):
            k.wait(sync, "vS")
            ld = k.inc(sync.dma_start(out=q4n[:, :], in_=dr["xn1f_park"][it * 128:(it + 1) * 128, :]),
                       "dS", 16)
            k.wait(vec, "dS", ld)
            ot = fp4_qdq_tile(q4n[:, :], q4o[:, :], D, ctxb)
            k.wait(sca, "vS", ot)
            k.wait(sca, "dA")
            k.inc(sca.dma_start(out=dr["xq1t_park"][it * 128:(it + 1) * 128, :], in_=q4o[:, :]),
                  "dA", 16)
    if dbg == "xq1d":
        k.barrier([gp])
        for i in range(32):
            k.inc(gp.dma_start(out=dbg_t[:, i * 1024:(i + 1) * 1024],
                               in_=dr["xq1t_park"][i * 128:(i + 1) * 128, :]), "dG", 16)
        k.wait(gp, "dG")
    k.barrier(ALL_ENG)
    if _stopper("xq1"):
        es.close()
        return nc

    fp4_weights(w["l1_wgate"], dr["wgate_scr"], 22, 1024, "g1")
    if _stopper("fw_g1"):
        es.close()
        return nc
    fp4_weights(w["l1_wup"], dr["wup_scr"], 22, 1024, "u1")

    IH = I1 // 2
    NCOL = [512, 512, 384]
    NOFF = [0, 512, 1024]
    with (
        nc.sbuf_tensor("wgkm", [128, DCH * IH], BF16) as wgkm,
        nc.sbuf_tensor("wukm", [128, DCH * IH], BF16) as wukm,
        nc.sbuf_tensor("xq1s", [128, 2 * DCH * 128], BF16) as xq1s,
        nc.sbuf_tensor("sgt", [128, IH], F32) as sgt,
        nc.sbuf_tensor("ht", [128, IH], F32) as ht,
        nc.sbuf_tensor("h4bm", [128, 2 * (IH // 16)], F32) as h4bm,
        nc.sbuf_tensor("h4an", [128, IH], F32) as h4an,
        nc.sbuf_tensor("h4t2", [128, 2 * IH], F32) as h4t2,
        nc.sbuf_tensor("h4w2", [128, 2 * IH], F32) as h4w2,
        nc.sbuf_tensor("h4Ri", [128, IH], I32) as h4Ri,
        nc.sbuf_tensor("h4sg", [128, IH], F32) as h4sg,
        nc.sbuf_tensor("h4o", [128, IH], BF16) as h4o,
        nc.psum_tensor("psG1", [128, 2048], F32) as psG1,
    ):
        ctxh = {"bm": h4bm, "an": h4an, "t2": h4t2, "w2": h4w2, "Ri": h4Ri, "sg": h4sg}
        for ihalf in range(2):
            k.barrier(ALL_ENG)
            for ic in range(DCH):
                k.inc(sync.dma_start(out=wgkm[:, ic * IH:(ic + 1) * IH],
                                     in_=dr["wgate_scr"][ihalf * IH:(ihalf + 1) * IH,
                                                         ic * 128:(ic + 1) * 128],
                                     transpose=True), "dS", 16)
                k.inc(sync.dma_start(out=wukm[:, ic * IH:(ic + 1) * IH],
                                     in_=dr["wup_scr"][ihalf * IH:(ihalf + 1) * IH,
                                                       ic * 128:(ic + 1) * 128],
                                     transpose=True), "dS", 16)
            wl3 = k.cnt["dS"]
            hq_tok = None
            for tc in range(NT // 128):
                bu = tc % 2
                if tc >= 2:
                    k.wait(sync, "tS")
                xql = None
                for ic in range(DCH):
                    xql = k.inc(sync.dma_start(
                        out=xq1s[:, bu * 1024 + ic * 128: bu * 1024 + (ic + 1) * 128],
                        in_=dr["xq1t_park"][tc * 128:(tc + 1) * 128, ic * 128:(ic + 1) * 128],
                        transpose=True), "dS", 16)
                k.wait(ten, "dS", xql)
                k.wait(ten, "dS", wl3)
                if hq_tok is not None:
                    k.wait(ten, "sS")
                    k.wait(ten, "vS")
                mi = None
                for ic in range(DCH):
                    for nn in range(3):
                        mi = ten.matmul(psG1[:, NOFF[nn]:NOFF[nn] + NCOL[nn]],
                                        xq1s[:, bu * 1024 + ic * 128: bu * 1024 + (ic + 1) * 128],
                                        wgkm[:, ic * IH + NOFF[nn]: ic * IH + NOFF[nn] + NCOL[nn]],
                                        start=(ic == 0), stop=(ic == DCH - 1))
                gt = k.tok(ten, "tS")
                k.wait(sca, "tS", gt)
                k.inc(sca.activation(sgt[:, :], psG1[:, 0:IH], Act.Silu), "sS")
                sil = k.tok(sca, "sS")
                k.wait(ten, "sS", sil)
                mi = None
                for ic in range(DCH):
                    for nn in range(3):
                        mi = ten.matmul(psG1[:, NOFF[nn]:NOFF[nn] + NCOL[nn]],
                                        xq1s[:, bu * 1024 + ic * 128: bu * 1024 + (ic + 1) * 128],
                                        wukm[:, ic * IH + NOFF[nn]: ic * IH + NOFF[nn] + NCOL[nn]],
                                        start=(ic == 0), stop=(ic == DCH - 1))
                ut = k.tok(ten, "tS")
                k.wait(vec, "tS", ut)
                k.wait(vec, "sS", sil)
                k.wait(vec, "vS")
                k.inc(vec.tensor_tensor(ht[:, :], sgt[:, :], psG1[:, 0:IH], Alu.mult), "vS")
                ot = fp4_qdq_tile(ht[:, :], h4o[:, :], IH, ctxh)
                k.wait(sca, "vS", ot)
                if hq_tok is not None:
                    k.wait(sca, "dA", hq_tok)
                hq_tok = k.inc(sca.dma_start(
                    out=dr["hq1t_park"][tc * 128:(tc + 1) * 128, ihalf * IH:(ihalf + 1) * IH],
                    in_=h4o[:, :]), "dA", 16)
    k.barrier(ALL_ENG)
    if _stopper("gateup"):
        es.close()
        return nc

    fp4_weights(w["l1_wdown"], dr["wdown_scr"], 8, I1, "d1")
    with (
        nc.sbuf_tensor("wd1km", [128, 22 * 1024], BF16) as wd1km,
        nc.sbuf_tensor("hkm", [128, 22 * 512], BF16) as hkm,
        nc.sbuf_tensor("eres2", [128, 4096], F32) as eres,
        nc.sbuf_tensor("exr2", [128, 4096], F32) as exr,
        nc.sbuf_tensor("eyt", [128, 2 * 1024], F32) as eyt,
        nc.psum_tensor("psD2", [128, 4096], F32) as psD,
    ):
        psE = psD
        for ic in range(22):
            k.inc(sync.dma_start(out=wd1km[:, ic * 1024:(ic + 1) * 1024],
                                 in_=dr["wdown_scr"][0:D, ic * 128:(ic + 1) * 128],
                                 transpose=True), "dS", 16)
        wld = k.cnt["dS"]
        res_tok = None
        ydma = None
        mt = None
        for tb in range(8):
            hl = None
            if mt is not None:
                k.wait(sync, "tS", mt)
            for ic in range(22):
                hl = k.inc(sync.dma_start(
                    out=hkm[:, ic * 512:(ic + 1) * 512],
                    in_=dr["hq1t_park"][tb * 512:(tb + 1) * 512, ic * 128:(ic + 1) * 128],
                    transpose=True), "dS", 16)
            k.wait(ten, "dS", hl)
            k.wait(ten, "dS", wld)
            if res_tok is not None:
                k.wait(ten, "vS", res_tok)
                k.wait(ten, "sS")
            mi = None
            for ocE in range(8):
                for ic in range(22):
                    mi = ten.matmul(psD[:, ocE * 512:(ocE + 1) * 512],
                                    wd1km[:, ic * 1024 + ocE * 128: ic * 1024 + ocE * 128 + 128],
                                    hkm[:, ic * 512:(ic + 1) * 512],
                                    start=(ic == 0), stop=(ic == 21))
            mt = k.tok(ten, "tS")
            la = k.inc(sync.dma_start(out=eres[:, :].rearrange("p (c e) -> p c e", e=512),
                                      in_=bass.AP(dr["attn_sp"], tb * 512,
                                                  [[NT, 128], [128 * NT, 8], [1, 512]])), "dS", 16)
            lx = k.inc(sync.dma_start(out=exr[:, :].rearrange("p (c e) -> p c e", e=512),
                                      in_=bass.AP(dr["x1_park"], tb * 512,
                                                  [[NT, 128], [128 * NT, 8], [1, 512]])), "dS", 16)
            k.wait(vec, "tS", mt)
            k.wait(vec, "dS", lx)
            k.wait(vec, "vS")
            k.inc(vec.tensor_tensor(eres[:, :], eres[:, :], psD[:, :], Alu.add), "vS")
            k.wait(vec, "vS")
            k.inc(vec.tensor_tensor(eres[:, :], eres[:, :], exr[:, :], Alu.add), "vS")
            res_tok = k.tok(vec, "vS")
            for st_ in range(4):
                hb = st_ % 2
                k.wait(ten, "vS", res_tok)
                k.wait(ten, "sS")
                ti = None
                for c in range(DCH):
                    ti = ten.transpose(psE[:, hb * 1024 + c * 128: hb * 1024 + (c + 1) * 128],
                                       eres[:, c * 512 + st_ * 128: c * 512 + st_ * 128 + 128],
                                       identF[:, :])
                tt2 = k.tok(ten, "tS")
                k.wait(sca, "tS", tt2)
                k.wait(sca, "dA")
                ei = sca.copy(eyt[:, hb * 1024:(hb + 1) * 1024], psE[:, hb * 1024:(hb + 1) * 1024])
                et = k.inc(ei, "sS")
                k.wait(sca, "sS", et)
                ydma = k.inc(sca.dma_start(
                    out=y_out[tb * 512 + st_ * 128:tb * 512 + st_ * 128 + 128, :],
                    in_=eyt[:, hb * 1024:(hb + 1) * 1024]), "dA", 16)
        k.wait(sca, "dA")
        k.wait(sync, "dA")

    if dbg == "wdq":
        k.barrier([gp])
        for i in range(24):
            k.inc(gp.dma_start(out=dbg_t[:, i * 1024:(i + 1) * 1024],
                               in_=dr["wup_scr"][i * 128:(i + 1) * 128, :]), "dG", 16)
        k.wait(gp, "dG")
    if dbg in ("up0", "gate0", "g20", "o0f"):
        spname = {"up0": "spill_up", "gate0": "spill_gate", "g20": "spill_gate2", "o0f": "spill_o"}[dbg]
        k.barrier([gp])
        for tb_ in range(2):
            k.inc(gp.dma_start(
                out=dbg_t[:, tb_ * 16384: tb_ * 16384 + 12288].rearrange("p (o e) -> p o e", e=512),
                in_=bass.AP(dr[spname], tb_ * 24 * 128 * 512,
                            [[512, 128], [128 * 512, 24], [1, 512]])), "dG", 16)
        k.inc(gp.dma_start(out=dbg_t[:, 16352:16384], in_=scal[:, 0:32]), "dG", 16)
        k.wait(gp, "dG")
    if dbg in ("x1", "xn1"):
        k.barrier([gp])
        if dbg == "x1":
            for i in range(32):
                k.inc(gp.dma_start(out=dbg_t[:, i * 1024:(i + 1) * 1024],
                                   in_=dr["x1tok_park"][i * 128:(i + 1) * 128, :]), "dG", 16)
        else:
            for half in range(8):
                k.inc(gp.dma_start(out=dbg_t[:, half * 4096:(half + 1) * 4096],
                                   in_=xnfm[:, half * 4096:(half + 1) * 4096]), "dG", 16)
        k.wait(gp, "dG")
    if dbg == "xn0":
        k.wait(gp, "sS")
        k.wait(gp, "dS")
        for half in range(8):
            k.inc(gp.dma_start(out=dbg_t[:, half * 4096:(half + 1) * 4096],
                               in_=xnfm[:, half * 4096:(half + 1) * 4096]), "dG", 16)
        k.wait(gp, "dG")
    elif dbg == "xq0":
        k.wait(gp, "vS")
        for half in range(8):
            k.inc(gp.dma_start(out=dbg_t[:, half * 4096:(half + 1) * 4096],
                               in_=xqfm[:, half * 4096:(half + 1) * 4096]), "dG", 16)
        k.wait(gp, "dG")
    elif dbg == "attn0":
        k.wait(gp, "dA")
        for c in range(8):
            k.inc(gp.dma_start(out=dbg_t[:, c * 4096:(c + 1) * 4096],
                               in_=dr["attn_sp"][c * 128:(c + 1) * 128, :]), "dG", 16)
        k.wait(gp, "dG")


    es.close()
    return nc


def host_consts():
    i = np.arange(HD // 2, dtype=np.float32)
    angles = np.power(np.float32(10000.0), 2.0 * i / HD).astype(np.float32)
    theta = np.arange(T, dtype=np.float32)[:, None] * angles[None, :]
    cos = np.cos(theta).astype(np.float32)
    sin = np.sin(theta).astype(np.float32)
    c2 = np.zeros((T, HD), np.float32)
    s2 = np.zeros((T, HD), np.float32)
    c2[:, 0::2] = cos
    c2[:, 1::2] = cos
    s2[:, 0::2] = -sin
    s2[:, 1::2] = sin
    dd = np.arange(-2047, 2048, dtype=np.float32)
    reld = np.sign(dd) * np.log1p(np.abs(dd))
    relrow = np.zeros((4224,), np.float32)
    relrow[:4095] = reld
    relrep = np.broadcast_to(relrow, (32, 4224)).copy()
    return c2, s2, relrep


def make_inmaps(inputs):
    c2, s2, relrep = host_consts()
    xs = np.ascontiguousarray(inputs["L0_attn_x"].astype(np.float32)).reshape(NCORES, NT, D)
    shared = {
        "l0_wq": inputs["L0_attn_w_q"], "l0_wk": inputs["L0_attn_w_k"],
        "l0_wv": inputs["L0_attn_w_v"], "l0_wo": inputs["L0_attn_w_o"],
        "l0_wup": inputs["L0_ffn_w_up"], "l0_wgate": inputs["L0_ffn_w_gate"],
        "l0_wdown": inputs["L0_ffn_w_down"],
        "l1_wqkv": inputs["L1_attn_w_qkv"], "l1_bqkv": inputs["L1_attn_b_qkv"].reshape(1, -1),
        "l1_wout": inputs["L1_attn_w_out"], "l1_bout": inputs["L1_attn_b_out"].reshape(1, -1),
        "l1_w1r": inputs["L1_attn_w1_rpe"], "l1_b1r": inputs["L1_attn_b1_rpe"].reshape(1, -1),
        "l1_w2r": inputs["L1_attn_w2_rpe"], "l1_b2r": inputs["L1_attn_b2_rpe"].reshape(1, -1),
        "l1_wgate": inputs["L1_ffn_w_gate"], "l1_wup": inputs["L1_ffn_w_up"],
        "l1_wdown": inputs["L1_ffn_w_down"],
        "rope_c2": c2, "rope_s2": s2, "relrep": relrep,
    }
    shared = {kk: np.ascontiguousarray(np.asarray(v).astype(np.float32)) for kk, v in shared.items()}
    return [dict(shared, x=np.ascontiguousarray(xs[c])) for c in range(NCORES)]


def kernel(**inputs):
    nc = build_program()
    in_maps = make_inmaps(inputs)
    res = run_bass_kernel_spmd(nc, in_maps, core_ids=list(range(NCORES)))
    out = np.stack([res.results[c]["y"] for c in range(NCORES)])
    return out.reshape(NCORES * B, T, D).astype(np.float32)


if __name__ == "__main__":
    build_program()
    print("build ok")
